# version 01
# baseline (speedup 1.0000x reference)
from contextlib import ExitStack
import numpy as np
import ml_dtypes
import concourse.bass as bass
import concourse.mybir as mybir
from concourse.bass_utils import run_bass_kernel_spmd

F32 = mybir.dt.float32
BF16 = mybir.dt.bfloat16
I32 = mybir.dt.int32
ALU = mybir.AluOpType
AF = mybir.ActivationFunctionType
AX = mybir.AxisListType

NCORES = 8
R = 2048
NT = 16
D = 1024
KC = 8
NH = 16
HD = 64
EPS = 1e-5


class Sem:
    __slots__ = ("h", "n")

    def __init__(self, h):
        self.h = h
        self.n = 0


class Prog:
    ENGINES = ("sync", "scalar", "vector", "tensor", "gpsimd")

    def __init__(self, nc):
        self.nc = nc
        self.ops = {e: [] for e in self.ENGINES}

    def emit(self, ek, make, inc=None, inck=1):
        if inc is not None:
            def fn(eng, make=make, s=inc, k=inck):
                make(eng).then_inc(s.h, k)
            inc.n += inck
        else:
            fn = make
        self.ops[ek].append(fn)

    def dma(self, ek, out, in_, inc, **kw):
        self.emit(ek, lambda eng, o=out, i=in_, kw=kw: eng.dma_start(out=o, in_=i, **kw),
                  inc=inc, inck=16)

    def wait(self, ek, sem, at=None):
        v = sem.n if at is None else at
        self.ops[ek].append(lambda eng, s=sem, vv=v: eng.wait_ge(s.h, vv))

    def drain(self, ek):
        self.ops[ek].append(lambda eng: eng.drain())

    def build(self):
        with self.nc.Block() as block:
            for ek in self.ENGINES:
                def runner(eng, ek=ek):
                    for fn in self.ops[ek]:
                        fn(eng)
                getattr(block, ek)(runner)


def _ln_tile(p, xs, xn_out, sa, stats, extra_absmax=None):
    cp = p.sems["lncp"]
    v, s = "vector", "scalar"
    m, var, sd, rstd, eps_col = stats["m"], stats["var"], stats["sd"], stats["rstd"], stats["eps_col"]
    p.emit(v, lambda e, a=xs, o=m: e.reduce_sum(out=o, in_=a, axis=AX.X))
    p.drain(v)
    p.emit(v, lambda e, o=m, a=m: e.tensor_scalar(out=o, in0=a, scalar1=1.0 / D, scalar2=None, op0=ALU.mult))
    p.drain(v)
    p.emit(v, lambda e, o=sa, a=xs, mm=m: e.tensor_scalar(out=o, in0=a, scalar1=mm, scalar2=None, op0=ALU.subtract), inc=cp)
    p.wait(s, cp)
    p.emit(s, lambda e, o=xn_out, a=sa: e.activation(out=o, in_=a, func=AF.Square), inc=cp)
    p.wait(v, cp)
    p.emit(v, lambda e, o=var, a=xn_out: e.reduce_sum(out=o, in_=a, axis=AX.X))
    p.drain(v)
    p.emit(v, lambda e, o=var, a=var: e.tensor_scalar(out=o, in0=a, scalar1=1.0 / D, scalar2=None, op0=ALU.mult), inc=cp)
    p.wait(s, cp)
    p.emit(s, lambda e, o=sd, a=var, b=eps_col: e.activation(out=o, in_=a, func=AF.Sqrt, bias=b), inc=cp)
    p.wait(v, cp)
    p.emit(v, lambda e, o=rstd, a=sd: e.reciprocal(out=o, in_=a))
    p.drain(v)
    p.emit(v, lambda e, o=xn_out, a=sa, rr=rstd: e.tensor_scalar(out=o, in0=a, scalar1=rr, scalar2=None, op0=ALU.mult), inc=cp)
    if extra_absmax is not None:
        p.drain(v)
        p.emit(v, lambda e, o=extra_absmax, a=xn_out: e.tensor_reduce(
            out=o, in_=a, axis=AX.X, op=ALU.max, apply_absolute_value=True))
        p.drain(v)


def build_launch1():
    nc = bass.Bass("TRN2", target_bir_lowering=False)
    xin = nc.dram_tensor("xin", [R, D], F32, kind="ExternalInput").ap()
    w1t = nc.dram_tensor("w1t", [D, 3 * D], F32, kind="ExternalInput").ap()
    identb = nc.dram_tensor("identb", [128, 128], BF16, kind="ExternalInput").ap()
    identf = nc.dram_tensor("identf", [128, 128], F32, kind="ExternalInput").ap()

    xn_o = nc.dram_tensor("xn_o", [R, D], F32, kind="ExternalOutput").ap()
    phiq_fm_o = nc.dram_tensor("phiq_fm_o", [128, KC * R], BF16, kind="ExternalOutput").ap()
    phik_fm_o = nc.dram_tensor("phik_fm_o", [128, KC * R], BF16, kind="ExternalOutput").ap()
    phik_tm_o = nc.dram_tensor("phik_tm_o", [R, D], BF16, kind="ExternalOutput").ap()
    v_tm_o = nc.dram_tensor("v_tm_o", [R, NH * (HD + 1)], BF16, kind="ExternalOutput").ap()
    ssum_o = nc.dram_tensor("ssum_o", [64, 128 * 65], F32, kind="ExternalOutput").ap()
    sc0_o = nc.dram_tensor("sc0_o", [64, 128 * 65], F32, kind="ExternalOutput").ap()
    maxp1_o = nc.dram_tensor("maxp1_o", [1, 1], F32, kind="ExternalOutput").ap()

    with ExitStack() as ctx:
        sbuf = lambda n, sh, dt: ctx.enter_context(nc.sbuf_tensor(n, sh, dt))[:, :]
        psum = lambda n, sh, dt: ctx.enter_context(nc.psum_tensor(n, sh, dt))[:, :]
        sem = lambda n: ctx.enter_context(nc.semaphore(n))
        xw = sbuf("xw", [128, NT * D], F32)
        sa = sbuf("sa", [128, D], F32)
        sb = sbuf("sb", [128, D], F32)
        stat = sbuf("stat", [128, 8], F32)
        pmx = sbuf("pmx", [128, NT], F32)
        mxc = sbuf("mxc", [128, 1], F32)
        mx1 = sbuf("mx1", [1, 128], F32)
        xnn_bf = sbuf("xnn_bf", [128, D], BF16)
        xnn_fm = sbuf("xnn_fm", [128, KC * R], BF16)
        w1b = sbuf("w1b", [128, KC * 3 * D], BF16)
        idb = sbuf("idb", [128, 128], BF16)
        idf = sbuf("idf", [128, 128], F32)
        phr = sbuf("phr", [128, D], F32)
        phe = sbuf("phe", [128, D], F32)
        phq_bf = sbuf("phq_bf", [128, D], BF16)
        phk_bf = sbuf("phk_bf", [128, D], BF16)
        v_bf = sbuf("v_bf", [128, NH * (HD + 1)], BF16)
        tstage = sbuf("tstage", [128, D], BF16)
        scst = sbuf("scst", [64, 2 * NH * 65], F32)
        ps_qkv = psum("ps_qkv", [128, D], F32)
        ps_t = psum("ps_t", [128, D], BF16)
        ps_scd0 = psum("ps_scd0", [64, 512], F32)
        ps_scd1 = psum("ps_scd1", [64, 512], F32)
        ps_scz = psum("ps_scz", [64, 16], F32)
        ps_mx = psum("ps_mx", [1, 128], F32)
        s_ld = sem("s_ld"); s_lncp = sem("s_lncp"); s_mm = sem("s_mm")
        s_ev = sem("s_ev"); s_st = sem("s_st"); s_aux = sem("s_aux")

        p = Prog(nc)
        p.sems = {"lncp": Sem(s_lncp)}
        ld = Sem(s_ld); mm = Sem(s_mm); ev = Sem(s_ev); st = Sem(s_st); aux = Sem(s_aux)
        stats = {"m": stat[:, 0:1], "var": stat[:, 1:2], "sd": stat[:, 2:3],
                 "rstd": stat[:, 3:4], "eps_col": stat[:, 4:5]}

        p.emit("vector", lambda e: e.memset(stat[:, 4:5], EPS), inc=p.sems["lncp"])
        for t in range(NT):
            p.dma("sync", xw[:, t * D:(t + 1) * D], xin[t * 128:(t + 1) * 128, :], ld)
        p.dma("sync", idb[:, :], identb, ld)
        p.dma("sync", idf[:, :], identf, ld)
        for c in range(KC):
            p.dma("gpsimd", w1b[:, c * 3 * D:(c + 1) * 3 * D], w1t[c * 128:(c + 1) * 128, :], ld)
        ld_all = ld.n

        p.wait("vector", ld, ld_all)
        p.wait("scalar", ld, ld_all)
        for t in range(NT):
            xt = xw[:, t * D:(t + 1) * D]
            _ln_tile(p, xt, sb, sa, stats, extra_absmax=pmx[:, t:t + 1])
            p.emit("vector", lambda e, o=xt, a=sb: e.tensor_copy(out=o, in_=a), inc=p.sems["lncp"])
            _ln_tile(p, xt, sb, sa, stats)
            p.emit("vector", lambda e, o=xnn_bf, a=sb: e.tensor_copy(out=o, in_=a), inc=p.sems["lncp"])
            cpv = p.sems["lncp"].n
            p.wait("sync", p.sems["lncp"], cpv)
            p.dma("sync", xn_o[t * 128:(t + 1) * 128, :], xt, st)
            p.wait("tensor", p.sems["lncp"], cpv)
            if t > 0:
                p.wait("tensor", ev)
            for c in range(KC):
                p.emit("tensor", lambda e, c=c: e.transpose(
                    ps_t[:, c * 128:(c + 1) * 128], xnn_bf[:, c * 128:(c + 1) * 128], idb[:, :]),
                    inc=(mm if c == KC - 1 else None))
            p.wait("scalar", mm)
            p.emit("scalar", lambda e, t=t: e.activation(
                out=xnn_fm.rearrange("p (c j) -> p c j", j=R)[:, :, t * 128:(t + 1) * 128],
                in_=ps_t.rearrange("p (c j) -> p c j", j=128), func=AF.Copy), inc=ev)

        for t in range(NT):
            b, ch = t // 2, t % 2
            for part in range(3):
                p.wait("tensor", ev)
                for c in range(KC):
                    lhs = xnn_fm.rearrange("p (c j) -> p c j", j=R)[:, c, t * 128:(t + 1) * 128]
                    for jn in range(2):
                        rhs = w1b[:, c * 3 * D + part * D + jn * 512: c * 3 * D + part * D + (jn + 1) * 512]
                        p.emit("tensor", lambda e, l=lhs, r=rhs, jn=jn, c=c: e.matmul(
                            ps_qkv[:, jn * 512:(jn + 1) * 512], l, r,
                            start=(c == 0), stop=(c == KC - 1)),
                            inc=(mm if c == KC - 1 and jn == 1 else None))
                p.wait("scalar", mm)
                p.wait("vector", mm)
                if part < 2:
                    p.emit("scalar", lambda e: e.activation(out=phr[:, :], in_=ps_qkv[:, :], func=AF.Relu), inc=ev)
                    p.wait("vector", ev)
                    p.emit("vector", lambda e: e.tensor_tensor(out=phe[:, :], in0=ps_qkv[:, :], in1=phr[:, :], op=ALU.subtract), inc=ev)
                    p.wait("scalar", ev)
                    p.emit("scalar", lambda e: e.activation(out=phe[:, :], in_=phe[:, :], func=AF.Exp), inc=ev)
                    p.wait("vector", ev)
                    dst = phq_bf if part == 0 else phk_bf
                    p.emit("vector", lambda e, d=dst: e.tensor_tensor(out=d[:, :], in0=phe[:, :], in1=phr[:, :], op=ALU.add), inc=ev)
                else:
                    p.emit("vector", lambda e: e.memset(
                        v_bf.rearrange("p (h e) -> p h e", e=HD + 1)[:, :, HD:HD + 1], 1.0))
                    p.emit("vector", lambda e: e.tensor_copy(
                        out=v_bf.rearrange("p (h e) -> p h e", e=HD + 1)[:, :, 0:HD],
                        in_=ps_qkv.rearrange("p (h d) -> p h d", d=HD)), inc=ev)
                p.wait("tensor", ev)

            for src, dst_fm in ((phq_bf, phiq_fm_o), (phk_bf, phik_fm_o)):
                p.wait("tensor", ev)
                for c in range(KC):
                    p.emit("tensor", lambda e, s=src, c=c: e.transpose(
                        ps_t[:, c * 128:(c + 1) * 128], s[:, c * 128:(c + 1) * 128], idb[:, :]),
                        inc=(mm if c == KC - 1 else None))
                p.wait("scalar", mm)
                p.wait("scalar", st)
                p.emit("scalar", lambda e: e.activation(out=tstage[:, :], in_=ps_t[:, :], func=AF.Copy), inc=ev)
                p.wait("sync", ev)
                p.dma("sync", dst_fm.rearrange("p (c j) -> p c j", j=R)[:, :, t * 128:(t + 1) * 128],
                      tstage.rearrange("p (c j) -> p c j", j=128), st)

            p.wait("tensor", ev)
            for hh in range(NH):
                bank = ps_scd0 if hh < 8 else ps_scd1
                col = (hh % 8) * 64
                lhs = phk_bf[:, hh * 64:(hh + 1) * 64]
                rhs = v_bf[:, hh * 65:hh * 65 + 64]
                p.emit("tensor", lambda e, bk=bank, co=col, l=lhs, r=rhs: e.matmul(
                    bk[:, co:co + 64], l, r, start=True, stop=True))
                rhs1 = v_bf[:, hh * 65 + 64:hh * 65 + 65]
                p.emit("tensor", lambda e, hh=hh, l=lhs, r=rhs1: e.matmul(
                    ps_scz[:, hh:hh + 1], l, r, start=True, stop=True),
                    inc=(mm if hh == NH - 1 else None))
            p.wait("vector", mm)
            scv = scst.rearrange("p (c x e) -> p c x e", c=2, e=65)
            p.emit("vector", lambda e, ch=ch, scv=scv: e.tensor_copy(
                out=scv[0:64, ch, 0:8, 0:64],
                in_=ps_scd0.rearrange("p (x d) -> p x d", d=64)))
            p.emit("vector", lambda e, ch=ch, scv=scv: e.tensor_copy(
                out=scv[0:64, ch, 8:16, 0:64],
                in_=ps_scd1.rearrange("p (x d) -> p x d", d=64)))
            p.emit("vector", lambda e, ch=ch, scv=scv: e.tensor_copy(
                out=scv[0:64, ch, 0:16, 64:65],
                in_=ps_scz.rearrange("p (x o) -> p x o", o=1)), inc=ev)
            p.wait("tensor", ev)
            if ch == 1:
                blk = NH * 65
                p.wait("sync", ev)
                p.dma("sync", sc0_o[0:64, b * blk:(b + 1) * blk], scst[0:64, 0:blk], st)
                p.wait("vector", st)
                p.emit("vector", lambda e, blk=blk: e.tensor_tensor(
                    out=scst[0:64, 0:blk], in0=scst[0:64, 0:blk],
                    in1=scst[0:64, blk:2 * blk], op=ALU.add), inc=ev)
                p.wait("sync", ev)
                p.dma("sync", ssum_o[0:64, b * blk:(b + 1) * blk], scst[0:64, 0:blk], st)

            p.wait("sync", ev)
            p.dma("sync", phik_tm_o[t * 128:(t + 1) * 128, :], phk_bf[:, :], st)
            p.dma("sync", v_tm_o[t * 128:(t + 1) * 128, :], v_bf[:, :], st)
            p.wait("scalar", st)
            p.wait("vector", st)

        p.emit("vector", lambda e: e.reduce_max(out=mxc[:, :], in_=pmx[:, :], axis=AX.X), inc=ev)
        p.wait("tensor", ev)
        p.emit("tensor", lambda e: e.transpose(ps_mx[0:1, 0:128], mxc[:, 0:1], idf[:, :]), inc=mm)
        p.wait("vector", mm)
        p.emit("vector", lambda e: e.tensor_copy(out=mx1[:, :], in_=ps_mx[0:1, :]))
        p.drain("vector")
        p.emit("vector", lambda e: e.reduce_max(out=stat[0:1, 5:6], in_=mx1[0:1, :], axis=AX.X), inc=ev)
        p.wait("sync", ev)
        p.dma("sync", maxp1_o, stat[0:1, 5:6], st)

        p.wait("gpsimd", st)
        p.build()
    return nc


def _prep_host(inputs):
    h = {}
    x = np.ascontiguousarray(inputs["L0_attn_x"], dtype=np.float32)
    h["x_slices"] = [np.ascontiguousarray(x[:, c * 256:(c + 1) * 256, :].reshape(R, D))
                     for c in range(NCORES)]
    h["w1t"] = np.ascontiguousarray(inputs["L0_attn_W1"].T)
    h["identb"] = np.eye(128).astype(ml_dtypes.bfloat16)
    h["identf"] = np.eye(128, dtype=np.float32)
    h["triu"] = np.triu(np.ones((128, 128), np.float32))
    h["w2t"] = np.ascontiguousarray(inputs["L0_attn_W2"].T)
    h["wup"] = np.ascontiguousarray(inputs["L0_ffn_w_up"])
    h["wgate"] = np.ascontiguousarray(inputs["L0_ffn_w_gate"])
    h["wupt"] = np.ascontiguousarray(inputs["L0_ffn_w_up"].T)
    h["wgatet"] = np.ascontiguousarray(inputs["L0_ffn_w_gate"].T)
    h["wdown"] = np.ascontiguousarray(inputs["L0_ffn_w_down"])
    perm = np.zeros(3072, np.int64)
    for j in range(3):
        for hh in range(16):
            for d in range(64):
                perm[j * 1024 + hh * 64 + d] = hh * 192 + j * 64 + d
    h["ipw_t"] = np.ascontiguousarray(inputs["L1_attn_in_proj_weight"][perm, :].T)
    h["ipb_row"] = np.ascontiguousarray(inputs["L1_attn_in_proj_bias"][perm][None, :])
    h["opw_t"] = np.ascontiguousarray(inputs["L1_attn_out_proj_weight"].T)
    h["opb_row"] = np.ascontiguousarray(inputs["L1_attn_out_proj_bias"][None, :])
    h["rt_t"] = np.ascontiguousarray(inputs["L1_ffn_router_w"].T)
    h["rb_row"] = np.ascontiguousarray(inputs["L1_ffn_router_bias"][None, :])
    h["gat"] = np.ascontiguousarray(inputs["L1_ffn_gate_w"].reshape(4096, 1024).T)
    h["upt"] = np.ascontiguousarray(inputs["L1_ffn_up_w"].reshape(4096, 1024).T)
    h["dat"] = np.ascontiguousarray(inputs["L1_ffn_down_w"].transpose(0, 2, 1).reshape(4096, 1024))
    h["sgt"] = np.ascontiguousarray(inputs["L1_ffn_sgate_w"].T)
    h["sut"] = np.ascontiguousarray(inputs["L1_ffn_sup_w"].T)
    h["sdt"] = np.ascontiguousarray(inputs["L1_ffn_sdown_w"].T)
    return h


def kernel(**inputs):
    h = _prep_host(inputs)
    cores = list(range(NCORES))

    nc1 = build_launch1()
    im1 = [{"xin": h["x_slices"][c], "w1t": h["w1t"], "identb": h["identb"],
            "identf": h["identf"]} for c in cores]
    r1 = run_bass_kernel_spmd(nc1, im1, core_ids=cores).results

    maxp1_all = np.stack([r1[c]["maxp1_o"][0] for c in cores])
    nc2 = build_launch2()
    im2 = []
    for c in cores:
        prev = np.zeros((7, 64, 128 * 65), np.float32)
        for j in range(c):
            prev[j] = r1[j]["ssum_o"]
        im2.append({
            "xin": h["x_slices"][c], "xn_i": r1[c]["xn_o"],
            "phiq_fm": r1[c]["phiq_fm_o"], "phik_fm": r1[c]["phik_fm_o"],
            "phik_tm": r1[c]["phik_tm_o"], "v_tm": r1[c]["v_tm_o"],
            "sc0_i": r1[c]["sc0_o"], "ssum_prev": prev, "maxp_all": maxp1_all,
            "w2t": h["w2t"], "wupt": h["wupt"], "wgatet": h["wgatet"],
            "wup": h["wup"], "wgate": h["wgate"],
            "identb": h["identb"], "identf": h["identf"], "triu": h["triu"],
        })
    r2 = run_bass_kernel_spmd(nc2, im2, core_ids=cores).results

    maxp2_all = np.stack([r2[c]["maxp2_o"][0] for c in cores])
    maxp3_all = np.stack([r2[c]["maxp3_o"][0] for c in cores])
    nc3 = build_launch3()
    im3 = [{"up_raw_i": r2[c]["up_raw_o"], "gate_raw_i": r2[c]["gate_raw_o"],
            "maxp2_all": maxp2_all, "maxp3_all": maxp3_all,
            "identf": h["identf"]} for c in cores]
    r3 = run_bass_kernel_spmd(nc3, im3, core_ids=cores).results

    maxp4_all = np.stack([r3[c]["maxp4_o"][0] for c in cores])
    nc4 = build_launch4()
    im4 = [{"t_i": r3[c]["t_o"], "upq_i": r3[c]["upq_o"],
            "maxp4_all": maxp4_all, "identf": h["identf"]} for c in cores]
    r4 = run_bass_kernel_spmd(nc4, im4, core_ids=cores).results

    maxp5_all = np.stack([r4[c]["maxp5_o"][0] for c in cores])
    nc5 = build_launch5()
    im5 = [{"oraw_i": r4[c]["oraw_o"], "maxp5_all": maxp5_all, "xa_i": r2[c]["xa_o"],
            "wdown": h["wdown"], "ipw_t": h["ipw_t"], "ipb_row": h["ipb_row"],
            "opw_t": h["opw_t"], "opb_row": h["opb_row"], "rt_t": h["rt_t"],
            "rb_row": h["rb_row"], "gat": h["gat"], "upt": h["upt"], "dat": h["dat"],
            "sgt": h["sgt"], "sut": h["sut"], "sdt": h["sdt"],
            "identb": h["identb"], "identf": h["identf"]} for c in cores]
    r5 = run_bass_kernel_spmd(nc5, im5, core_ids=cores).results

    out = np.zeros((8, 2048, 1024), np.float32)
    for c in cores:
        out[:, c * 256:(c + 1) * 256, :] = r5[c]["out_o"].reshape(8, 256, 1024)
    return out


def build_launch2():
    nc = bass.Bass("TRN2", target_bir_lowering=False)
    ap = lambda n, sh, dt, kind: nc.dram_tensor(n, sh, dt, kind=kind).ap()
    xin = ap("xin", [R, D], F32, "ExternalInput")
    xn_i = ap("xn_i", [R, D], F32, "ExternalInput")
    phiq_fm = ap("phiq_fm", [128, KC * R], BF16, "ExternalInput")
    phik_fm = ap("phik_fm", [128, KC * R], BF16, "ExternalInput")
    phik_tm = ap("phik_tm", [R, D], BF16, "ExternalInput")
    v_tm = ap("v_tm", [R, NH * 65], BF16, "ExternalInput")
    sc0_i = ap("sc0_i", [64, 128 * 65], F32, "ExternalInput")
    ssum_prev = ap("ssum_prev", [7, 64, 128 * 65], F32, "ExternalInput")
    maxp_all = ap("maxp_all", [8, 1], F32, "ExternalInput")
    w2t = ap("w2t", [D, D], F32, "ExternalInput")
    wупs = None
    wupt = ap("wupt", [D, 3 * D], F32, "ExternalInput")
    wgatet = ap("wgatet", [D, 3 * D], F32, "ExternalInput")
    wup = ap("wup", [3 * D, D], F32, "ExternalInput")
    wgate = ap("wgate", [3 * D, D], F32, "ExternalInput")
    identb = ap("identb", [128, 128], BF16, "ExternalInput")
    identf = ap("identf", [128, 128], F32, "ExternalInput")
    triu = ap("triu", [128, 128], F32, "ExternalInput")

    xa_o = ap("xa_o", [R, D], F32, "ExternalOutput")
    up_raw_o = ap("up_raw_o", [128, 24 * R], F32, "ExternalOutput")
    gate_raw_o = ap("gate_raw_o", [128, 24 * R], F32, "ExternalOutput")
    maxp2_o = ap("maxp2_o", [1, 1], F32, "ExternalOutput")
    maxp3_o = ap("maxp3_o", [1, 1], F32, "ExternalOutput")
    scr = nc.dram_tensor("scr", [1, 4], F32)

    BLK = NH * 65

    with ExitStack() as ctx:
        sbuf = lambda n, sh, dt: ctx.enter_context(nc.sbuf_tensor(n, sh, dt))[:, :]
        psum = lambda n, sh, dt: ctx.enter_context(nc.psum_tensor(n, sh, dt))[:, :]
        sem = lambda n: ctx.enter_context(nc.semaphore(n))
        sinf = sbuf("sinf", [64, BLK], F32)
        smidf = sbuf("smidf", [64, BLK], F32)
        stg = sbuf("stg", [64, BLK], F32)
        sin_bf = sbuf("sin_bf", [128, 8 * 65], BF16)
        smid_bf = sbuf("smid_bf", [128, 8 * 65], BF16)
        pq_b = sbuf("pq_b", [128, KC * 256], BF16)
        pk_b = sbuf("pk_b", [128, KC * 256], BF16)
        v_b = sbuf("v_b", [128, 2 * BLK], BF16)
        att_bf = sbuf("att_bf", [128, 128], BF16)
        rec1 = sbuf("rec1", [128, 1], F32)
        ctxt = sbuf("ctxt", [128, D], F32)
        ctx_bf = sbuf("ctx_bf", [128, D], BF16)
        ctx_fm = sbuf("ctx_fm", [128, NT * D // 8 * 8], BF16)
        w2b = sbuf("w2b", [128, KC * D], BF16)
        w2o = sbuf("w2o", [128, 512], F32)
        xa_st = sbuf("xa_st", [128, 4 * D], F32)
        xnt = sbuf("xnt", [128, D], F32)
        xq_i32 = sbuf("xq_i32", [128, D], I32)
        dm_s = sbuf("dm_s", [128, D], F32)
        rr_s = sbuf("rr_s", [128, D], F32)
        xq_bf = sbuf("xq_bf", [128, D], BF16)
        xq_fm = sbuf("xq_fm", [128, KC * R], BF16)
        wtile = sbuf("wtile", [128, D], F32)
        bmx = sbuf("bmx", [128, 32], F32)
        sblk = sbuf("sblk", [128, 32], F32)
        rsblk = sbuf("rsblk", [128, 32], F32)
        yw = sbuf("yw", [128, D], F32)
        iw = sbuf("iw", [128, D], I32)
        dw = sbuf("dw", [128, D], F32)
        wq_bf = sbuf("wq_bf", [128, D], BF16)
        wlhs = sbuf("wlhs", [128, D], BF16)
        upst = sbuf("upst", [128, 2 * 512], F32)
        runm = sbuf("runm", [128, 2], F32)
        cols = sbuf("cols", [128, 4], F32)
        m1row = sbuf("m1row", [1, 8], F32)
        m1sc = sbuf("m1sc", [1, 4], F32)
        mx1 = sbuf("mx1", [1, 128], F32)
        idb = sbuf("idb", [128, 128], BF16)
        idf = sbuf("idf", [128, 128], F32)
        trm = sbuf("trm", [128, 128], F32)
        ps_att = psum("ps_att", [128, 128], F32)
        ps_y = psum("ps_y", [128, 65], F32)
        ps_t = psum("ps_t", [128, D], BF16)
        ps_tf = psum("ps_tf", [128, 512], F32)
        ps_w2 = psum("ps_w2", [128, 512], F32)
        ps_m0 = psum("ps_m0", [128, 512], F32)
        ps_m1 = psum("ps_m1", [128, 512], F32)
        ps_mx = psum("ps_mx", [1, 128], F32)
        s_ld = sem("s_ld"); s_mm = sem("s_mm"); s_ev = sem("s_ev")
        s_st = sem("s_st"); s_aux = sem("s_aux"); s_cp = sem("s_cp")

        p = Prog(nc)
        ld = Sem(s_ld); mm = Sem(s_mm); ev = Sem(s_ev); st = Sem(s_st)
        aux = Sem(s_aux); cp = Sem(s_cp)

        p.dma("sync", idb, identb, ld)
        p.dma("sync", idf, identf, ld)
        p.dma("sync", trm, triu, ld)
        p.dma("sync", m1row, maxp_all.rearrange("a b -> b a"), ld)
        ld0 = ld.n

        p.wait("vector", ld, ld0)
        p.emit("vector", lambda e: e.reduce_max(out=m1sc[0:1, 0:1], in_=m1row[0:1, :], axis=AX.X))
        p.drain("vector")
        p.emit("vector", lambda e: e.tensor_scalar(out=m1sc[0:1, 1:2], in0=m1sc[0:1, 0:1],
                                                   scalar1=1.0 / 32767.0, scalar2=1e-12, op0=ALU.mult, op1=ALU.max))
        p.drain("vector")
        p.emit("vector", lambda e: e.reciprocal(out=m1sc[0:1, 2:3], in_=m1sc[0:1, 1:2]), inc=cp)
        p.wait("sync", cp)
        p.dma("sync", scr.ap()[0:1, 0:2], m1sc[0:1, 1:3], aux)
        p.wait("gpsimd", aux)
        p.dma("gpsimd", cols[:, 0:1], scr.ap()[0:1, 0:1].broadcast_to([128, 1]), ld)
        p.dma("gpsimd", cols[:, 1:2], scr.ap()[0:1, 1:2].broadcast_to([128, 1]), ld)
        p.emit("vector", lambda e: e.memset(cols[:, 2:3], 32768.0))
        p.emit("vector", lambda e: e.memset(cols[:, 3:4], 16.0), inc=cp)
        cols_rdy = (ld.n, cp.n)

        p.wait("scalar", ld, ld.n)
        p.wait("scalar", cp)
        for t in range(NT):
            p.wait("sync", ev)
            p.dma("sync", xnt, xn_i[t * 128:(t + 1) * 128, :], ld)
            p.wait("scalar", ld)
            p.emit("scalar", lambda e: e.activation(out=xq_i32, in_=xnt, func=AF.Identity,
                                                    bias=cols[:, 2:3], scale=cols[:, 1:2]), inc=cp)
            p.wait("vector", cp)
            p.emit("vector", lambda e: e.scalar_tensor_tensor(out=dm_s, in0=xnt, scalar=cols[:, 1:2],
                                                              in1=xq_i32, op0=ALU.mult, op1=ALU.subtract))
            p.drain("vector")
            p.emit("vector", lambda e: e.scalar_tensor_tensor(out=rr_s, in0=dm_s, scalar=-32767.5,
                                                              in1=xq_i32, op0=ALU.is_ge, op1=ALU.add))
            p.drain("vector")
            p.emit("vector", lambda e: e.tensor_scalar(out=xq_bf, in0=rr_s, scalar1=32768.0,
                                                       scalar2=cols[:, 0:1], op0=ALU.subtract, op1=ALU.mult), inc=cp)
            p.wait("tensor", cp)
            p.wait("tensor", ev)
            for c in range(KC):
                p.emit("tensor", lambda e, c=c: e.transpose(
                    ps_t[:, c * 128:(c + 1) * 128], xq_bf[:, c * 128:(c + 1) * 128], idb),
                    inc=(mm if c == KC - 1 else None))
            p.wait("scalar", mm)
            p.emit("scalar", lambda e, t=t: e.activation(
                out=xq_fm.rearrange("p (c j) -> p c j", j=R)[:, :, t * 128:(t + 1) * 128],
                in_=ps_t.rearrange("p (c j) -> p c j", j=128), func=AF.Copy), inc=ev)

        for b in range(8):
            p.wait("sync", ev)
            p.wait("sync", mm)
            p.dma("sync", pq_b.rearrange("p (c j) -> p c j", j=256),
                  phiq_fm.rearrange("p (c j) -> p c j", j=R)[:, :, b * 256:(b + 1) * 256], ld)
            p.dma("sync", pk_b.rearrange("p (c j) -> p c j", j=256),
                  phik_fm.rearrange("p (c j) -> p c j", j=R)[:, :, b * 256:(b + 1) * 256], ld)
            p.dma("sync", v_b.rearrange("p (c j) -> p c j", j=BLK),
                  v_tm[b * 256:(b + 1) * 256, :].rearrange("(c p) j -> p c j", p=128), ld)
            p.dma("sync", stg, sc0_i[0:64, b * BLK:(b + 1) * BLK], ld)
            p.wait("vector", ld)
            p.emit("vector", lambda e: e.memset(sinf, 0.0), inc=cp)
            for j in range(7):
                p.wait("sync", cp)
                p.dma("sync", smidf, ssum_prev[j, 0:64, b * BLK:(b + 1) * BLK], ld)
                p.wait("vector", ld)
                p.emit("vector", lambda e: e.tensor_tensor(out=sinf, in0=sinf, in1=smidf, op=ALU.add), inc=cp)
            p.wait("vector", cp)
            p.emit("vector", lambda e: e.tensor_tensor(out=smidf, in0=sinf, in1=stg, op=ALU.add), inc=cp)
            p.wait("gpsimd", cp)
            p.wait("gpsimd", mm)
            for src, dst in ((sinf, sin_bf), (smidf, smid_bf)):
                sv = src.rearrange("p (h2 two e) -> p h2 two e", two=2, e=65)
                dv = dst.rearrange("p (h2 e) -> p h2 e", e=65)
                p.dma("gpsimd", dv[0:64, :, :], sv[0:64, :, 0, :], ld)
                p.dma("gpsimd", dv[64:128, :, :], sv[0:64, :, 1, :], ld)
            p.wait("tensor", ld)
            p.wait("tensor", cp)
            for chn in range(2):
                for hh in range(NH):
                    cc, po = hh // 2, (hh % 2) * 64
                    lq = pq_b[po:po + 64, cc * 256 + chn * 128: cc * 256 + (chn + 1) * 128]
                    lk = pk_b[po:po + 64, cc * 256 + chn * 128: cc * 256 + (chn + 1) * 128]
                    p.emit("tensor", lambda e, lk=lk, lq=lq: e.matmul(ps_att, lk, lq, start=True, stop=True), inc=mm)
                    p.wait("vector", mm)
                    p.emit("vector", lambda e: e.tensor_tensor(out=att_bf, in0=ps_att, in1=trm, op=ALU.mult), inc=ev)
                    p.wait("tensor", ev)
                    sbf = sin_bf if chn == 0 else smid_bf
                    p.emit("tensor", lambda e, chn=chn, hh=hh: e.matmul(
                        ps_y, att_bf, v_b[:, chn * BLK + hh * 65: chn * BLK + (hh + 1) * 65],
                        start=True, stop=False))
                    p.emit("tensor", lambda e, lq=lq, sbf=sbf, hh=hh, po=po: e.matmul(
                        ps_y, lq, sbf[po:po + 64, (hh // 2) * 65:(hh // 2 + 1) * 65],
                        start=False, stop=True), inc=mm)
                    p.wait("vector", mm)
                    p.emit("vector", lambda e: e.reciprocal(out=rec1, in_=ps_y[:, 64:65]))
                    p.drain("vector")
                    p.emit("vector", lambda e, hh=hh: e.tensor_scalar(
                        out=ctxt[:, hh * 64:(hh + 1) * 64], in0=ps_y[:, 0:64],
                        scalar1=rec1[:, 0:1], scalar2=None, op0=ALU.mult), inc=ev)
                p.wait("scalar", ev)
                p.emit("scalar", lambda e: e.activation(out=ctx_bf, in_=ctxt, func=AF.Copy), inc=cp)
                p.wait("tensor", cp)
                p.wait("tensor", ev)
                for c in range(KC):
                    p.emit("tensor", lambda e, c=c: e.transpose(
                        ps_t[:, c * 128:(c + 1) * 128], ctx_bf[:, c * 128:(c + 1) * 128], idb),
                        inc=(mm if c == KC - 1 else None))
                p.wait("scalar", mm)
                t_idx = b * 2 + chn
                p.emit("scalar", lambda e, t_idx=t_idx: e.activation(
                    out=ctx_fm.rearrange("p (c j) -> p c j", j=R)[:, :, t_idx * 128:(t_idx + 1) * 128],
                    in_=ps_t.rearrange("p (c j) -> p c j", j=128), func=AF.Copy), inc=ev)

        for c in range(KC):
            p.dma("gpsimd", w2b[:, c * D:(c + 1) * D], w2t[c * 128:(c + 1) * 128, :], ld)
        w2_ld = ld.n
        p.wait("tensor", ld, w2_ld)
        p.wait("tensor", ev)
        for tokc in range(4):
            p.wait("sync", ev if tokc == 0 else st)
            for tb in range(4):
                p.dma("sync", xa_st[:, tb * D:(tb + 1) * D],
                      xin[(tokc * 4 + tb) * 128:(tokc * 4 + tb + 1) * 128, :], ld)
            xa_ld = ld.n
            p.wait("vector", ld, xa_ld)
            for oc in range(KC):
                for c in range(KC):
                    p.emit("tensor", lambda e, oc=oc, c=c, tokc=tokc: e.matmul(
                        ps_w2, w2b[:, c * D + oc * 128: c * D + (oc + 1) * 128],
                        ctx_fm[:, c * R + tokc * 512: c * R + (tokc + 1) * 512],
                        start=(c == 0), stop=(c == KC - 1)), inc=(mm if c == KC - 1 else None))
                p.wait("scalar", mm)
                p.emit("scalar", lambda e: e.activation(out=w2o, in_=ps_w2, func=AF.Copy), inc=ev)
                p.wait("tensor", ev)
                for tb in range(4):
                    p.emit("tensor", lambda e, tb=tb: e.transpose(
                        ps_tf[:, tb * 128:(tb + 1) * 128], w2o[:, tb * 128:(tb + 1) * 128], idf),
                        inc=(mm if tb == 3 else None))
                p.wait("vector", mm)
                p.emit("vector", lambda e, oc=oc: e.tensor_tensor(
                    out=xa_st.rearrange("p (t j) -> p t j", j=D)[:, :, oc * 128:(oc + 1) * 128],
                    in0=xa_st.rearrange("p (t j) -> p t j", j=D)[:, :, oc * 128:(oc + 1) * 128],
                    in1=ps_tf.rearrange("p (t j) -> p t j", j=128), op=ALU.add), inc=ev)
                p.wait("tensor", ev)
            p.wait("sync", ev)
            for tb in range(4):
                p.dma("sync", xa_o[(tokc * 4 + tb) * 128:(tokc * 4 + tb + 1) * 128, :],
                      xa_st[:, tb * D:(tb + 1) * D], st)

        p.emit("vector", lambda e: e.memset(runm, 0.0), inc=cp)
        for wi, (wsrc, dst, psm) in enumerate(((wup, up_raw_o, ps_m0), (wgate, gate_raw_o, ps_m1))):
            for oc in range(24):
                p.wait("sync", cp)
                p.dma("sync", wtile, wsrc[oc * 128:(oc + 1) * 128, :], ld)
                p.wait("vector", ld)
                p.emit("vector", lambda e: e.tensor_reduce(
                    out=bmx, in_=wtile.rearrange("p (g x) -> p g x", x=32), axis=AX.X,
                    op=ALU.max, apply_absolute_value=True))
                p.drain("vector")
                p.emit("vector", lambda e: e.tensor_scalar(out=sblk, in0=bmx, scalar1=1.0 / 7.0,
                                                           scalar2=1e-12, op0=ALU.mult, op1=ALU.max))
                p.drain("vector")
                p.emit("vector", lambda e: e.reciprocal(out=rsblk, in_=sblk))
                p.drain("vector")
                p.emit("vector", lambda e: e.tensor_tensor(
                    out=yw.rearrange("p (g x) -> p g x", x=32), in0=wtile.rearrange("p (g x) -> p g x", x=32),
                    in1=rsblk.rearrange("p (g o) -> p g o", o=1).broadcast_to([128, 32, 32]), op=ALU.mult), inc=cp)
                p.wait("scalar", cp)
                p.emit("scalar", lambda e: e.activation(out=iw, in_=yw, func=AF.Identity,
                                                        bias=cols[:, 3:4], scale=1.0), inc=cp)
                p.wait("vector", cp)
                p.emit("vector", lambda e: e.scalar_tensor_tensor(out=dw, in0=yw, scalar=16.0,
                                                                  in1=iw, op0=ALU.add, op1=ALU.subtract))
                p.drain("vector")
                p.emit("vector", lambda e: e.scalar_tensor_tensor(out=dw, in0=dw, scalar=0.5,
                                                                  in1=iw, op0=ALU.is_ge, op1=ALU.add))
                p.drain("vector")
                p.emit("vector", lambda e: e.tensor_scalar(out=dw, in0=dw, scalar1=16.0, scalar2=None,
                                                           op0=ALU.subtract))
                p.drain("vector")
                p.emit("vector", lambda e: e.tensor_tensor(
                    out=wq_bf.rearrange("p (g x) -> p g x", x=32), in0=dw.rearrange("p (g x) -> p g x", x=32),
                    in1=sblk.rearrange("p (g o) -> p g o", o=1).broadcast_to([128, 32, 32]), op=ALU.mult), inc=cp)
                p.wait("tensor", cp)
                p.wait("tensor", ev)
                for c in range(KC):
                    p.emit("tensor", lambda e, c=c: e.transpose(
                        ps_t[:, c * 128:(c + 1) * 128], wq_bf[:, c * 128:(c + 1) * 128], idb),
                        inc=(mm if c == KC - 1 else None))
                p.wait("scalar", mm)
                p.emit("scalar", lambda e: e.activation(out=wlhs, in_=ps_t, func=AF.Copy), inc=ev)
                p.wait("tensor", ev)
                for tokc in range(4):
                    pp = psm if tokc % 2 == 0 else (ps_m1 if psm is ps_m0 else ps_m0)
                    for c in range(KC):
                        p.emit("tensor", lambda e, pp=pp, c=c, tokc=tokc: e.matmul(
                            pp, wlhs[:, c * 128:(c + 1) * 128],
                            xq_fm[:, c * R + tokc * 512: c * R + (tokc + 1) * 512],
                            start=(c == 0), stop=(c == KC - 1)), inc=(mm if c == KC - 1 else None))
                    p.wait("scalar", mm)
                    sl = upst[:, (tokc % 2) * 512:(tokc % 2 + 1) * 512]
                    p.emit("scalar", lambda e, sl=sl, pp=pp: e.activation(out=sl, in_=pp, func=AF.Copy), inc=ev)
                    p.wait("sync", ev)
                    p.dma("sync", dst.rearrange("p (o j) -> p o j", j=R)[:, oc, tokc * 512:(tokc + 1) * 512], sl, st)
                    p.wait("vector", ev)
                    p.emit("vector", lambda e, sl=sl: e.tensor_reduce(
                        out=bmx[:, 0:1], in_=sl, axis=AX.X, op=ALU.max, apply_absolute_value=True))
                    p.drain("vector")
                    p.emit("vector", lambda e, wi=wi: e.tensor_tensor(
                        out=runm[:, wi:wi + 1], in0=runm[:, wi:wi + 1], in1=bmx[:, 0:1], op=ALU.max), inc=cp)
                    p.wait("tensor", cp)
                p.wait("scalar", st)
        for wi, dst in ((0, maxp2_o), (1, maxp3_o)):
            p.wait("tensor", cp)
            if wi == 1:
                p.wait("tensor", ev)
            p.emit("tensor", lambda e, wi=wi: e.transpose(ps_mx[0:1, 0:128], runm[:, wi:wi + 1], idf), inc=mm)
            p.wait("vector", mm)
            p.emit("vector", lambda e: e.tensor_copy(out=mx1, in_=ps_mx[0:1, :]), inc=ev)
            p.drain("vector")
            p.emit("vector", lambda e, wi=wi: e.reduce_max(out=m1sc[0:1, 3:4], in_=mx1[0:1, :], axis=AX.X), inc=ev)
            p.wait("sync", ev)
            p.dma("sync", dst, m1sc[0:1, 3:4], st)
            p.wait("vector", st)

        p.wait("gpsimd", st)
        p.build()
    return nc


def _emit_scales(p, nc_scr, mrow, msc, cols_s, cols_rs, ld, cp, aux, idx):
    p.emit("vector", lambda e: e.reduce_max(out=msc[0:1, 0:1], in_=mrow[0:1, :], axis=AX.X))
    p.drain("vector")
    p.emit("vector", lambda e: e.tensor_scalar(out=msc[0:1, 1:2], in0=msc[0:1, 0:1],
                                               scalar1=1.0 / 32767.0, scalar2=1e-12, op0=ALU.mult, op1=ALU.max))
    p.drain("vector")
    p.emit("vector", lambda e: e.reciprocal(out=msc[0:1, 2:3], in_=msc[0:1, 1:2]), inc=cp)
    p.wait("sync", cp)
    p.dma("sync", nc_scr.ap()[0:1, idx:idx + 2], msc[0:1, 1:3], aux)
    p.wait("gpsimd", aux)
    p.dma("gpsimd", cols_s, nc_scr.ap()[0:1, idx:idx + 1].broadcast_to([128, 1]), ld)
    p.dma("gpsimd", cols_rs, nc_scr.ap()[0:1, idx + 1:idx + 2].broadcast_to([128, 1]), ld)


def _emit_qdq(p, src, dst, i32b, dmb, s_col, rs_col, b_col, cp):
    p.wait("scalar", cp)
    p.emit("scalar", lambda e: e.activation(out=i32b, in_=src, func=AF.Identity,
                                            bias=b_col, scale=rs_col), inc=cp)
    p.wait("vector", cp)
    p.emit("vector", lambda e: e.scalar_tensor_tensor(out=dmb, in0=src, scalar=rs_col,
                                                      in1=i32b, op0=ALU.mult, op1=ALU.subtract))
    p.drain("vector")
    p.emit("vector", lambda e: e.scalar_tensor_tensor(out=dmb, in0=dmb, scalar=-32767.5,
                                                      in1=i32b, op0=ALU.is_ge, op1=ALU.add))
    p.drain("vector")
    p.emit("vector", lambda e: e.tensor_scalar(out=dst, in0=dmb, scalar1=32768.0,
                                               scalar2=s_col, op0=ALU.subtract, op1=ALU.mult), inc=cp)


def _emit_colmax_out(p, runcol, mx1, msc_sl, ps_mx, idf, dst, mm, ev, st):
    p.emit("tensor", lambda e: e.transpose(ps_mx[0:1, 0:128], runcol, idf), inc=mm)
    p.wait("vector", mm)
    p.emit("vector", lambda e: e.tensor_copy(out=mx1, in_=ps_mx[0:1, :]), inc=ev)
    p.drain("vector")
    p.emit("vector", lambda e: e.reduce_max(out=msc_sl, in_=mx1[0:1, :], axis=AX.X), inc=ev)
    p.wait("sync", ev)
    p.dma("sync", dst, msc_sl, st)
    p.wait("vector", st)


def build_launch3():
    nc = bass.Bass("TRN2", target_bir_lowering=False)
    ap = lambda n, sh, dt, kind: nc.dram_tensor(n, sh, dt, kind=kind).ap()
    up_raw_i = ap("up_raw_i", [128, 24 * R], F32, "ExternalInput")
    gate_raw_i = ap("gate_raw_i", [128, 24 * R], F32, "ExternalInput")
    maxp2_all = ap("maxp2_all", [8, 1], F32, "ExternalInput")
    maxp3_all = ap("maxp3_all", [8, 1], F32, "ExternalInput")
    identf = ap("identf", [128, 128], F32, "ExternalInput")
    upq_o = ap("upq_o", [128, 24 * R], F32, "ExternalOutput")
    t_o = ap("t_o", [128, 24 * R], F32, "ExternalOutput")
    maxp4_o = ap("maxp4_o", [1, 1], F32, "ExternalOutput")
    scr = nc.dram_tensor("scr", [1, 8], F32)

    with ExitStack() as ctx:
        sbuf = lambda n, sh, dt: ctx.enter_context(nc.sbuf_tensor(n, sh, dt))[:, :]
        psum = lambda n, sh, dt: ctx.enter_context(nc.psum_tensor(n, sh, dt))[:, :]
        sem = lambda n: ctx.enter_context(nc.semaphore(n))
        upc = sbuf("upc", [128, R], F32)
        gac = sbuf("gac", [128, R], F32)
        i32b = sbuf("i32b", [128, R], I32)
        dmb = sbuf("dmb", [128, R], F32)
        gq = sbuf("gq", [128, R], F32)
        sg = sbuf("sg", [128, R], F32)
        outb = sbuf("outb", [128, R], F32)
        m2row = sbuf("m2row", [1, 8], F32)
        m3row = sbuf("m3row", [1, 8], F32)
        msc = sbuf("msc", [1, 8], F32)
        cols = sbuf("cols", [128, 8], F32)
        runm = sbuf("runm", [128, 1], F32)
        mx1 = sbuf("mx1", [1, 128], F32)
        idf = sbuf("idf", [128, 128], F32)
        ps_mx = psum("ps_mx", [1, 128], F32)
        s_ld = sem("s_ld"); s_cp = sem("s_cp"); s_ev = sem("s_ev")
        s_st = sem("s_st"); s_aux = sem("s_aux"); s_mm = sem("s_mm")
        p = Prog(nc)
        ld = Sem(s_ld); cp = Sem(s_cp); ev = Sem(s_ev); st = Sem(s_st)
        aux = Sem(s_aux); mm = Sem(s_mm)

        p.dma("sync", m2row, maxp2_all.rearrange("a b -> b a"), ld)
        p.dma("sync", m3row, maxp3_all.rearrange("a b -> b a"), ld)
        p.dma("sync", idf, identf, ld)
        p.emit("vector", lambda e: e.memset(cols[:, 4:5], 32768.0))
        p.emit("vector", lambda e: e.memset(runm, 0.0), inc=cp)
        p.wait("vector", ld)
        _emit_scales(p, scr, m2row, msc, cols[:, 0:1], cols[:, 1:2], ld, cp, aux, 0)
        _emit_scales(p, scr, m3row, msc, cols[:, 2:3], cols[:, 3:4], ld, cp, aux, 2)
        p.wait("scalar", ld)
        p.wait("scalar", cp)
        p.wait("vector", ld)
        for oc in range(24):
            sl = lambda tsr: tsr
            p.wait("sync", cp)
            p.wait("sync", st)
            p.dma("sync", upc, up_raw_i[:, oc * R:(oc + 1) * R], ld)
            p.dma("sync", gac, gate_raw_i[:, oc * R:(oc + 1) * R], ld)
            p.wait("scalar", ld)
            p.wait("vector", ld)
            _emit_qdq(p, upc, outb, i32b, dmb, cols[:, 0:1], cols[:, 1:2], cols[:, 4:5], cp)
            p.wait("sync", cp)
            p.dma("sync", upq_o[:, oc * R:(oc + 1) * R], outb, st)
            _emit_qdq(p, gac, gq, i32b, dmb, cols[:, 2:3], cols[:, 3:4], cols[:, 4:5], cp)
            p.wait("scalar", cp)
            p.emit("scalar", lambda e: e.activation(out=sg, in_=gq, func=AF.Sigmoid), inc=cp)
            p.emit("scalar", lambda e: e.activation(out=i32b, in_=sg, func=AF.Identity, scale=65536.0), inc=cp)
            p.wait("vector", cp)
            p.emit("vector", lambda e: e.scalar_tensor_tensor(out=dmb, in0=sg, scalar=65536.0,
                                                              in1=i32b, op0=ALU.mult, op1=ALU.subtract))
            p.drain("vector")
            p.emit("vector", lambda e: e.scalar_tensor_tensor(out=dmb, in0=dmb, scalar=0.5,
                                                              in1=i32b, op0=ALU.is_ge, op1=ALU.add))
            p.drain("vector")
            p.emit("vector", lambda e: e.tensor_scalar(out=sg, in0=dmb, scalar1=65535.0,
                                                       scalar2=1.0 / 65536.0, op0=ALU.min, op1=ALU.mult))
            p.drain("vector")
            p.wait("vector", st)
            p.emit("vector", lambda e: e.tensor_tensor(out=outb, in0=gq, in1=sg, op=ALU.mult), inc=cp)
            p.wait("sync", cp)
            p.dma("sync", t_o[:, oc * R:(oc + 1) * R], outb, st)
            p.emit("vector", lambda e: e.tensor_reduce(out=dmb[:, 0:1], in_=outb, axis=AX.X,
                                                       op=ALU.max, apply_absolute_value=True))
            p.drain("vector")
            p.emit("vector", lambda e: e.tensor_tensor(out=runm, in0=runm, in1=dmb[:, 0:1], op=ALU.max), inc=cp)
        p.wait("tensor", cp)
        _emit_colmax_out(p, runm[:, 0:1], mx1, msc[0:1, 7:8], ps_mx, idf, maxp4_o, mm, ev, st)
        p.wait("gpsimd", st)
        p.build()
    return nc


def build_launch4():
    nc = bass.Bass("TRN2", target_bir_lowering=False)
    ap = lambda n, sh, dt, kind: nc.dram_tensor(n, sh, dt, kind=kind).ap()
    t_i = ap("t_i", [128, 24 * R], F32, "ExternalInput")
    upq_i = ap("upq_i", [128, 24 * R], F32, "ExternalInput")
    maxp4_all = ap("maxp4_all", [8, 1], F32, "ExternalInput")
    identf = ap("identf", [128, 128], F32, "ExternalInput")
    oraw_o = ap("oraw_o", [128, 24 * R], F32, "ExternalOutput")
    maxp5_o = ap("maxp5_o", [1, 1], F32, "ExternalOutput")
    scr = nc.dram_tensor("scr", [1, 8], F32)

    with ExitStack() as ctx:
        sbuf = lambda n, sh, dt: ctx.enter_context(nc.sbuf_tensor(n, sh, dt))[:, :]
        psum = lambda n, sh, dt: ctx.enter_context(nc.psum_tensor(n, sh, dt))[:, :]
        sem = lambda n: ctx.enter_context(nc.semaphore(n))
        tc = sbuf("tc", [128, R], F32)
        uc = sbuf("uc", [128, R], F32)
        i32b = sbuf("i32b", [128, R], I32)
        dmb = sbuf("dmb", [128, R], F32)
        g2 = sbuf("g2", [128, R], F32)
        outb = sbuf("outb", [128, R], F32)
        m4row = sbuf("m4row", [1, 8], F32)
        msc = sbuf("msc", [1, 8], F32)
        cols = sbuf("cols", [128, 4], F32)
        runm = sbuf("runm", [128, 1], F32)
        mx1 = sbuf("mx1", [1, 128], F32)
        idf = sbuf("idf", [128, 128], F32)
        ps_mx = psum("ps_mx", [1, 128], F32)
        s_ld = sem("s_ld"); s_cp = sem("s_cp"); s_ev = sem("s_ev")
        s_st = sem("s_st"); s_aux = sem("s_aux"); s_mm = sem("s_mm")
        p = Prog(nc)
        ld = Sem(s_ld); cp = Sem(s_cp); ev = Sem(s_ev); st = Sem(s_st)
        aux = Sem(s_aux); mm = Sem(s_mm)

        p.dma("sync", m4row, maxp4_all.rearrange("a b -> b a"), ld)
        p.dma("sync", idf, identf, ld)
        p.emit("vector", lambda e: e.memset(cols[:, 2:3], 32768.0))
        p.emit("vector", lambda e: e.memset(runm, 0.0), inc=cp)
        p.wait("vector", ld)
        _emit_scales(p, scr, m4row, msc, cols[:, 0:1], cols[:, 1:2], ld, cp, aux, 0)
        p.wait("scalar", ld)
        p.wait("scalar", cp)
        for oc in range(24):
            p.wait("sync", cp)
            p.wait("sync", st)
            p.dma("sync", tc, t_i[:, oc * R:(oc + 1) * R], ld)
            p.dma("sync", uc, upq_i[:, oc * R:(oc + 1) * R], ld)
            p.wait("scalar", ld)
            p.wait("vector", ld)
            _emit_qdq(p, tc, g2, i32b, dmb, cols[:, 0:1], cols[:, 1:2], cols[:, 2:3], cp)
            p.emit("vector", lambda e: e.tensor_tensor(out=outb, in0=g2, in1=uc, op=ALU.mult), inc=cp)
            p.wait("sync", cp)
            p.dma("sync", oraw_o[:, oc * R:(oc + 1) * R], outb, st)
            p.emit("vector", lambda e: e.tensor_reduce(out=dmb[:, 0:1], in_=outb, axis=AX.X,
                                                       op=ALU.max, apply_absolute_value=True))
            p.drain("vector")
            p.emit("vector", lambda e: e.tensor_tensor(out=runm, in0=runm, in1=dmb[:, 0:1], op=ALU.max), inc=cp)
        p.wait("tensor", cp)
        _emit_colmax_out(p, runm[:, 0:1], mx1, msc[0:1, 7:8], ps_mx, idf, maxp5_o, mm, ev, st)
        p.wait("gpsimd", st)
        p.build()
    return nc


def build_launch5():
    nc = bass.Bass("TRN2", target_bir_lowering=False)
    ap = lambda n, sh, dt, kind: nc.dram_tensor(n, sh, dt, kind=kind).ap()
    oraw_i = ap("oraw_i", [128, 24 * R], F32, "ExternalInput")
    maxp5_all = ap("maxp5_all", [8, 1], F32, "ExternalInput")
    xa_i = ap("xa_i", [R, D], F32, "ExternalInput")
    wdown = ap("wdown", [D, 3 * D], F32, "ExternalInput")
    ipw_t = ap("ipw_t", [D, 3 * D], F32, "ExternalInput")
    ipb_row = ap("ipb_row", [1, 3 * D], F32, "ExternalInput")
    opw_t = ap("opw_t", [D, D], F32, "ExternalInput")
    opb_row = ap("opb_row", [1, D], F32, "ExternalInput")
    rt_t = ap("rt_t", [D, 32], F32, "ExternalInput")
    rb_row = ap("rb_row", [1, 32], F32, "ExternalInput")
    gat = ap("gat", [D, 4096], F32, "ExternalInput")
    upt = ap("upt", [D, 4096], F32, "ExternalInput")
    dat = ap("dat", [4096, D], F32, "ExternalInput")
    sgt = ap("sgt", [D, 512], F32, "ExternalInput")
    sut = ap("sut", [D, 512], F32, "ExternalInput")
    sdt = ap("sdt", [512, D], F32, "ExternalInput")
    identb = ap("identb", [128, 128], BF16, "ExternalInput")
    identf = ap("identf", [128, 128], F32, "ExternalInput")
    out_o = ap("out_o", [R, D], F32, "ExternalOutput")
    scr = nc.dram_tensor("scr", [1, 8], F32)
    wt_dram = nc.dram_tensor("wt_dram", [32, R], F32)

    with ExitStack() as ctx:
        sbuf = lambda n, sh, dt: ctx.enter_context(nc.sbuf_tensor(n, sh, dt))[:, :]
        psum = lambda n, sh, dt: ctx.enter_context(nc.psum_tensor(n, sh, dt))[:, :]
        sem = lambda n: ctx.enter_context(nc.semaphore(n))
        x1 = sbuf("x1", [128, NT * D], F32)
        xn1_fm = sbuf("xn1_fm", [128, KC * R], BF16)
        big1 = sbuf("big1", [128, 12288], F32)
        big2 = sbuf("big2", [128, 8192], F32)
        big3 = sbuf("big3", [128, 4096], F32)
        q512b = sbuf("q512b", [128, 512], F32)
        q512c = sbuf("q512c", [128, 512], F32)
        ev512 = sbuf("ev512", [128, 512], F32)
        bias5 = q512b
        wpool = sbuf("wpool", [128, 4096], BF16)
        dlhs = wpool
        gulhs = wpool[:, 0:2048]
        oplhs = wpool[:, 0:1024]
        rtb = wpool[:, 0:256]
        sglhs = wpool[:, 0:1024]
        sulhs = wpool[:, 1024:2048]
        sdlhs = wpool[:, 0:512]
        wbc = big1[:, 0:512]
        wtstg = big3[0:32, 1024:1152]
        wall = sbuf("wall", [128, 32], F32)
        opb = sbuf("opb", [128, 8], F32)
        rbb = sbuf("rbb", [128, 32], F32)
        bmx3 = sbuf("bmx3", [128, 96], F32)
        stat = sbuf("stat", [128, 8], F32)
        cols = sbuf("cols", [128, 4], F32)
        m5row = sbuf("m5row", [1, 8], F32)
        msc = sbuf("msc", [1, 8], F32)
        idb = sbuf("idb", [128, 128], BF16)
        idf = sbuf("idf", [128, 128], F32)
        ps_a = psum("ps_a", [128, 512], F32)
        ps_b = psum("ps_b", [128, 512], F32)
        ps_c = psum("ps_c", [128, 512], F32)
        ps_d = psum("ps_d", [128, 512], F32)
        ps_t = psum("ps_t", [128, D], BF16)
        ps_tf = psum("ps_tf", [128, 512], F32)
        s_ld = sem("s_ld"); s_cp = sem("s_cp"); s_ev = sem("s_ev")
        s_st = sem("s_st"); s_aux = sem("s_aux"); s_mm = sem("s_mm")
        s_lncp = sem("s_lncp")
        p = Prog(nc)
        p.sems = {"lncp": Sem(s_lncp)}
        ld = Sem(s_ld); cp = Sem(s_cp); ev = Sem(s_ev); st = Sem(s_st)
        aux = Sem(s_aux); mm = Sem(s_mm)

        p.dma("sync", m5row, maxp5_all.rearrange("a b -> b a"), ld)
        p.dma("sync", idb, identb, ld)
        p.dma("sync", idf, identf, ld)
        p.emit("vector", lambda e: e.memset(cols[:, 2:3], 32768.0), inc=cp)
        p.wait("vector", ld)
        _emit_scales(p, scr, m5row, msc, cols[:, 0:1], cols[:, 1:2], ld, cp, aux, 0)
        p.dma("gpsimd", opb, opb_row.rearrange("a (c p) -> p a c", p=128)[:, 0, :], ld,
              allow_slow_non_contiguous=True)
        p.dma("gpsimd", rbb, rb_row.broadcast_to([128, 32]), ld)
        for t in range(NT):
            p.dma("sync", x1[:, t * D:(t + 1) * D], xa_i[t * 128:(t + 1) * 128, :], ld)
        xa_ld = ld.n

        wdlhs = big1.bitcast(BF16)
        wtile = big3[:, 0:1024]
        yw = big3[:, 1024:2048]
        iw = big3[:, 2048:3072].bitcast(I32)[:, 0:1024]
        dw = big3[:, 3072:4096]
        wq_bf = ev512.bitcast(BF16)
        p.emit("vector", lambda e: e.memset(stat[:, 0:1], 16.0), inc=cp)
        p.wait("scalar", ld, xa_ld)
        for oc in range(8):
            for pc in range(3):
                p.wait("sync", cp)
                p.dma("sync", wtile, wdown[oc * 128:(oc + 1) * 128, pc * 1024:(pc + 1) * 1024], ld)
                p.wait("vector", ld)
                p.emit("vector", lambda e: e.tensor_reduce(
                    out=bmx3[:, 0:32], in_=wtile.rearrange("p (g x) -> p g x", x=32), axis=AX.X,
                    op=ALU.max, apply_absolute_value=True))
                p.drain("vector")
                p.emit("vector", lambda e: e.tensor_scalar(out=bmx3[:, 32:64], in0=bmx3[:, 0:32],
                                                           scalar1=1.0 / 7.0, scalar2=1e-12, op0=ALU.mult, op1=ALU.max))
                p.drain("vector")
                p.emit("vector", lambda e: e.reciprocal(out=bmx3[:, 64:96], in_=bmx3[:, 32:64]))
                p.drain("vector")
                p.emit("vector", lambda e: e.tensor_tensor(
                    out=yw.rearrange("p (g x) -> p g x", x=32), in0=wtile.rearrange("p (g x) -> p g x", x=32),
                    in1=bmx3[:, 64:96].rearrange("p (g o) -> p g o", o=1).broadcast_to([128, 32, 32]),
                    op=ALU.mult), inc=cp)
                p.wait("scalar", cp)
                p.emit("scalar", lambda e: e.activation(out=iw, in_=yw, func=AF.Identity,
                                                        bias=stat[:, 0:1], scale=1.0), inc=cp)
                p.wait("vector", cp)
                p.emit("vector", lambda e: e.scalar_tensor_tensor(out=dw, in0=yw, scalar=16.0,
                                                                  in1=iw, op0=ALU.add, op1=ALU.subtract))
                p.drain("vector")
                p.emit("vector", lambda e: e.scalar_tensor_tensor(out=dw, in0=dw, scalar=0.5,
                                                                  in1=iw, op0=ALU.is_ge, op1=ALU.add))
                p.drain("vector")
                p.emit("vector", lambda e: e.tensor_scalar(out=dw, in0=dw, scalar1=16.0, scalar2=None,
                                                           op0=ALU.subtract))
                p.drain("vector")
                p.emit("vector", lambda e: e.tensor_tensor(
                    out=wq_bf.rearrange("p (g x) -> p g x", x=32), in0=dw.rearrange("p (g x) -> p g x", x=32),
                    in1=bmx3[:, 32:64].rearrange("p (g o) -> p g o", o=1).broadcast_to([128, 32, 32]),
                    op=ALU.mult), inc=cp)
                p.wait("tensor", cp)
                p.wait("tensor", ev)
                for c in range(8):
                    p.emit("tensor", lambda e, c=c: e.transpose(
                        ps_t[:, c * 128:(c + 1) * 128], wq_bf[:, c * 128:(c + 1) * 128], idb),
                        inc=(mm if c == 7 else None))
                p.wait("scalar", mm)
                p.emit("scalar", lambda e, oc=oc, pc=pc: e.activation(
                    out=wdlhs.rearrange("p (o k j) -> p o k j", o=8, j=128)[:, oc, pc * 8:(pc + 1) * 8, :],
                    in_=ps_t.rearrange("p (k j) -> p k j", j=128), func=AF.Copy), inc=ev)
        o_slab = big2.bitcast(BF16)[:, 0:24 * 512]
        p.wait("scalar", cp)
        p.wait("scalar", ev)
        for tq in range(4):
            for kc in range(24):
                p.wait("sync", cp)
                p.dma("sync", q512b, oraw_i[:, kc * R + tq * 512: kc * R + (tq + 1) * 512], ld)
                p.wait("scalar", ld)
                _emit_qdq(p, q512b, o_slab[:, kc * 512:(kc + 1) * 512],
                          q512c.bitcast(I32)[:, 0:512], ev512, cols[:, 0:1], cols[:, 1:2], cols[:, 2:3], cp)
            p.wait("tensor", cp)
            for oc in range(8):
                pp = ps_a if oc % 2 == 0 else ps_b
                for kc in range(24):
                    p.emit("tensor", lambda e, pp=pp, oc=oc, kc=kc: e.matmul(
                        pp, wdlhs.rearrange("p (o k j) -> p o k j", o=8, j=128)[:, oc, kc, :],
                        o_slab[:, kc * 512:(kc + 1) * 512],
                        start=(kc == 0), stop=(kc == 23)), inc=(mm if kc == 23 else None))
                p.wait("scalar", mm)
                p.wait("scalar", st)
                p.emit("scalar", lambda e, pp=pp: e.activation(out=ev512, in_=pp, func=AF.Copy), inc=ev)
                p.wait("tensor", ev)
                for tb in range(4):
                    p.emit("tensor", lambda e, tb=tb: e.transpose(
                        ps_tf[:, tb * 128:(tb + 1) * 128], ev512[:, tb * 128:(tb + 1) * 128], idf),
                        inc=(mm if tb == 3 else None))
                p.wait("vector", mm)
                p.wait("vector", ld, xa_ld)
                p.emit("vector", lambda e, tq=tq, oc=oc: e.tensor_tensor(
                    out=x1.rearrange("p (t j) -> p t j", j=D)[:, tq * 4:(tq + 1) * 4, oc * 128:(oc + 1) * 128],
                    in0=x1.rearrange("p (t j) -> p t j", j=D)[:, tq * 4:(tq + 1) * 4, oc * 128:(oc + 1) * 128],
                    in1=ps_tf.rearrange("p (t j) -> p t j", j=128), op=ALU.add), inc=ev)
                p.wait("tensor", ev)
            p.wait("scalar", ev)

        sa = big3[:, 0:1024]
        sb = big3[:, 1024:2048]
        xb = big3[:, 2048:2560].bitcast(BF16)[:, 0:1024]
        stats = {"m": stat[:, 1:2], "var": stat[:, 2:3], "sd": stat[:, 3:4],
                 "rstd": stat[:, 4:5], "eps_col": stat[:, 5:6]}
        p.emit("vector", lambda e: e.memset(stat[:, 5:6], EPS))
        p.wait("vector", ev)
        for t in range(NT):
            xt = x1[:, t * D:(t + 1) * D]
            _ln_tile(p, xt, sb, sa, stats)
            p.emit("vector", lambda e: e.tensor_copy(out=xb, in_=sb), inc=p.sems["lncp"])
            cpv = p.sems["lncp"].n
            p.wait("tensor", p.sems["lncp"], cpv)
            p.wait("tensor", ev)
            for c in range(KC):
                p.emit("tensor", lambda e, c=c: e.transpose(
                    ps_t[:, c * 128:(c + 1) * 128], xb[:, c * 128:(c + 1) * 128], idb),
                    inc=(mm if c == KC - 1 else None))
            p.wait("scalar", mm)
            p.emit("scalar", lambda e, t=t: e.activation(
                out=xn1_fm.rearrange("p (c j) -> p c j", j=R)[:, :, t * 128:(t + 1) * 128],
                in_=ps_t.rearrange("p (c j) -> p c j", j=128), func=AF.Copy), inc=ev)
            p.wait("vector", ev)

        qkv1 = big1.bitcast(BF16)
        wblk = big3[:, 0:2048].bitcast(BF16)
        prod = big3[:, 2048:3072]
        ctx_s = big3[:, 3072:4096]
        scoresb = q512c[:, 0:128]
        smx = q512c[:, 128:144]
        ssm = q512c[:, 144:160]
        src_ = q512c[:, 160:176]
        ctx_fm = big2.bitcast(BF16)[:, 0:16384]
        for pb in range(2):
            for oc in range(6):
                p.wait("gpsimd", mm)
                p.dma("gpsimd", wblk.rearrange("p (k j) -> p k j", j=512),
                      ipw_t.rearrange("(k p) f -> p k f", p=128)[:, :, oc * 512:(oc + 1) * 512], ld)
                p.wait("gpsimd", st)
                p.dma("gpsimd", bias5.rearrange("p (o j) -> p o j", o=1),
                      ipb_row[0:1, oc * 512:(oc + 1) * 512].rearrange("a (o j) -> a o j", o=1).broadcast_to([128, 1, 512]), ld)
                p.wait("tensor", ld)
                for s in range(8):
                    t_idx = s * 2 + pb
                    pp = ps_a if s % 2 == 0 else ps_b
                    p.wait("tensor", ev)
                    for kc in range(KC):
                        p.emit("tensor", lambda e, pp=pp, kc=kc, t_idx=t_idx: e.matmul(
                            pp, xn1_fm.rearrange("p (c j) -> p c j", j=R)[:, kc, t_idx * 128:(t_idx + 1) * 128],
                            wblk[:, kc * 512:(kc + 1) * 512],
                            start=(kc == 0), stop=(kc == KC - 1)), inc=(mm if kc == KC - 1 else None))
                    p.wait("vector", mm)
                    p.emit("vector", lambda e, pp=pp, s=s, oc=oc: e.tensor_tensor(
                        out=qkv1.rearrange("p (s f) -> p s f", f=3072)[:, s, oc * 512:(oc + 1) * 512],
                        in0=pp, in1=bias5, op=ALU.add), inc=ev)
            qv = qkv1.rearrange("p (s th d) -> p s th d", th=3 * NH, d=64)
            p.wait("vector", ev)
            p.wait("gpsimd", ev)
            for s in range(8):
                for t in range(8):
                    p.emit("vector", lambda e, s=s, t=t, qv=qv: e.tensor_tensor(
                        out=prod.rearrange("p (h d) -> p h d", d=64), in0=qv[:, s, 0:NH, :],
                        in1=qv[:, t, NH:2 * NH, :], op=ALU.mult))
                    p.drain("vector")
                    p.emit("vector", lambda e, t=t: e.tensor_reduce(
                        out=scoresb.rearrange("p (h t) -> p h t", t=8)[:, :, t:t + 1],
                        in_=prod.rearrange("p (h d) -> p h d", d=64), axis=AX.X, op=ALU.add))
                    p.drain("vector")
                p.emit("vector", lambda e: e.tensor_reduce(
                    out=smx, in_=scoresb.rearrange("p (h t) -> p h t", t=8), axis=AX.X, op=ALU.max))
                p.drain("vector")
                p.emit("vector", lambda e: e.tensor_tensor(
                    out=scoresb.rearrange("p (h t) -> p h t", t=8), in0=scoresb.rearrange("p (h t) -> p h t", t=8),
                    in1=smx.rearrange("p (h o) -> p h o", o=1).broadcast_to([128, 16, 8]), op=ALU.subtract), inc=cp)
                p.wait("scalar", cp)
                p.emit("scalar", lambda e: e.activation(out=scoresb, in_=scoresb, func=AF.Exp, scale=0.125), inc=cp)
                p.wait("vector", cp)
                p.emit("vector", lambda e: e.tensor_reduce(
                    out=ssm, in_=scoresb.rearrange("p (h t) -> p h t", t=8), axis=AX.X, op=ALU.add))
                p.drain("vector")
                p.emit("vector", lambda e: e.reciprocal(out=src_, in_=ssm))
                p.drain("vector")
                p.emit("vector", lambda e: e.tensor_tensor(
                    out=scoresb.rearrange("p (h t) -> p h t", t=8), in0=scoresb.rearrange("p (h t) -> p h t", t=8),
                    in1=src_.rearrange("p (h o) -> p h o", o=1).broadcast_to([128, 16, 8]), op=ALU.mult))
                p.drain("vector")
                p.emit("vector", lambda e: e.memset(ctx_s, 0.0))
                p.drain("vector")
                for t in range(8):
                    p.emit("vector", lambda e, t=t, qv=qv: e.tensor_tensor(
                        out=prod.rearrange("p (h d) -> p h d", d=64), in0=qv[:, t, 2 * NH:3 * NH, :],
                        in1=scoresb.rearrange("p (h t) -> p h t", t=8)[:, :, t:t + 1].broadcast_to([128, 16, 64]),
                        op=ALU.mult))
                    p.drain("vector")
                    p.emit("vector", lambda e: e.tensor_tensor(out=ctx_s, in0=ctx_s, in1=prod, op=ALU.add))
                    p.drain("vector")
                p.emit("vector", lambda e: e.tensor_copy(out=xb, in_=ctx_s), inc=cp)
                p.wait("tensor", cp)
                p.wait("tensor", ev)
                t_idx = s * 2 + pb
                for c in range(KC):
                    p.emit("tensor", lambda e, c=c: e.transpose(
                        ps_t[:, c * 128:(c + 1) * 128], xb[:, c * 128:(c + 1) * 128], idb),
                        inc=(mm if c == KC - 1 else None))
                p.wait("scalar", mm)
                p.emit("scalar", lambda e, t_idx=t_idx: e.activation(
                    out=ctx_fm.rearrange("p (c j) -> p c j", j=R)[:, :, t_idx * 128:(t_idx + 1) * 128],
                    in_=ps_t.rearrange("p (c j) -> p c j", j=128), func=AF.Copy), inc=ev)
                p.wait("vector", ev)

        for oc in range(8):
            p.wait("gpsimd", mm)
            p.dma("gpsimd", oplhs.rearrange("p (k j) -> p k j", j=128),
                  opw_t.rearrange("(k p) f -> p k f", p=128)[:, :, oc * 128:(oc + 1) * 128], ld)
            p.wait("tensor", ld)
            for tq in range(4):
                pp = ps_a if tq % 2 == 0 else ps_b
                p.wait("tensor", ev)
                for kc in range(KC):
                    p.emit("tensor", lambda e, pp=pp, kc=kc, oc=oc, tq=tq: e.matmul(
                        pp, oplhs[:, kc * 128:(kc + 1) * 128],
                        ctx_fm[:, kc * R + tq * 512: kc * R + (tq + 1) * 512],
                        start=(kc == 0), stop=(kc == KC - 1)), inc=(mm if kc == KC - 1 else None))
                p.wait("scalar", mm)
                p.emit("scalar", lambda e, pp=pp, oc=oc: e.activation(
                    out=ev512, in_=pp, func=AF.Identity, bias=opb[:, oc:oc + 1], scale=1.0), inc=ev)
                p.wait("tensor", ev)
                for tb in range(4):
                    p.emit("tensor", lambda e, tb=tb: e.transpose(
                        ps_tf[:, tb * 128:(tb + 1) * 128], ev512[:, tb * 128:(tb + 1) * 128], idf),
                        inc=(mm if tb == 3 else None))
                p.wait("vector", mm)
                p.emit("vector", lambda e, tq=tq, oc=oc: e.tensor_tensor(
                    out=x1.rearrange("p (t j) -> p t j", j=D)[:, tq * 4:(tq + 1) * 4, oc * 128:(oc + 1) * 128],
                    in0=x1.rearrange("p (t j) -> p t j", j=D)[:, tq * 4:(tq + 1) * 4, oc * 128:(oc + 1) * 128],
                    in1=ps_tf.rearrange("p (t j) -> p t j", j=128), op=ALU.add), inc=ev)
                p.wait("tensor", ev)

        p.wait("gpsimd", mm)
        p.dma("gpsimd", rtb.rearrange("p (k j) -> p k j", j=32),
              rt_t.rearrange("(k p) f -> p k f", p=128), ld)
        p.wait("tensor", ld)
        for t in range(NT):
            pp = ps_a if t % 2 == 0 else ps_b
            p.wait("tensor", ev)
            for kc in range(KC):
                p.emit("tensor", lambda e, pp=pp, kc=kc, t=t: e.matmul(
                    pp[:, 0:32],
                    xn1_fm.rearrange("p (c j) -> p c j", j=R)[:, kc, t * 128:(t + 1) * 128],
                    rtb[:, kc * 32:(kc + 1) * 32],
                    start=(kc == 0), stop=(kc == KC - 1)), inc=(mm if kc == KC - 1 else None))
            p.wait("vector", mm)
            wsl = wall[:, 0:32]
            p.emit("vector", lambda e, pp=pp, wsl=wsl: e.tensor_tensor(out=prod[:, 0:32], in0=pp[:, 0:32], in1=rbb, op=ALU.add), inc=ev)
            p.drain("vector")
            p.emit("vector", lambda e: e.reduce_max(out=stat[:, 6:7], in_=prod[:, 0:32], axis=AX.X))
            p.drain("vector")
            p.emit("vector", lambda e: e.tensor_scalar(out=prod[:, 0:32], in0=prod[:, 0:32],
                                                       scalar1=stat[:, 6:7], scalar2=None, op0=ALU.subtract), inc=cp)
            p.wait("scalar", cp)
            p.emit("scalar", lambda e: e.activation(out=prod[:, 0:32], in_=prod[:, 0:32], func=AF.Exp), inc=cp)
            p.wait("vector", cp)
            p.emit("vector", lambda e: e.reduce_sum(out=stat[:, 6:7], in_=prod[:, 0:32], axis=AX.X))
            p.drain("vector")
            p.emit("vector", lambda e: e.reciprocal(out=stat[:, 7:8], in_=stat[:, 6:7]))
            p.drain("vector")
            p.emit("vector", lambda e: e.tensor_scalar(out=prod[:, 0:32], in0=prod[:, 0:32],
                                                       scalar1=stat[:, 7:8], scalar2=None, op0=ALU.mult))
            p.drain("vector")
            p.emit("vector", lambda e: e.tensor_copy(out=prod[:, 32:64], in_=prod[:, 0:32]))
            p.emit("vector", lambda e, wsl=wsl: e.memset(wsl, 0.0))
            p.drain("vector")
            for it in range(4):
                p.emit("vector", lambda e: e.reduce_max(out=stat[:, 6:7], in_=prod[:, 32:64], axis=AX.X))
                p.drain("vector")
                p.emit("vector", lambda e: e.tensor_scalar(out=prod[:, 64:96], in0=prod[:, 32:64],
                                                           scalar1=stat[:, 6:7], scalar2=None, op0=ALU.is_ge))
                p.drain("vector")
                p.emit("vector", lambda e: e.tensor_tensor(out=prod[:, 96:128], in0=prod[:, 64:96], in1=prod[:, 0:32], op=ALU.mult))
                p.drain("vector")
                p.emit("vector", lambda e, wsl=wsl: e.tensor_tensor(out=wsl, in0=wsl, in1=prod[:, 96:128], op=ALU.add))
                p.emit("vector", lambda e: e.tensor_scalar(out=prod[:, 96:128], in0=prod[:, 64:96],
                                                           scalar1=-1.0, scalar2=1.0, op0=ALU.mult, op1=ALU.add))
                p.drain("vector")
                p.emit("vector", lambda e: e.tensor_tensor(out=prod[:, 32:64], in0=prod[:, 32:64], in1=prod[:, 96:128], op=ALU.mult))
                p.drain("vector")
                p.emit("vector", lambda e: e.tensor_tensor(out=prod[:, 32:64], in0=prod[:, 32:64], in1=prod[:, 64:96], op=ALU.subtract))
                p.drain("vector")
            p.emit("vector", lambda e, wsl=wsl: e.reduce_sum(out=stat[:, 6:7], in_=wsl, axis=AX.X))
            p.drain("vector")
            p.emit("vector", lambda e: e.tensor_scalar(out=stat[:, 6:7], in0=stat[:, 6:7],
                                                       scalar1=1e-12, scalar2=None, op0=ALU.max))
            p.drain("vector")
            p.emit("vector", lambda e: e.reciprocal(out=stat[:, 7:8], in_=stat[:, 6:7]))
            p.drain("vector")
            p.emit("vector", lambda e, wsl=wsl: e.tensor_scalar(out=wsl, in0=wsl,
                                                                scalar1=stat[:, 7:8], scalar2=None, op0=ALU.mult), inc=cp)
            p.wait("tensor", cp)
            p.wait("tensor", aux)
            p.emit("tensor", lambda e, wsl=wsl: e.transpose(ps_tf[0:32, 0:128], wsl, idf), inc=mm)
            p.wait("vector", mm)
            p.wait("vector", st)
            p.emit("vector", lambda e: e.tensor_copy(out=wtstg, in_=ps_tf[0:32, 0:128]), inc=aux)
            p.wait("sync", aux)
            p.dma("sync", wt_dram.ap()[0:32, t * 128:(t + 1) * 128], wtstg, st)
        p.wait("vector", st)

        h_fm = big2.bitcast(BF16)[:, 0:32 * 512]
        sil = big3[:, 0:512]
        h1 = big3[:, 512:1024]
        for tq in range(4):
            p.wait("vector", mm)
            for ei in range(32):
                pg = ps_a if ei % 2 == 0 else ps_c
                pu = ps_b if ei % 2 == 0 else ps_d
                p.wait("gpsimd", mm)
                p.dma("gpsimd", gulhs.rearrange("p (w k j) -> p w k j", w=2, j=128)[:, 0, :, :],
                      gat.rearrange("(k p) f -> p k f", p=128)[:, :, ei * 128:(ei + 1) * 128], ld)
                p.dma("gpsimd", gulhs.rearrange("p (w k j) -> p w k j", w=2, j=128)[:, 1, :, :],
                      upt.rearrange("(k p) f -> p k f", p=128)[:, :, ei * 128:(ei + 1) * 128], ld)
                p.wait("gpsimd", cp)
                p.dma("gpsimd", wbc.rearrange("p (o j) -> p o j", o=1),
                      wt_dram.ap()[ei:ei + 1, tq * 512:(tq + 1) * 512].rearrange("a (o j) -> a o j", o=1).broadcast_to([128, 1, 512]), ld)
                p.wait("tensor", ld)
                p.wait("tensor", cp)
                for kc in range(KC):
                    p.emit("tensor", lambda e, pg=pg, kc=kc, tq=tq: e.matmul(
                        pg, gulhs[:, kc * 128:(kc + 1) * 128],
                        xn1_fm[:, kc * R + tq * 512: kc * R + (tq + 1) * 512],
                        start=(kc == 0), stop=(kc == KC - 1)))
                    p.emit("tensor", lambda e, pu=pu, kc=kc, tq=tq: e.matmul(
                        pu, gulhs[:, 1024 + kc * 128: 1024 + (kc + 1) * 128],
                        xn1_fm[:, kc * R + tq * 512: kc * R + (tq + 1) * 512],
                        start=(kc == 0), stop=(kc == KC - 1)), inc=(mm if kc == KC - 1 else None))
                p.wait("scalar", mm)
                p.emit("scalar", lambda e, pg=pg: e.activation(out=sil, in_=pg, func=AF.Silu), inc=cp)
                p.wait("vector", cp)
                p.emit("vector", lambda e, pu=pu: e.tensor_tensor(out=h1, in0=pu, in1=sil, op=ALU.mult))
                p.drain("vector")
                p.emit("vector", lambda e, ei=ei: e.tensor_tensor(
                    out=h_fm[:, ei * 512:(ei + 1) * 512], in0=h1, in1=wbc, op=ALU.mult), inc=cp)
            p.wait("tensor", cp)
            for oc in range(8):
                pp = ps_a if oc % 2 == 0 else ps_b
                p.wait("gpsimd", mm)
                p.dma("gpsimd", dlhs.rearrange("p (k j) -> p k j", j=128),
                      dat.rearrange("(k p) f -> p k f", p=128)[:, :, oc * 128:(oc + 1) * 128], ld)
                p.wait("tensor", ld)
                for kc in range(32):
                    p.emit("tensor", lambda e, pp=pp, kc=kc: e.matmul(
                        pp, dlhs[:, kc * 128:(kc + 1) * 128], h_fm[:, kc * 512:(kc + 1) * 512],
                        start=(kc == 0), stop=(kc == 31)), inc=(mm if kc == 31 else None))
                p.wait("scalar", mm)
                p.emit("scalar", lambda e, pp=pp: e.activation(out=ev512, in_=pp, func=AF.Copy), inc=ev)
                p.wait("tensor", ev)
                for tb in range(4):
                    p.emit("tensor", lambda e, tb=tb: e.transpose(
                        ps_tf[:, tb * 128:(tb + 1) * 128], ev512[:, tb * 128:(tb + 1) * 128], idf),
                        inc=(mm if tb == 3 else None))
                p.wait("vector", mm)
                p.emit("vector", lambda e, tq=tq, oc=oc: e.tensor_tensor(
                    out=x1.rearrange("p (t j) -> p t j", j=D)[:, tq * 4:(tq + 1) * 4, oc * 128:(oc + 1) * 128],
                    in0=x1.rearrange("p (t j) -> p t j", j=D)[:, tq * 4:(tq + 1) * 4, oc * 128:(oc + 1) * 128],
                    in1=ps_tf.rearrange("p (t j) -> p t j", j=128), op=ALU.add), inc=ev)
                p.wait("tensor", ev)

        s_h = big3.bitcast(BF16)[:, 0:4 * R]
        for oc in range(4):
            p.wait("gpsimd", mm)
            p.dma("gpsimd", sglhs.rearrange("p (k j) -> p k j", j=128),
                  sgt.rearrange("(k p) f -> p k f", p=128)[:, :, oc * 128:(oc + 1) * 128], ld)
            p.dma("gpsimd", sulhs.rearrange("p (k j) -> p k j", j=128),
                  sut.rearrange("(k p) f -> p k f", p=128)[:, :, oc * 128:(oc + 1) * 128], ld)
            p.wait("tensor", ld)
            for tq in range(4):
                pg = ps_a if tq % 2 == 0 else ps_c
                pu = ps_b if tq % 2 == 0 else ps_d
                p.wait("tensor", cp)
                for kc in range(KC):
                    p.emit("tensor", lambda e, pg=pg, kc=kc, tq=tq: e.matmul(
                        pg, sglhs[:, kc * 128:(kc + 1) * 128],
                        xn1_fm[:, kc * R + tq * 512: kc * R + (tq + 1) * 512],
                        start=(kc == 0), stop=(kc == KC - 1)))
                    p.emit("tensor", lambda e, pu=pu, kc=kc, tq=tq: e.matmul(
                        pu, sulhs[:, kc * 128:(kc + 1) * 128],
                        xn1_fm[:, kc * R + tq * 512: kc * R + (tq + 1) * 512],
                        start=(kc == 0), stop=(kc == KC - 1)), inc=(mm if kc == KC - 1 else None))
                p.wait("scalar", mm)
                p.emit("scalar", lambda e, pg=pg: e.activation(out=q512b, in_=pg, func=AF.Silu), inc=cp)
                p.wait("vector", cp)
                p.emit("vector", lambda e, pu=pu, oc=oc, tq=tq: e.tensor_tensor(
                    out=s_h[:, oc * R + tq * 512: oc * R + (tq + 1) * 512],
                    in0=pu, in1=q512b, op=ALU.mult), inc=cp)
        p.wait("tensor", cp)
        for oc in range(8):
            p.wait("gpsimd", mm)
            p.dma("gpsimd", sdlhs.rearrange("p (k j) -> p k j", j=128),
                  sdt.rearrange("(k p) f -> p k f", p=128)[:, :, oc * 128:(oc + 1) * 128], ld)
            p.wait("tensor", ld)
            for tq in range(4):
                pp = ps_a if tq % 2 == 0 else ps_b
                p.wait("tensor", ev)
                for kc in range(4):
                    p.emit("tensor", lambda e, pp=pp, kc=kc, tq=tq: e.matmul(
                        pp, sdlhs[:, kc * 128:(kc + 1) * 128],
                        s_h[:, kc * R + tq * 512: kc * R + (tq + 1) * 512],
                        start=(kc == 0), stop=(kc == 3)), inc=(mm if kc == 3 else None))
                p.wait("scalar", mm)
                p.emit("scalar", lambda e, pp=pp: e.activation(out=ev512, in_=pp, func=AF.Copy), inc=ev)
                p.wait("tensor", ev)
                for tb in range(4):
                    p.emit("tensor", lambda e, tb=tb: e.transpose(
                        ps_tf[:, tb * 128:(tb + 1) * 128], ev512[:, tb * 128:(tb + 1) * 128], idf),
                        inc=(mm if tb == 3 else None))
                p.wait("vector", mm)
                p.emit("vector", lambda e, tq=tq, oc=oc: e.tensor_tensor(
                    out=x1.rearrange("p (t j) -> p t j", j=D)[:, tq * 4:(tq + 1) * 4, oc * 128:(oc + 1) * 128],
                    in0=x1.rearrange("p (t j) -> p t j", j=D)[:, tq * 4:(tq + 1) * 4, oc * 128:(oc + 1) * 128],
                    in1=ps_tf.rearrange("p (t j) -> p t j", j=128), op=ALU.add), inc=ev)
                p.wait("tensor", ev)

        p.wait("sync", ev)
        for t in range(NT):
            p.dma("sync", out_o[t * 128:(t + 1) * 128, :], x1[:, t * D:(t + 1) * D], st)
        p.wait("gpsimd", st)
        p.build()
    return nc


# version 02
# speedup vs baseline: 1.1148x; 1.1148x over previous
from contextlib import ExitStack
import numpy as np
import ml_dtypes
import concourse.bass as bass
import concourse.mybir as mybir
from concourse.bass_utils import run_bass_kernel_spmd

F32 = mybir.dt.float32
BF16 = mybir.dt.bfloat16
I32 = mybir.dt.int32
ALU = mybir.AluOpType
AF = mybir.ActivationFunctionType
AX = mybir.AxisListType

NCORES = 8
R = 2048
NT = 16
D = 1024
KC = 8
NH = 16
HD = 64
EPS = 1e-5


class Sem:
    __slots__ = ("h", "n")

    def __init__(self, h):
        self.h = h
        self.n = 0


class Prog:
    ENGINES = ("sync", "scalar", "vector", "tensor", "gpsimd")

    def __init__(self, nc):
        self.nc = nc
        self.ops = {e: [] for e in self.ENGINES}

    def emit(self, ek, make, inc=None, inck=1):
        if inc is not None:
            def fn(eng, make=make, s=inc, k=inck):
                make(eng).then_inc(s.h, k)
            inc.n += inck
        else:
            fn = make
        self.ops[ek].append(fn)

    def dma(self, ek, out, in_, inc, **kw):
        self.emit(ek, lambda eng, o=out, i=in_, kw=kw: eng.dma_start(out=o, in_=i, **kw),
                  inc=inc, inck=16)

    def wait(self, ek, sem, at=None):
        v = sem.n if at is None else at
        self.ops[ek].append(lambda eng, s=sem, vv=v: eng.wait_ge(s.h, vv))

    def drain(self, ek):
        self.ops[ek].append(lambda eng: eng.drain())

    def build(self):
        with self.nc.Block() as block:
            for ek in self.ENGINES:
                def runner(eng, ek=ek):
                    for fn in self.ops[ek]:
                        fn(eng)
                getattr(block, ek)(runner)


def _ln_tile(p, xs, xn_out, sa, stats, extra_absmax=None):
    cp = p.sems["lncp"]
    v, s = "vector", "scalar"
    m, var, sd, rstd, eps_col = stats["m"], stats["var"], stats["sd"], stats["rstd"], stats["eps_col"]
    p.emit(v, lambda e, a=xs, o=m: e.reduce_sum(out=o, in_=a, axis=AX.X))
    p.drain(v)
    p.emit(v, lambda e, o=m, a=m: e.tensor_scalar(out=o, in0=a, scalar1=1.0 / D, scalar2=None, op0=ALU.mult))
    p.drain(v)
    p.emit(v, lambda e, o=sa, a=xs, mm=m: e.tensor_scalar(out=o, in0=a, scalar1=mm, scalar2=None, op0=ALU.subtract), inc=cp)
    p.wait(s, cp)
    p.emit(s, lambda e, o=xn_out, a=sa: e.activation(out=o, in_=a, func=AF.Square), inc=cp)
    p.wait(v, cp)
    p.emit(v, lambda e, o=var, a=xn_out: e.reduce_sum(out=o, in_=a, axis=AX.X))
    p.drain(v)
    p.emit(v, lambda e, o=var, a=var: e.tensor_scalar(out=o, in0=a, scalar1=1.0 / D, scalar2=None, op0=ALU.mult), inc=cp)
    p.wait(s, cp)
    p.emit(s, lambda e, o=sd, a=var, b=eps_col: e.activation(out=o, in_=a, func=AF.Sqrt, bias=b), inc=cp)
    p.wait(v, cp)
    p.emit(v, lambda e, o=rstd, a=sd: e.reciprocal(out=o, in_=a))
    p.drain(v)
    p.emit(v, lambda e, o=xn_out, a=sa, rr=rstd: e.tensor_scalar(out=o, in0=a, scalar1=rr, scalar2=None, op0=ALU.mult), inc=cp)
    if extra_absmax is not None:
        p.drain(v)
        p.emit(v, lambda e, o=extra_absmax, a=xn_out: e.tensor_reduce(
            out=o, in_=a, axis=AX.X, op=ALU.max, apply_absolute_value=True))
        p.drain(v)


def build_launch1():
    nc = bass.Bass("TRN2", target_bir_lowering=False)
    xin = nc.dram_tensor("xin", [R, D], F32, kind="ExternalInput").ap()
    w1t = nc.dram_tensor("w1t", [D, 3 * D], F32, kind="ExternalInput").ap()
    identb = nc.dram_tensor("identb", [128, 128], BF16, kind="ExternalInput").ap()
    identf = nc.dram_tensor("identf", [128, 128], F32, kind="ExternalInput").ap()

    xn_o = nc.dram_tensor("xn_o", [R, D], F32, kind="ExternalOutput").ap()
    phiq_fm_o = nc.dram_tensor("phiq_fm_o", [128, KC * R], BF16, kind="ExternalOutput").ap()
    phik_fm_o = nc.dram_tensor("phik_fm_o", [128, KC * R], BF16, kind="ExternalOutput").ap()
    phik_tm_o = nc.dram_tensor("phik_tm_o", [R, D], BF16, kind="ExternalOutput").ap()
    v_tm_o = nc.dram_tensor("v_tm_o", [R, NH * (HD + 1)], BF16, kind="ExternalOutput").ap()
    ssum_o = nc.dram_tensor("ssum_o", [64, 128 * 65], F32, kind="ExternalOutput").ap()
    sc0_o = nc.dram_tensor("sc0_o", [64, 128 * 65], F32, kind="ExternalOutput").ap()
    maxp1_o = nc.dram_tensor("maxp1_o", [1, 1], F32, kind="ExternalOutput").ap()

    with ExitStack() as ctx:
        sbuf = lambda n, sh, dt: ctx.enter_context(nc.sbuf_tensor(n, sh, dt))[:, :]
        psum = lambda n, sh, dt: ctx.enter_context(nc.psum_tensor(n, sh, dt))[:, :]
        sem = lambda n: ctx.enter_context(nc.semaphore(n))
        xw = sbuf("xw", [128, NT * D], F32)
        sa = sbuf("sa", [128, D], F32)
        sb = sbuf("sb", [128, D], F32)
        stat = sbuf("stat", [128, 8], F32)
        pmx = sbuf("pmx", [128, NT], F32)
        mxc = sbuf("mxc", [128, 1], F32)
        mx1 = sbuf("mx1", [1, 128], F32)
        xnn_bf = sbuf("xnn_bf", [128, D], BF16)
        xnn_fm = sbuf("xnn_fm", [128, KC * R], BF16)
        w1b = sbuf("w1b", [128, KC * 3 * D], BF16)
        idb = sbuf("idb", [128, 128], BF16)
        idf = sbuf("idf", [128, 128], F32)
        phr = sbuf("phr", [128, D], F32)
        phe = sbuf("phe", [128, D], F32)
        phq_bf = sbuf("phq_bf", [128, D], BF16)
        phk_bf = sbuf("phk_bf", [128, D], BF16)
        v_bf = sbuf("v_bf", [128, NH * (HD + 1)], BF16)
        tstage = sbuf("tstage", [128, D], BF16)
        scst = sbuf("scst", [64, 2 * NH * 65], F32)
        ps_qkv = psum("ps_qkv", [128, D], F32)
        ps_t = psum("ps_t", [128, D], BF16)
        ps_scd0 = psum("ps_scd0", [64, 512], F32)
        ps_scd1 = psum("ps_scd1", [64, 512], F32)
        ps_scz = psum("ps_scz", [64, 16], F32)
        ps_mx = psum("ps_mx", [1, 128], F32)
        s_ld = sem("s_ld"); s_lncp = sem("s_lncp"); s_mm = sem("s_mm")
        s_ev = sem("s_ev"); s_st = sem("s_st"); s_aux = sem("s_aux")

        p = Prog(nc)
        p.sems = {"lncp": Sem(s_lncp)}
        ld = Sem(s_ld); mm = Sem(s_mm); ev = Sem(s_ev); st = Sem(s_st); aux = Sem(s_aux)
        stats = {"m": stat[:, 0:1], "var": stat[:, 1:2], "sd": stat[:, 2:3],
                 "rstd": stat[:, 3:4], "eps_col": stat[:, 4:5]}

        p.emit("vector", lambda e: e.memset(stat[:, 4:5], EPS), inc=p.sems["lncp"])
        for t in range(NT):
            p.dma("sync", xw[:, t * D:(t + 1) * D], xin[t * 128:(t + 1) * 128, :], ld)
        p.dma("sync", idb[:, :], identb, ld)
        p.dma("sync", idf[:, :], identf, ld)
        for c in range(KC):
            p.dma("gpsimd", w1b[:, c * 3 * D:(c + 1) * 3 * D], w1t[c * 128:(c + 1) * 128, :], ld)
        ld_all = ld.n

        p.wait("vector", ld, ld_all)
        p.wait("scalar", ld, ld_all)
        for t in range(NT):
            xt = xw[:, t * D:(t + 1) * D]
            _ln_tile(p, xt, sb, sa, stats, extra_absmax=pmx[:, t:t + 1])
            p.emit("vector", lambda e, o=xt, a=sb: e.tensor_copy(out=o, in_=a), inc=p.sems["lncp"])
            _ln_tile(p, xt, sb, sa, stats)
            p.emit("vector", lambda e, o=xnn_bf, a=sb: e.tensor_copy(out=o, in_=a), inc=p.sems["lncp"])
            cpv = p.sems["lncp"].n
            p.wait("sync", p.sems["lncp"], cpv)
            p.dma("sync", xn_o[t * 128:(t + 1) * 128, :], xt, st)
            p.wait("tensor", p.sems["lncp"], cpv)
            if t > 0:
                p.wait("tensor", ev)
            for c in range(KC):
                p.emit("tensor", lambda e, c=c: e.transpose(
                    ps_t[:, c * 128:(c + 1) * 128], xnn_bf[:, c * 128:(c + 1) * 128], idb[:, :]),
                    inc=(mm if c == KC - 1 else None))
            p.wait("scalar", mm)
            p.emit("scalar", lambda e, t=t: e.activation(
                out=xnn_fm.rearrange("p (c j) -> p c j", j=R)[:, :, t * 128:(t + 1) * 128],
                in_=ps_t.rearrange("p (c j) -> p c j", j=128), func=AF.Copy), inc=ev)

        for t in range(NT):
            b, ch = t // 2, t % 2
            for part in range(3):
                p.wait("tensor", ev)
                for c in range(KC):
                    lhs = xnn_fm.rearrange("p (c j) -> p c j", j=R)[:, c, t * 128:(t + 1) * 128]
                    for jn in range(2):
                        rhs = w1b[:, c * 3 * D + part * D + jn * 512: c * 3 * D + part * D + (jn + 1) * 512]
                        p.emit("tensor", lambda e, l=lhs, r=rhs, jn=jn, c=c: e.matmul(
                            ps_qkv[:, jn * 512:(jn + 1) * 512], l, r,
                            start=(c == 0), stop=(c == KC - 1)),
                            inc=(mm if c == KC - 1 and jn == 1 else None))
                p.wait("scalar", mm)
                p.wait("vector", mm)
                if part < 2:
                    p.emit("scalar", lambda e: e.activation(out=phr[:, :], in_=ps_qkv[:, :], func=AF.Relu), inc=ev)
                    p.wait("vector", ev)
                    p.emit("vector", lambda e: e.tensor_tensor(out=phe[:, :], in0=ps_qkv[:, :], in1=phr[:, :], op=ALU.subtract), inc=ev)
                    p.wait("scalar", ev)
                    p.emit("scalar", lambda e: e.activation(out=phe[:, :], in_=phe[:, :], func=AF.Exp), inc=ev)
                    p.wait("vector", ev)
                    dst = phq_bf if part == 0 else phk_bf
                    p.emit("vector", lambda e, d=dst: e.tensor_tensor(out=d[:, :], in0=phe[:, :], in1=phr[:, :], op=ALU.add), inc=ev)
                else:
                    p.emit("vector", lambda e: e.memset(
                        v_bf.rearrange("p (h e) -> p h e", e=HD + 1)[:, :, HD:HD + 1], 1.0))
                    p.emit("vector", lambda e: e.tensor_copy(
                        out=v_bf.rearrange("p (h e) -> p h e", e=HD + 1)[:, :, 0:HD],
                        in_=ps_qkv.rearrange("p (h d) -> p h d", d=HD)), inc=ev)
                p.wait("tensor", ev)

            for src, dst_fm in ((phq_bf, phiq_fm_o), (phk_bf, phik_fm_o)):
                p.wait("tensor", ev)
                for c in range(KC):
                    p.emit("tensor", lambda e, s=src, c=c: e.transpose(
                        ps_t[:, c * 128:(c + 1) * 128], s[:, c * 128:(c + 1) * 128], idb[:, :]),
                        inc=(mm if c == KC - 1 else None))
                p.wait("scalar", mm)
                p.wait("scalar", st)
                p.emit("scalar", lambda e: e.activation(out=tstage[:, :], in_=ps_t[:, :], func=AF.Copy), inc=ev)
                p.wait("sync", ev)
                p.dma("sync", dst_fm.rearrange("p (c j) -> p c j", j=R)[:, :, t * 128:(t + 1) * 128],
                      tstage.rearrange("p (c j) -> p c j", j=128), st)

            p.wait("tensor", ev)
            for hh in range(NH):
                bank = ps_scd0 if hh < 8 else ps_scd1
                col = (hh % 8) * 64
                lhs = phk_bf[:, hh * 64:(hh + 1) * 64]
                rhs = v_bf[:, hh * 65:hh * 65 + 64]
                p.emit("tensor", lambda e, bk=bank, co=col, l=lhs, r=rhs: e.matmul(
                    bk[:, co:co + 64], l, r, start=True, stop=True))
                rhs1 = v_bf[:, hh * 65 + 64:hh * 65 + 65]
                p.emit("tensor", lambda e, hh=hh, l=lhs, r=rhs1: e.matmul(
                    ps_scz[:, hh:hh + 1], l, r, start=True, stop=True),
                    inc=(mm if hh == NH - 1 else None))
            p.wait("vector", mm)
            scv = scst.rearrange("p (c x e) -> p c x e", c=2, e=65)
            p.emit("vector", lambda e, ch=ch, scv=scv: e.tensor_copy(
                out=scv[0:64, ch, 0:8, 0:64],
                in_=ps_scd0.rearrange("p (x d) -> p x d", d=64)))
            p.emit("vector", lambda e, ch=ch, scv=scv: e.tensor_copy(
                out=scv[0:64, ch, 8:16, 0:64],
                in_=ps_scd1.rearrange("p (x d) -> p x d", d=64)))
            p.emit("vector", lambda e, ch=ch, scv=scv: e.tensor_copy(
                out=scv[0:64, ch, 0:16, 64:65],
                in_=ps_scz.rearrange("p (x o) -> p x o", o=1)), inc=ev)
            p.wait("tensor", ev)
            if ch == 1:
                blk = NH * 65
                p.wait("sync", ev)
                p.dma("sync", sc0_o[0:64, b * blk:(b + 1) * blk], scst[0:64, 0:blk], st)
                p.wait("vector", st)
                p.emit("vector", lambda e, blk=blk: e.tensor_tensor(
                    out=scst[0:64, 0:blk], in0=scst[0:64, 0:blk],
                    in1=scst[0:64, blk:2 * blk], op=ALU.add), inc=ev)
                p.wait("sync", ev)
                p.dma("sync", ssum_o[0:64, b * blk:(b + 1) * blk], scst[0:64, 0:blk], st)

            p.wait("sync", ev)
            p.dma("sync", phik_tm_o[t * 128:(t + 1) * 128, :], phk_bf[:, :], st)
            p.dma("sync", v_tm_o[t * 128:(t + 1) * 128, :], v_bf[:, :], st)
            p.wait("scalar", st)
            p.wait("vector", st)

        p.emit("vector", lambda e: e.reduce_max(out=mxc[:, :], in_=pmx[:, :], axis=AX.X), inc=ev)
        p.wait("tensor", ev)
        p.emit("tensor", lambda e: e.transpose(ps_mx[0:1, 0:128], mxc[:, 0:1], idf[:, :]), inc=mm)
        p.wait("vector", mm)
        p.emit("vector", lambda e: e.tensor_copy(out=mx1[:, :], in_=ps_mx[0:1, :]))
        p.drain("vector")
        p.emit("vector", lambda e: e.reduce_max(out=stat[0:1, 5:6], in_=mx1[0:1, :], axis=AX.X), inc=ev)
        p.wait("sync", ev)
        p.dma("sync", maxp1_o, stat[0:1, 5:6], st)

        p.wait("gpsimd", st)
        p.build()
    return nc


def _prep_host(inputs):
    h = {}
    x = np.ascontiguousarray(inputs["L0_attn_x"], dtype=np.float32)
    h["x_slices"] = [np.ascontiguousarray(x[:, c * 256:(c + 1) * 256, :].reshape(R, D))
                     for c in range(NCORES)]
    h["w1t"] = np.ascontiguousarray(inputs["L0_attn_W1"].T)
    h["identb"] = np.eye(128).astype(ml_dtypes.bfloat16)
    h["identf"] = np.eye(128, dtype=np.float32)
    h["triu"] = np.triu(np.ones((128, 128), np.float32))
    h["w2t"] = np.ascontiguousarray(inputs["L0_attn_W2"].T)
    h["wup"] = np.ascontiguousarray(inputs["L0_ffn_w_up"])
    h["wgate"] = np.ascontiguousarray(inputs["L0_ffn_w_gate"])
    h["wupt"] = np.ascontiguousarray(inputs["L0_ffn_w_up"].T)
    h["wgatet"] = np.ascontiguousarray(inputs["L0_ffn_w_gate"].T)
    h["wdown"] = np.ascontiguousarray(inputs["L0_ffn_w_down"])
    perm = np.zeros(3072, np.int64)
    for j in range(3):
        for hh in range(16):
            for d in range(64):
                perm[j * 1024 + hh * 64 + d] = hh * 192 + j * 64 + d
    h["ipw_t"] = np.ascontiguousarray(inputs["L1_attn_in_proj_weight"][perm, :].T)
    h["ipb_row"] = np.ascontiguousarray(inputs["L1_attn_in_proj_bias"][perm][None, :])
    h["opw_t"] = np.ascontiguousarray(inputs["L1_attn_out_proj_weight"].T)
    h["opb_row"] = np.ascontiguousarray(inputs["L1_attn_out_proj_bias"][None, :])
    h["rt_t"] = np.ascontiguousarray(inputs["L1_ffn_router_w"].T)
    h["rb_row"] = np.ascontiguousarray(inputs["L1_ffn_router_bias"][None, :])
    h["gat"] = np.ascontiguousarray(inputs["L1_ffn_gate_w"].reshape(4096, 1024).T)
    h["upt"] = np.ascontiguousarray(inputs["L1_ffn_up_w"].reshape(4096, 1024).T)
    h["dat"] = np.ascontiguousarray(inputs["L1_ffn_down_w"].transpose(0, 2, 1).reshape(4096, 1024))
    h["sgt"] = np.ascontiguousarray(inputs["L1_ffn_sgate_w"].T)
    h["sut"] = np.ascontiguousarray(inputs["L1_ffn_sup_w"].T)
    h["sdt"] = np.ascontiguousarray(inputs["L1_ffn_sdown_w"].T)
    return h


def kernel(**inputs):
    h = _prep_host(inputs)
    cores = list(range(NCORES))

    nc1 = build_launch1()
    im1 = [{"xin": h["x_slices"][c], "w1t": h["w1t"], "identb": h["identb"],
            "identf": h["identf"]} for c in cores]
    r1 = run_bass_kernel_spmd(nc1, im1, core_ids=cores).results

    maxp1_all = np.stack([r1[c]["maxp1_o"][0] for c in cores])
    nc2 = build_launch2()
    im2 = []
    for c in cores:
        prev = np.zeros((7, 64, 128 * 65), np.float32)
        for j in range(c):
            prev[j] = r1[j]["ssum_o"]
        im2.append({
            "xin": h["x_slices"][c], "xn_i": r1[c]["xn_o"],
            "phiq_fm": r1[c]["phiq_fm_o"], "phik_fm": r1[c]["phik_fm_o"],
            "phik_tm": r1[c]["phik_tm_o"], "v_tm": r1[c]["v_tm_o"],
            "sc0_i": r1[c]["sc0_o"], "ssum_prev": prev, "maxp_all": maxp1_all,
            "w2t": h["w2t"], "wupt": h["wupt"], "wgatet": h["wgatet"],
            "wup": h["wup"], "wgate": h["wgate"],
            "identb": h["identb"], "identf": h["identf"], "triu": h["triu"],
        })
    r2 = run_bass_kernel_spmd(nc2, im2, core_ids=cores).results

    maxp2_all = np.stack([r2[c]["maxp2_o"][0] for c in cores])
    maxp3_all = np.stack([r2[c]["maxp3_o"][0] for c in cores])
    nc3 = build_launch3()
    im3 = [{"up_raw_i": r2[c]["up_raw_o"], "gate_raw_i": r2[c]["gate_raw_o"],
            "maxp2_all": maxp2_all, "maxp3_all": maxp3_all,
            "identf": h["identf"]} for c in cores]
    r3 = run_bass_kernel_spmd(nc3, im3, core_ids=cores).results

    maxp4_all = np.stack([r3[c]["maxp4_o"][0] for c in cores])
    nc4 = build_launch4()
    im4 = [{"t_i": r3[c]["t_o"], "upq_i": r3[c]["upq_o"],
            "maxp4_all": maxp4_all, "identf": h["identf"]} for c in cores]
    r4 = run_bass_kernel_spmd(nc4, im4, core_ids=cores).results

    maxp5_all = np.stack([r4[c]["maxp5_o"][0] for c in cores])
    nc5 = build_launch5()
    im5 = [{"oraw_i": r4[c]["oraw_o"], "maxp5_all": maxp5_all, "xa_i": r2[c]["xa_o"],
            "wdown": h["wdown"], "ipw_t": h["ipw_t"], "ipb_row": h["ipb_row"],
            "opw_t": h["opw_t"], "opb_row": h["opb_row"], "rt_t": h["rt_t"],
            "rb_row": h["rb_row"], "gat": h["gat"], "upt": h["upt"], "dat": h["dat"],
            "sgt": h["sgt"], "sut": h["sut"], "sdt": h["sdt"],
            "identb": h["identb"], "identf": h["identf"]} for c in cores]
    r5 = run_bass_kernel_spmd(nc5, im5, core_ids=cores).results

    out = np.zeros((8, 2048, 1024), np.float32)
    for c in cores:
        out[:, c * 256:(c + 1) * 256, :] = r5[c]["out_o"].reshape(8, 256, 1024)
    return out


def build_launch2():
    nc = bass.Bass("TRN2", target_bir_lowering=False)
    ap = lambda n, sh, dt, kind: nc.dram_tensor(n, sh, dt, kind=kind).ap()
    xin = ap("xin", [R, D], F32, "ExternalInput")
    xn_i = ap("xn_i", [R, D], F32, "ExternalInput")
    phiq_fm = ap("phiq_fm", [128, KC * R], BF16, "ExternalInput")
    phik_fm = ap("phik_fm", [128, KC * R], BF16, "ExternalInput")
    phik_tm = ap("phik_tm", [R, D], BF16, "ExternalInput")
    v_tm = ap("v_tm", [R, NH * 65], BF16, "ExternalInput")
    sc0_i = ap("sc0_i", [64, 128 * 65], F32, "ExternalInput")
    ssum_prev = ap("ssum_prev", [7, 64, 128 * 65], F32, "ExternalInput")
    maxp_all = ap("maxp_all", [8, 1], F32, "ExternalInput")
    w2t = ap("w2t", [D, D], F32, "ExternalInput")
    wупs = None
    wupt = ap("wupt", [D, 3 * D], F32, "ExternalInput")
    wgatet = ap("wgatet", [D, 3 * D], F32, "ExternalInput")
    wup = ap("wup", [3 * D, D], F32, "ExternalInput")
    wgate = ap("wgate", [3 * D, D], F32, "ExternalInput")
    identb = ap("identb", [128, 128], BF16, "ExternalInput")
    identf = ap("identf", [128, 128], F32, "ExternalInput")
    triu = ap("triu", [128, 128], F32, "ExternalInput")

    xa_o = ap("xa_o", [R, D], F32, "ExternalOutput")
    up_raw_o = ap("up_raw_o", [128, 24 * R], F32, "ExternalOutput")
    gate_raw_o = ap("gate_raw_o", [128, 24 * R], F32, "ExternalOutput")
    maxp2_o = ap("maxp2_o", [1, 1], F32, "ExternalOutput")
    maxp3_o = ap("maxp3_o", [1, 1], F32, "ExternalOutput")
    scr = nc.dram_tensor("scr", [1, 4], F32)

    BLK = NH * 65

    with ExitStack() as ctx:
        sbuf = lambda n, sh, dt: ctx.enter_context(nc.sbuf_tensor(n, sh, dt))[:, :]
        psum = lambda n, sh, dt: ctx.enter_context(nc.psum_tensor(n, sh, dt))[:, :]
        sem = lambda n: ctx.enter_context(nc.semaphore(n))
        sinf = sbuf("sinf", [64, BLK], F32)
        smidf = sbuf("smidf", [64, BLK], F32)
        stg = sbuf("stg", [64, BLK], F32)
        sin_bf = sbuf("sin_bf", [128, 8 * 65], BF16)
        smid_bf = sbuf("smid_bf", [128, 8 * 65], BF16)
        pq_b = sbuf("pq_b", [128, KC * 256], BF16)
        pk_b = sbuf("pk_b", [128, KC * 256], BF16)
        v_b = sbuf("v_b", [128, 2 * BLK], BF16)
        att_bf = sbuf("att_bf", [128, 128], BF16)
        rec1 = sbuf("rec1", [128, 1], F32)
        ctxt = sbuf("ctxt", [128, D], F32)
        ctx_bf = sbuf("ctx_bf", [128, D], BF16)
        ctx_fm = sbuf("ctx_fm", [128, NT * D // 8 * 8], BF16)
        w2b = sbuf("w2b", [128, KC * D], BF16)
        w2o = sbuf("w2o", [128, 512], F32)
        xa_st = sbuf("xa_st", [128, 4 * D], F32)
        xnt = sbuf("xnt", [128, D], F32)
        xq_i32 = sbuf("xq_i32", [128, D], I32)
        dm_s = sbuf("dm_s", [128, D], F32)
        rr_s = sbuf("rr_s", [128, D], F32)
        xq_bf = sbuf("xq_bf", [128, D], BF16)
        xq_fm = sbuf("xq_fm", [128, KC * R], BF16)
        wtile = sbuf("wtile", [128, D], F32)
        bmx = sbuf("bmx", [128, 32], F32)
        sblk = sbuf("sblk", [128, 32], F32)
        rsblk = sbuf("rsblk", [128, 32], F32)
        yw = sbuf("yw", [128, D], F32)
        iw = sbuf("iw", [128, D], I32)
        dw = sbuf("dw", [128, D], F32)
        wq_bf = sbuf("wq_bf", [128, D], BF16)
        wlhs = sbuf("wlhs", [128, D], BF16)
        upst = sbuf("upst", [128, 2 * 512], F32)
        runm = sbuf("runm", [128, 2], F32)
        cols = sbuf("cols", [128, 4], F32)
        m1row = sbuf("m1row", [1, 8], F32)
        m1sc = sbuf("m1sc", [1, 4], F32)
        mx1 = sbuf("mx1", [1, 128], F32)
        idb = sbuf("idb", [128, 128], BF16)
        idf = sbuf("idf", [128, 128], F32)
        trm = sbuf("trm", [128, 128], F32)
        ps_att = psum("ps_att", [128, 128], F32)
        ps_y = psum("ps_y", [128, 65], F32)
        ps_t = psum("ps_t", [128, D], BF16)
        ps_tf = psum("ps_tf", [128, 512], F32)
        ps_w2 = psum("ps_w2", [128, 512], F32)
        ps_m0 = psum("ps_m0", [128, 512], F32)
        ps_m1 = psum("ps_m1", [128, 512], F32)
        ps_mx = psum("ps_mx", [1, 128], F32)
        s_ld = sem("s_ld"); s_mm = sem("s_mm"); s_ev = sem("s_ev")
        s_st = sem("s_st"); s_aux = sem("s_aux"); s_cp = sem("s_cp")

        p = Prog(nc)
        ld = Sem(s_ld); mm = Sem(s_mm); ev = Sem(s_ev); st = Sem(s_st)
        aux = Sem(s_aux); cp = Sem(s_cp)

        p.dma("sync", idb, identb, ld)
        p.dma("sync", idf, identf, ld)
        p.dma("sync", trm, triu, ld)
        p.dma("sync", m1row, maxp_all.rearrange("a b -> b a"), ld)
        ld0 = ld.n

        p.wait("vector", ld, ld0)
        p.emit("vector", lambda e: e.reduce_max(out=m1sc[0:1, 0:1], in_=m1row[0:1, :], axis=AX.X))
        p.drain("vector")
        p.emit("vector", lambda e: e.tensor_scalar(out=m1sc[0:1, 1:2], in0=m1sc[0:1, 0:1],
                                                   scalar1=1.0 / 32767.0, scalar2=1e-12, op0=ALU.mult, op1=ALU.max))
        p.drain("vector")
        p.emit("vector", lambda e: e.reciprocal(out=m1sc[0:1, 2:3], in_=m1sc[0:1, 1:2]), inc=cp)
        p.wait("sync", cp)
        p.dma("sync", scr.ap()[0:1, 0:2], m1sc[0:1, 1:3], aux)
        p.wait("gpsimd", aux)
        p.dma("gpsimd", cols[:, 0:1], scr.ap()[0:1, 0:1].broadcast_to([128, 1]), ld)
        p.dma("gpsimd", cols[:, 1:2], scr.ap()[0:1, 1:2].broadcast_to([128, 1]), ld)
        p.emit("vector", lambda e: e.memset(cols[:, 2:3], 32768.0))
        p.emit("vector", lambda e: e.memset(cols[:, 3:4], 16.0), inc=cp)
        cols_rdy = (ld.n, cp.n)

        p.wait("scalar", ld, ld.n)
        p.wait("scalar", cp)
        for t in range(NT):
            p.wait("sync", ev)
            p.dma("sync", xnt, xn_i[t * 128:(t + 1) * 128, :], ld)
            p.wait("scalar", ld)
            p.emit("scalar", lambda e: e.activation(out=xq_i32, in_=xnt, func=AF.Identity,
                                                    bias=cols[:, 2:3], scale=cols[:, 1:2]), inc=cp)
            p.wait("vector", cp)
            p.emit("vector", lambda e: e.scalar_tensor_tensor(out=dm_s, in0=xnt, scalar=cols[:, 1:2],
                                                              in1=xq_i32, op0=ALU.mult, op1=ALU.subtract))
            p.drain("vector")
            p.emit("vector", lambda e: e.scalar_tensor_tensor(out=rr_s, in0=dm_s, scalar=-32767.5,
                                                              in1=xq_i32, op0=ALU.is_ge, op1=ALU.add))
            p.drain("vector")
            p.emit("vector", lambda e: e.tensor_scalar(out=xq_bf, in0=rr_s, scalar1=32768.0,
                                                       scalar2=cols[:, 0:1], op0=ALU.subtract, op1=ALU.mult), inc=cp)
            p.wait("tensor", cp)
            p.wait("tensor", ev)
            for c in range(KC):
                p.emit("tensor", lambda e, c=c: e.transpose(
                    ps_t[:, c * 128:(c + 1) * 128], xq_bf[:, c * 128:(c + 1) * 128], idb),
                    inc=(mm if c == KC - 1 else None))
            p.wait("scalar", mm)
            p.emit("scalar", lambda e, t=t: e.activation(
                out=xq_fm.rearrange("p (c j) -> p c j", j=R)[:, :, t * 128:(t + 1) * 128],
                in_=ps_t.rearrange("p (c j) -> p c j", j=128), func=AF.Copy), inc=ev)

        for b in range(8):
            p.wait("sync", ev)
            p.wait("sync", mm)
            p.dma("sync", pq_b.rearrange("p (c j) -> p c j", j=256),
                  phiq_fm.rearrange("p (c j) -> p c j", j=R)[:, :, b * 256:(b + 1) * 256], ld)
            p.dma("sync", pk_b.rearrange("p (c j) -> p c j", j=256),
                  phik_fm.rearrange("p (c j) -> p c j", j=R)[:, :, b * 256:(b + 1) * 256], ld)
            p.dma("sync", v_b.rearrange("p (c j) -> p c j", j=BLK),
                  v_tm[b * 256:(b + 1) * 256, :].rearrange("(c p) j -> p c j", p=128), ld)
            p.dma("sync", stg, sc0_i[0:64, b * BLK:(b + 1) * BLK], ld)
            p.wait("vector", ld)
            p.emit("vector", lambda e: e.memset(sinf, 0.0), inc=cp)
            for j in range(7):
                p.wait("sync", cp)
                p.dma("sync", smidf, ssum_prev[j, 0:64, b * BLK:(b + 1) * BLK], ld)
                p.wait("vector", ld)
                p.emit("vector", lambda e: e.tensor_tensor(out=sinf, in0=sinf, in1=smidf, op=ALU.add), inc=cp)
            p.wait("vector", cp)
            p.emit("vector", lambda e: e.tensor_tensor(out=smidf, in0=sinf, in1=stg, op=ALU.add), inc=cp)
            p.wait("gpsimd", cp)
            p.wait("gpsimd", mm)
            for src, dst in ((sinf, sin_bf), (smidf, smid_bf)):
                sv = src.rearrange("p (h2 two e) -> p h2 two e", two=2, e=65)
                dv = dst.rearrange("p (h2 e) -> p h2 e", e=65)
                p.dma("gpsimd", dv[0:64, :, :], sv[0:64, :, 0, :], ld)
                p.dma("gpsimd", dv[64:128, :, :], sv[0:64, :, 1, :], ld)
            p.wait("tensor", ld)
            p.wait("tensor", cp)
            for chn in range(2):
                for hh in range(NH):
                    cc, po = hh // 2, (hh % 2) * 64
                    lq = pq_b[po:po + 64, cc * 256 + chn * 128: cc * 256 + (chn + 1) * 128]
                    lk = pk_b[po:po + 64, cc * 256 + chn * 128: cc * 256 + (chn + 1) * 128]
                    p.emit("tensor", lambda e, lk=lk, lq=lq: e.matmul(ps_att, lk, lq, start=True, stop=True), inc=mm)
                    p.wait("vector", mm)
                    p.emit("vector", lambda e: e.tensor_tensor(out=att_bf, in0=ps_att, in1=trm, op=ALU.mult), inc=ev)
                    p.wait("tensor", ev)
                    sbf = sin_bf if chn == 0 else smid_bf
                    p.emit("tensor", lambda e, chn=chn, hh=hh: e.matmul(
                        ps_y, att_bf, v_b[:, chn * BLK + hh * 65: chn * BLK + (hh + 1) * 65],
                        start=True, stop=False))
                    p.emit("tensor", lambda e, lq=lq, sbf=sbf, hh=hh, po=po: e.matmul(
                        ps_y, lq, sbf[po:po + 64, (hh // 2) * 65:(hh // 2 + 1) * 65],
                        start=False, stop=True), inc=mm)
                    p.wait("vector", mm)
                    p.emit("vector", lambda e: e.reciprocal(out=rec1, in_=ps_y[:, 64:65]))
                    p.drain("vector")
                    p.emit("vector", lambda e, hh=hh: e.tensor_scalar(
                        out=ctxt[:, hh * 64:(hh + 1) * 64], in0=ps_y[:, 0:64],
                        scalar1=rec1[:, 0:1], scalar2=None, op0=ALU.mult), inc=ev)
                p.wait("scalar", ev)
                p.emit("scalar", lambda e: e.activation(out=ctx_bf, in_=ctxt, func=AF.Copy), inc=cp)
                p.wait("tensor", cp)
                p.wait("tensor", ev)
                for c in range(KC):
                    p.emit("tensor", lambda e, c=c: e.transpose(
                        ps_t[:, c * 128:(c + 1) * 128], ctx_bf[:, c * 128:(c + 1) * 128], idb),
                        inc=(mm if c == KC - 1 else None))
                p.wait("scalar", mm)
                t_idx = b * 2 + chn
                p.emit("scalar", lambda e, t_idx=t_idx: e.activation(
                    out=ctx_fm.rearrange("p (c j) -> p c j", j=R)[:, :, t_idx * 128:(t_idx + 1) * 128],
                    in_=ps_t.rearrange("p (c j) -> p c j", j=128), func=AF.Copy), inc=ev)

        for c in range(KC):
            p.dma("gpsimd", w2b[:, c * D:(c + 1) * D], w2t[c * 128:(c + 1) * 128, :], ld)
        w2_ld = ld.n
        p.wait("tensor", ld, w2_ld)
        p.wait("tensor", ev)
        for tokc in range(4):
            p.wait("sync", ev if tokc == 0 else st)
            for tb in range(4):
                p.dma("sync", xa_st[:, tb * D:(tb + 1) * D],
                      xin[(tokc * 4 + tb) * 128:(tokc * 4 + tb + 1) * 128, :], ld)
            xa_ld = ld.n
            p.wait("vector", ld, xa_ld)
            for oc in range(KC):
                for c in range(KC):
                    p.emit("tensor", lambda e, oc=oc, c=c, tokc=tokc: e.matmul(
                        ps_w2, w2b[:, c * D + oc * 128: c * D + (oc + 1) * 128],
                        ctx_fm[:, c * R + tokc * 512: c * R + (tokc + 1) * 512],
                        start=(c == 0), stop=(c == KC - 1)), inc=(mm if c == KC - 1 else None))
                p.wait("scalar", mm)
                p.emit("scalar", lambda e: e.activation(out=w2o, in_=ps_w2, func=AF.Copy), inc=ev)
                p.wait("tensor", ev)
                for tb in range(4):
                    p.emit("tensor", lambda e, tb=tb: e.transpose(
                        ps_tf[:, tb * 128:(tb + 1) * 128], w2o[:, tb * 128:(tb + 1) * 128], idf),
                        inc=(mm if tb == 3 else None))
                p.wait("vector", mm)
                p.emit("vector", lambda e, oc=oc: e.tensor_tensor(
                    out=xa_st.rearrange("p (t j) -> p t j", j=D)[:, :, oc * 128:(oc + 1) * 128],
                    in0=xa_st.rearrange("p (t j) -> p t j", j=D)[:, :, oc * 128:(oc + 1) * 128],
                    in1=ps_tf.rearrange("p (t j) -> p t j", j=128), op=ALU.add), inc=ev)
                p.wait("tensor", ev)
            p.wait("sync", ev)
            for tb in range(4):
                p.dma("sync", xa_o[(tokc * 4 + tb) * 128:(tokc * 4 + tb + 1) * 128, :],
                      xa_st[:, tb * D:(tb + 1) * D], st)

        p.emit("vector", lambda e: e.memset(runm, 0.0), inc=cp)
        for wi, (wsrc, dst, psm) in enumerate(((wup, up_raw_o, ps_m0), (wgate, gate_raw_o, ps_m1))):
            for oc in range(24):
                p.wait("sync", cp)
                p.dma("sync", wtile, wsrc[oc * 128:(oc + 1) * 128, :], ld)
                p.wait("vector", ld)
                p.emit("vector", lambda e: e.tensor_reduce(
                    out=bmx, in_=wtile.rearrange("p (g x) -> p g x", x=32), axis=AX.X,
                    op=ALU.max, apply_absolute_value=True))
                p.drain("vector")
                p.emit("vector", lambda e: e.tensor_scalar(out=sblk, in0=bmx, scalar1=1.0 / 7.0,
                                                           scalar2=1e-12, op0=ALU.mult, op1=ALU.max))
                p.drain("vector")
                p.emit("vector", lambda e: e.reciprocal(out=rsblk, in_=sblk))
                p.drain("vector")
                p.emit("vector", lambda e: e.tensor_tensor(
                    out=yw.rearrange("p (g x) -> p g x", x=32), in0=wtile.rearrange("p (g x) -> p g x", x=32),
                    in1=rsblk.rearrange("p (g o) -> p g o", o=1).broadcast_to([128, 32, 32]), op=ALU.mult), inc=cp)
                p.wait("scalar", cp)
                p.emit("scalar", lambda e: e.activation(out=iw, in_=yw, func=AF.Identity,
                                                        bias=cols[:, 3:4], scale=1.0), inc=cp)
                p.wait("vector", cp)
                p.emit("vector", lambda e: e.scalar_tensor_tensor(out=dw, in0=yw, scalar=16.0,
                                                                  in1=iw, op0=ALU.add, op1=ALU.subtract))
                p.drain("vector")
                p.emit("vector", lambda e: e.scalar_tensor_tensor(out=dw, in0=dw, scalar=0.5,
                                                                  in1=iw, op0=ALU.is_ge, op1=ALU.add))
                p.drain("vector")
                p.emit("vector", lambda e: e.tensor_scalar(out=dw, in0=dw, scalar1=16.0, scalar2=None,
                                                           op0=ALU.subtract))
                p.drain("vector")
                p.emit("vector", lambda e: e.tensor_tensor(
                    out=wq_bf.rearrange("p (g x) -> p g x", x=32), in0=dw.rearrange("p (g x) -> p g x", x=32),
                    in1=sblk.rearrange("p (g o) -> p g o", o=1).broadcast_to([128, 32, 32]), op=ALU.mult), inc=cp)
                p.wait("tensor", cp)
                p.wait("tensor", ev)
                for c in range(KC):
                    p.emit("tensor", lambda e, c=c: e.transpose(
                        ps_t[:, c * 128:(c + 1) * 128], wq_bf[:, c * 128:(c + 1) * 128], idb),
                        inc=(mm if c == KC - 1 else None))
                p.wait("scalar", mm)
                p.emit("scalar", lambda e: e.activation(out=wlhs, in_=ps_t, func=AF.Copy), inc=ev)
                p.wait("tensor", ev)
                for tokc in range(4):
                    pp = psm if tokc % 2 == 0 else (ps_m1 if psm is ps_m0 else ps_m0)
                    for c in range(KC):
                        p.emit("tensor", lambda e, pp=pp, c=c, tokc=tokc: e.matmul(
                            pp, wlhs[:, c * 128:(c + 1) * 128],
                            xq_fm[:, c * R + tokc * 512: c * R + (tokc + 1) * 512],
                            start=(c == 0), stop=(c == KC - 1)), inc=(mm if c == KC - 1 else None))
                    p.wait("scalar", mm)
                    sl = upst[:, (tokc % 2) * 512:(tokc % 2 + 1) * 512]
                    p.emit("scalar", lambda e, sl=sl, pp=pp: e.activation(out=sl, in_=pp, func=AF.Copy), inc=ev)
                    p.wait("sync", ev)
                    p.dma("sync", dst.rearrange("p (o j) -> p o j", j=R)[:, oc, tokc * 512:(tokc + 1) * 512], sl, st)
                    p.wait("vector", ev)
                    p.emit("vector", lambda e, sl=sl: e.tensor_reduce(
                        out=bmx[:, 0:1], in_=sl, axis=AX.X, op=ALU.max, apply_absolute_value=True))
                    p.drain("vector")
                    p.emit("vector", lambda e, wi=wi: e.tensor_tensor(
                        out=runm[:, wi:wi + 1], in0=runm[:, wi:wi + 1], in1=bmx[:, 0:1], op=ALU.max), inc=cp)
                    p.wait("tensor", cp)
                p.wait("scalar", st)
        for wi, dst in ((0, maxp2_o), (1, maxp3_o)):
            p.wait("tensor", cp)
            if wi == 1:
                p.wait("tensor", ev)
            p.emit("tensor", lambda e, wi=wi: e.transpose(ps_mx[0:1, 0:128], runm[:, wi:wi + 1], idf), inc=mm)
            p.wait("vector", mm)
            p.emit("vector", lambda e: e.tensor_copy(out=mx1, in_=ps_mx[0:1, :]), inc=ev)
            p.drain("vector")
            p.emit("vector", lambda e, wi=wi: e.reduce_max(out=m1sc[0:1, 3:4], in_=mx1[0:1, :], axis=AX.X), inc=ev)
            p.wait("sync", ev)
            p.dma("sync", dst, m1sc[0:1, 3:4], st)
            p.wait("vector", st)

        p.wait("gpsimd", st)
        p.build()
    return nc


def _phase_barrier(p, sems, dum_v, dum_s, dum_g, bar):
    p.emit("vector", lambda e, d=dum_v: e.memset(d, 0.0), inc=bar)
    p.emit("scalar", lambda e, d=dum_s: e.activation(out=d, in_=d, func=AF.Copy), inc=bar)
    p.emit("gpsimd", lambda e, d=dum_g: e.memset(d, 0.0), inc=bar)
    tgt = bar.n
    for ek in ("sync", "scalar", "vector", "tensor", "gpsimd"):
        p.wait(ek, bar, at=tgt)
        for sm in sems:
            p.wait(ek, sm)


def _emit_scales(p, nc_scr, mrow, msc, cols_s, cols_rs, ld, cp, aux, idx):
    p.emit("vector", lambda e: e.reduce_max(out=msc[0:1, 0:1], in_=mrow[0:1, :], axis=AX.X))
    p.drain("vector")
    p.emit("vector", lambda e: e.tensor_scalar(out=msc[0:1, 1:2], in0=msc[0:1, 0:1],
                                               scalar1=1.0 / 32767.0, scalar2=1e-12, op0=ALU.mult, op1=ALU.max))
    p.drain("vector")
    p.emit("vector", lambda e: e.reciprocal(out=msc[0:1, 2:3], in_=msc[0:1, 1:2]), inc=cp)
    p.wait("sync", cp)
    p.dma("sync", nc_scr.ap()[0:1, idx:idx + 2], msc[0:1, 1:3], aux)
    p.wait("gpsimd", aux)
    p.dma("gpsimd", cols_s, nc_scr.ap()[0:1, idx:idx + 1].broadcast_to([128, 1]), ld)
    p.dma("gpsimd", cols_rs, nc_scr.ap()[0:1, idx + 1:idx + 2].broadcast_to([128, 1]), ld)


def _emit_qdq(p, src, dst, i32b, dmb, s_col, rs_col, b_col, cp):
    p.wait("scalar", cp)
    p.emit("scalar", lambda e: e.activation(out=i32b, in_=src, func=AF.Identity,
                                            bias=b_col, scale=rs_col), inc=cp)
    p.wait("vector", cp)
    p.emit("vector", lambda e: e.scalar_tensor_tensor(out=dmb, in0=src, scalar=rs_col,
                                                      in1=i32b, op0=ALU.mult, op1=ALU.subtract))
    p.drain("vector")
    p.emit("vector", lambda e: e.scalar_tensor_tensor(out=dmb, in0=dmb, scalar=-32767.5,
                                                      in1=i32b, op0=ALU.is_ge, op1=ALU.add))
    p.drain("vector")
    p.emit("vector", lambda e: e.tensor_scalar(out=dst, in0=dmb, scalar1=32768.0,
                                               scalar2=s_col, op0=ALU.subtract, op1=ALU.mult), inc=cp)


def _emit_colmax_out(p, runcol, mx1, msc_sl, ps_mx, idf, dst, mm, ev, st):
    p.emit("tensor", lambda e: e.transpose(ps_mx[0:1, 0:128], runcol, idf), inc=mm)
    p.wait("vector", mm)
    p.emit("vector", lambda e: e.tensor_copy(out=mx1, in_=ps_mx[0:1, :]), inc=ev)
    p.drain("vector")
    p.emit("vector", lambda e: e.reduce_max(out=msc_sl, in_=mx1[0:1, :], axis=AX.X), inc=ev)
    p.wait("sync", ev)
    p.dma("sync", dst, msc_sl, st)
    p.wait("vector", st)


def build_launch3():
    nc = bass.Bass("TRN2", target_bir_lowering=False)
    ap = lambda n, sh, dt, kind: nc.dram_tensor(n, sh, dt, kind=kind).ap()
    up_raw_i = ap("up_raw_i", [128, 24 * R], F32, "ExternalInput")
    gate_raw_i = ap("gate_raw_i", [128, 24 * R], F32, "ExternalInput")
    maxp2_all = ap("maxp2_all", [8, 1], F32, "ExternalInput")
    maxp3_all = ap("maxp3_all", [8, 1], F32, "ExternalInput")
    identf = ap("identf", [128, 128], F32, "ExternalInput")
    upq_o = ap("upq_o", [128, 24 * R], F32, "ExternalOutput")
    t_o = ap("t_o", [128, 24 * R], F32, "ExternalOutput")
    maxp4_o = ap("maxp4_o", [1, 1], F32, "ExternalOutput")
    scr = nc.dram_tensor("scr", [1, 8], F32)

    with ExitStack() as ctx:
        sbuf = lambda n, sh, dt: ctx.enter_context(nc.sbuf_tensor(n, sh, dt))[:, :]
        psum = lambda n, sh, dt: ctx.enter_context(nc.psum_tensor(n, sh, dt))[:, :]
        sem = lambda n: ctx.enter_context(nc.semaphore(n))
        upc = sbuf("upc", [128, R], F32)
        gac = sbuf("gac", [128, R], F32)
        i32b = sbuf("i32b", [128, R], I32)
        dmb = sbuf("dmb", [128, R], F32)
        gq = sbuf("gq", [128, R], F32)
        sg = sbuf("sg", [128, R], F32)
        outb = sbuf("outb", [128, R], F32)
        m2row = sbuf("m2row", [1, 8], F32)
        m3row = sbuf("m3row", [1, 8], F32)
        msc = sbuf("msc", [1, 8], F32)
        cols = sbuf("cols", [128, 8], F32)
        runm = sbuf("runm", [128, 1], F32)
        mx1 = sbuf("mx1", [1, 128], F32)
        idf = sbuf("idf", [128, 128], F32)
        ps_mx = psum("ps_mx", [1, 128], F32)
        s_ld = sem("s_ld"); s_cp = sem("s_cp"); s_ev = sem("s_ev")
        s_st = sem("s_st"); s_aux = sem("s_aux"); s_mm = sem("s_mm")
        p = Prog(nc)
        ld = Sem(s_ld); cp = Sem(s_cp); ev = Sem(s_ev); st = Sem(s_st)
        aux = Sem(s_aux); mm = Sem(s_mm)

        p.dma("sync", m2row, maxp2_all.rearrange("a b -> b a"), ld)
        p.dma("sync", m3row, maxp3_all.rearrange("a b -> b a"), ld)
        p.dma("sync", idf, identf, ld)
        p.emit("vector", lambda e: e.memset(cols[:, 4:5], 32768.0))
        p.emit("vector", lambda e: e.memset(runm, 0.0), inc=cp)
        p.wait("vector", ld)
        _emit_scales(p, scr, m2row, msc, cols[:, 0:1], cols[:, 1:2], ld, cp, aux, 0)
        _emit_scales(p, scr, m3row, msc, cols[:, 2:3], cols[:, 3:4], ld, cp, aux, 2)
        p.wait("scalar", ld)
        p.wait("scalar", cp)
        p.wait("vector", ld)
        for oc in range(24):
            sl = lambda tsr: tsr
            p.wait("sync", cp)
            p.wait("sync", st)
            p.dma("sync", upc, up_raw_i[:, oc * R:(oc + 1) * R], ld)
            p.dma("sync", gac, gate_raw_i[:, oc * R:(oc + 1) * R], ld)
            p.wait("scalar", ld)
            p.wait("vector", ld)
            _emit_qdq(p, upc, outb, i32b, dmb, cols[:, 0:1], cols[:, 1:2], cols[:, 4:5], cp)
            p.wait("sync", cp)
            p.dma("sync", upq_o[:, oc * R:(oc + 1) * R], outb, st)
            _emit_qdq(p, gac, gq, i32b, dmb, cols[:, 2:3], cols[:, 3:4], cols[:, 4:5], cp)
            p.wait("scalar", cp)
            p.emit("scalar", lambda e: e.activation(out=sg, in_=gq, func=AF.Sigmoid), inc=cp)
            p.emit("scalar", lambda e: e.activation(out=i32b, in_=sg, func=AF.Identity, scale=65536.0), inc=cp)
            p.wait("vector", cp)
            p.emit("vector", lambda e: e.scalar_tensor_tensor(out=dmb, in0=sg, scalar=65536.0,
                                                              in1=i32b, op0=ALU.mult, op1=ALU.subtract))
            p.drain("vector")
            p.emit("vector", lambda e: e.scalar_tensor_tensor(out=dmb, in0=dmb, scalar=0.5,
                                                              in1=i32b, op0=ALU.is_ge, op1=ALU.add))
            p.drain("vector")
            p.emit("vector", lambda e: e.tensor_scalar(out=sg, in0=dmb, scalar1=65535.0,
                                                       scalar2=1.0 / 65536.0, op0=ALU.min, op1=ALU.mult))
            p.drain("vector")
            p.wait("vector", st)
            p.emit("vector", lambda e: e.tensor_tensor(out=outb, in0=gq, in1=sg, op=ALU.mult), inc=cp)
            p.wait("sync", cp)
            p.dma("sync", t_o[:, oc * R:(oc + 1) * R], outb, st)
            p.emit("vector", lambda e: e.tensor_reduce(out=dmb[:, 0:1], in_=outb, axis=AX.X,
                                                       op=ALU.max, apply_absolute_value=True))
            p.drain("vector")
            p.emit("vector", lambda e: e.tensor_tensor(out=runm, in0=runm, in1=dmb[:, 0:1], op=ALU.max), inc=cp)
        p.wait("tensor", cp)
        _emit_colmax_out(p, runm[:, 0:1], mx1, msc[0:1, 7:8], ps_mx, idf, maxp4_o, mm, ev, st)
        p.wait("gpsimd", st)
        p.build()
    return nc


def build_launch4():
    nc = bass.Bass("TRN2", target_bir_lowering=False)
    ap = lambda n, sh, dt, kind: nc.dram_tensor(n, sh, dt, kind=kind).ap()
    t_i = ap("t_i", [128, 24 * R], F32, "ExternalInput")
    upq_i = ap("upq_i", [128, 24 * R], F32, "ExternalInput")
    maxp4_all = ap("maxp4_all", [8, 1], F32, "ExternalInput")
    identf = ap("identf", [128, 128], F32, "ExternalInput")
    oraw_o = ap("oraw_o", [128, 24 * R], F32, "ExternalOutput")
    maxp5_o = ap("maxp5_o", [1, 1], F32, "ExternalOutput")
    scr = nc.dram_tensor("scr", [1, 8], F32)

    with ExitStack() as ctx:
        sbuf = lambda n, sh, dt: ctx.enter_context(nc.sbuf_tensor(n, sh, dt))[:, :]
        psum = lambda n, sh, dt: ctx.enter_context(nc.psum_tensor(n, sh, dt))[:, :]
        sem = lambda n: ctx.enter_context(nc.semaphore(n))
        tc = sbuf("tc", [128, R], F32)
        uc = sbuf("uc", [128, R], F32)
        i32b = sbuf("i32b", [128, R], I32)
        dmb = sbuf("dmb", [128, R], F32)
        g2 = sbuf("g2", [128, R], F32)
        outb = sbuf("outb", [128, R], F32)
        m4row = sbuf("m4row", [1, 8], F32)
        msc = sbuf("msc", [1, 8], F32)
        cols = sbuf("cols", [128, 4], F32)
        runm = sbuf("runm", [128, 1], F32)
        mx1 = sbuf("mx1", [1, 128], F32)
        idf = sbuf("idf", [128, 128], F32)
        ps_mx = psum("ps_mx", [1, 128], F32)
        s_ld = sem("s_ld"); s_cp = sem("s_cp"); s_ev = sem("s_ev")
        s_st = sem("s_st"); s_aux = sem("s_aux"); s_mm = sem("s_mm")
        p = Prog(nc)
        ld = Sem(s_ld); cp = Sem(s_cp); ev = Sem(s_ev); st = Sem(s_st)
        aux = Sem(s_aux); mm = Sem(s_mm)

        p.dma("sync", m4row, maxp4_all.rearrange("a b -> b a"), ld)
        p.dma("sync", idf, identf, ld)
        p.emit("vector", lambda e: e.memset(cols[:, 2:3], 32768.0))
        p.emit("vector", lambda e: e.memset(runm, 0.0), inc=cp)
        p.wait("vector", ld)
        _emit_scales(p, scr, m4row, msc, cols[:, 0:1], cols[:, 1:2], ld, cp, aux, 0)
        p.wait("scalar", ld)
        p.wait("scalar", cp)
        for oc in range(24):
            p.wait("sync", cp)
            p.wait("sync", st)
            p.dma("sync", tc, t_i[:, oc * R:(oc + 1) * R], ld)
            p.dma("sync", uc, upq_i[:, oc * R:(oc + 1) * R], ld)
            p.wait("scalar", ld)
            p.wait("vector", ld)
            _emit_qdq(p, tc, g2, i32b, dmb, cols[:, 0:1], cols[:, 1:2], cols[:, 2:3], cp)
            p.emit("vector", lambda e: e.tensor_tensor(out=outb, in0=g2, in1=uc, op=ALU.mult), inc=cp)
            p.wait("sync", cp)
            p.dma("sync", oraw_o[:, oc * R:(oc + 1) * R], outb, st)
            p.emit("vector", lambda e: e.tensor_reduce(out=dmb[:, 0:1], in_=outb, axis=AX.X,
                                                       op=ALU.max, apply_absolute_value=True))
            p.drain("vector")
            p.emit("vector", lambda e: e.tensor_tensor(out=runm, in0=runm, in1=dmb[:, 0:1], op=ALU.max), inc=cp)
        p.wait("tensor", cp)
        _emit_colmax_out(p, runm[:, 0:1], mx1, msc[0:1, 7:8], ps_mx, idf, maxp5_o, mm, ev, st)
        p.wait("gpsimd", st)
        p.build()
    return nc


def build_launch5(debug=0):
    nc = bass.Bass("TRN2", target_bir_lowering=False)
    ap = lambda n, sh, dt, kind: nc.dram_tensor(n, sh, dt, kind=kind).ap()
    oraw_i = ap("oraw_i", [128, 24 * R], F32, "ExternalInput")
    maxp5_all = ap("maxp5_all", [8, 1], F32, "ExternalInput")
    xa_i = ap("xa_i", [R, D], F32, "ExternalInput")
    wdown = ap("wdown", [D, 3 * D], F32, "ExternalInput")
    ipw_t = ap("ipw_t", [D, 3 * D], F32, "ExternalInput")
    ipb_row = ap("ipb_row", [1, 3 * D], F32, "ExternalInput")
    opw_t = ap("opw_t", [D, D], F32, "ExternalInput")
    opb_row = ap("opb_row", [1, D], F32, "ExternalInput")
    rt_t = ap("rt_t", [D, 32], F32, "ExternalInput")
    rb_row = ap("rb_row", [1, 32], F32, "ExternalInput")
    gat = ap("gat", [D, 4096], F32, "ExternalInput")
    upt = ap("upt", [D, 4096], F32, "ExternalInput")
    dat = ap("dat", [4096, D], F32, "ExternalInput")
    sgt = ap("sgt", [D, 512], F32, "ExternalInput")
    sut = ap("sut", [D, 512], F32, "ExternalInput")
    sdt = ap("sdt", [512, D], F32, "ExternalInput")
    identb = ap("identb", [128, 128], BF16, "ExternalInput")
    identf = ap("identf", [128, 128], F32, "ExternalInput")
    out_o = ap("out_o", [R, D], F32, "ExternalOutput")
    scr = nc.dram_tensor("scr", [1, 8], F32)
    wt_dram = nc.dram_tensor("wt_dram", [32, R], F32)
    if debug:
        dbg_x1a = ap("dbg_x1a", [R, D], F32, "ExternalOutput")
        dbg_xn1 = ap("dbg_xn1", [128, KC * R], F32, "ExternalOutput")
        dbg_qkv1 = ap("dbg_qkv1", [128, 8 * 3072], F32, "ExternalOutput")
        dbg_ctx = ap("dbg_ctx", [128, NT * D], F32, "ExternalOutput")
        dbg_wt = ap("dbg_wt", [32, R], F32, "ExternalOutput")
        dbg_x1d1 = ap("dbg_x1d1", [R, D], F32, "ExternalOutput")
        dbg_hfm = ap("dbg_hfm", [128, 32 * 512], F32, "ExternalOutput")
        dbg_x1d2 = ap("dbg_x1d2", [R, D], F32, "ExternalOutput")

    with ExitStack() as ctx:
        sbuf = lambda n, sh, dt: ctx.enter_context(nc.sbuf_tensor(n, sh, dt))[:, :]
        psum = lambda n, sh, dt: ctx.enter_context(nc.psum_tensor(n, sh, dt))[:, :]
        sem = lambda n: ctx.enter_context(nc.semaphore(n))
        x1 = sbuf("x1", [128, NT * D], F32)
        xn1_fm = sbuf("xn1_fm", [128, KC * R], BF16)
        big1 = sbuf("big1", [128, 12288], F32)
        big2 = sbuf("big2", [128, 8192], F32)
        big3 = sbuf("big3", [128, 4096], F32)
        q512b = sbuf("q512b", [128, 512], F32)
        q512c = sbuf("q512c", [128, 512], F32)
        ev512 = sbuf("ev512", [128, 512], F32)
        bias5 = q512b
        wpool = sbuf("wpool", [128, 4096], BF16)
        dlhs = wpool
        gulhs = wpool[:, 0:2048]
        oplhs = wpool[:, 0:1024]
        rtb = wpool[:, 0:256]
        sglhs = wpool[:, 0:1024]
        sulhs = wpool[:, 1024:2048]
        sdlhs = wpool[:, 0:512]
        wbc = big1[:, 0:512]
        wtstg = big3[0:32, 1024:1152]
        wall = sbuf("wall", [128, 32], F32)
        opb = sbuf("opb", [128, 8], F32)
        rbb = sbuf("rbb", [128, 32], F32)
        bmx3 = sbuf("bmx3", [128, 96], F32)
        stat = sbuf("stat", [128, 8], F32)
        cols = sbuf("cols", [128, 4], F32)
        m5row = sbuf("m5row", [1, 8], F32)
        msc = sbuf("msc", [1, 8], F32)
        idb = sbuf("idb", [128, 128], BF16)
        idf = sbuf("idf", [128, 128], F32)
        ps_a = psum("ps_a", [128, 512], F32)
        ps_b = psum("ps_b", [128, 512], F32)
        ps_c = psum("ps_c", [128, 512], F32)
        ps_d = psum("ps_d", [128, 512], F32)
        ps_t = psum("ps_t", [128, D], BF16)
        ps_tf = psum("ps_tf", [128, 512], F32)
        s_ld = sem("s_ld"); s_cp = sem("s_cp"); s_ev = sem("s_ev")
        s_st = sem("s_st"); s_aux = sem("s_aux"); s_mm = sem("s_mm")
        s_lncp = sem("s_lncp")
        s_bar = sem("s_bar")
        slot_sems = [sem(f"s_sl{i}") for i in range(6)]
        p = Prog(nc)
        p.sems = {"lncp": Sem(s_lncp)}
        ld = Sem(s_ld); cp = Sem(s_cp); ev = Sem(s_ev); st = Sem(s_st)
        aux = Sem(s_aux); mm = Sem(s_mm)
        sg = [Sem(slot_sems[0]), Sem(slot_sems[1])]
        sw = [Sem(slot_sems[2]), Sem(slot_sems[3])]
        sd = [Sem(slot_sems[4]), Sem(slot_sems[5])]
        bar = Sem(s_bar)
        allsems = [ld, cp, ev, st, aux, mm, p.sems["lncp"]] + sg + sw + sd
        barrier = lambda: _phase_barrier(p, allsems, stat[0:1, 6:7], stat[0:1, 7:8], msc[0:1, 6:7], bar)

        p.dma("sync", m5row, maxp5_all.rearrange("a b -> b a"), ld)
        p.dma("sync", idb, identb, ld)
        p.dma("sync", idf, identf, ld)
        p.emit("vector", lambda e: e.memset(cols[:, 2:3], 32768.0), inc=cp)
        p.wait("vector", ld)
        _emit_scales(p, scr, m5row, msc, cols[:, 0:1], cols[:, 1:2], ld, cp, aux, 0)
        p.dma("gpsimd", opb, opb_row.rearrange("a (c p) -> p a c", p=128)[:, 0, :], ld,
              allow_slow_non_contiguous=True)
        p.dma("gpsimd", rbb, rb_row.broadcast_to([128, 32]), ld)
        for t in range(NT):
            p.dma("sync", x1[:, t * D:(t + 1) * D], xa_i[t * 128:(t + 1) * 128, :], ld)
        xa_ld = ld.n

        wdlhs = big1.bitcast(BF16)
        wtile = big3[:, 0:1024]
        yw = big3[:, 1024:2048]
        iw = big3[:, 2048:3072].bitcast(I32)[:, 0:1024]
        dw = big3[:, 3072:4096]
        wq_bf = ev512.bitcast(BF16)
        p.emit("vector", lambda e: e.memset(stat[:, 0:1], 16.0), inc=cp)
        p.wait("scalar", ld, xa_ld)
        for oc in range(8):
            for pc in range(3):
                p.wait("sync", cp)
                p.dma("sync", wtile, wdown[oc * 128:(oc + 1) * 128, pc * 1024:(pc + 1) * 1024], ld)
                p.wait("vector", ld)
                p.emit("vector", lambda e: e.tensor_reduce(
                    out=bmx3[:, 0:32], in_=wtile.rearrange("p (g x) -> p g x", x=32), axis=AX.X,
                    op=ALU.max, apply_absolute_value=True))
                p.drain("vector")
                p.emit("vector", lambda e: e.tensor_scalar(out=bmx3[:, 32:64], in0=bmx3[:, 0:32],
                                                           scalar1=1.0 / 7.0, scalar2=1e-12, op0=ALU.mult, op1=ALU.max))
                p.drain("vector")
                p.emit("vector", lambda e: e.reciprocal(out=bmx3[:, 64:96], in_=bmx3[:, 32:64]))
                p.drain("vector")
                p.emit("vector", lambda e: e.tensor_tensor(
                    out=yw.rearrange("p (g x) -> p g x", x=32), in0=wtile.rearrange("p (g x) -> p g x", x=32),
                    in1=bmx3[:, 64:96].rearrange("p (g o) -> p g o", o=1).broadcast_to([128, 32, 32]),
                    op=ALU.mult), inc=cp)
                p.wait("scalar", cp)
                p.emit("scalar", lambda e: e.activation(out=iw, in_=yw, func=AF.Identity,
                                                        bias=stat[:, 0:1], scale=1.0), inc=cp)
                p.wait("vector", cp)
                p.emit("vector", lambda e: e.scalar_tensor_tensor(out=dw, in0=yw, scalar=16.0,
                                                                  in1=iw, op0=ALU.add, op1=ALU.subtract))
                p.drain("vector")
                p.emit("vector", lambda e: e.scalar_tensor_tensor(out=dw, in0=dw, scalar=0.5,
                                                                  in1=iw, op0=ALU.is_ge, op1=ALU.add))
                p.drain("vector")
                p.emit("vector", lambda e: e.tensor_scalar(out=dw, in0=dw, scalar1=16.0, scalar2=None,
                                                           op0=ALU.subtract))
                p.drain("vector")
                p.emit("vector", lambda e: e.tensor_tensor(
                    out=wq_bf.rearrange("p (g x) -> p g x", x=32), in0=dw.rearrange("p (g x) -> p g x", x=32),
                    in1=bmx3[:, 32:64].rearrange("p (g o) -> p g o", o=1).broadcast_to([128, 32, 32]),
                    op=ALU.mult), inc=cp)
                p.wait("tensor", cp)
                p.wait("tensor", ev)
                for c in range(8):
                    p.emit("tensor", lambda e, c=c: e.transpose(
                        ps_t[:, c * 128:(c + 1) * 128], wq_bf[:, c * 128:(c + 1) * 128], idb),
                        inc=(mm if c == 7 else None))
                p.wait("scalar", mm)
                p.emit("scalar", lambda e, oc=oc, pc=pc: e.activation(
                    out=wdlhs.rearrange("p (o k j) -> p o k j", o=8, j=128)[:, oc, pc * 8:(pc + 1) * 8, :],
                    in_=ps_t.rearrange("p (k j) -> p k j", j=128), func=AF.Copy), inc=ev)
        o_slab = big2.bitcast(BF16)[:, 0:24 * 512]
        p.wait("scalar", cp)
        p.wait("scalar", ev)
        for tq in range(4):
            for kc in range(24):
                p.wait("sync", cp)
                p.dma("sync", q512b, oraw_i[:, kc * R + tq * 512: kc * R + (tq + 1) * 512], ld)
                p.wait("scalar", ld)
                _emit_qdq(p, q512b, o_slab[:, kc * 512:(kc + 1) * 512],
                          q512c.bitcast(I32)[:, 0:512], ev512, cols[:, 0:1], cols[:, 1:2], cols[:, 2:3], cp)
            p.wait("tensor", cp)
            for oc in range(8):
                pp = ps_a if oc % 2 == 0 else ps_b
                for kc in range(24):
                    p.emit("tensor", lambda e, pp=pp, oc=oc, kc=kc: e.matmul(
                        pp, wdlhs.rearrange("p (o k j) -> p o k j", o=8, j=128)[:, oc, kc, :],
                        o_slab[:, kc * 512:(kc + 1) * 512],
                        start=(kc == 0), stop=(kc == 23)), inc=(mm if kc == 23 else None))
                p.wait("scalar", mm)
                p.wait("scalar", st)
                p.emit("scalar", lambda e, pp=pp: e.activation(out=ev512, in_=pp, func=AF.Copy), inc=ev)
                p.wait("tensor", ev)
                for tb in range(4):
                    p.emit("tensor", lambda e, tb=tb: e.transpose(
                        ps_tf[:, tb * 128:(tb + 1) * 128], ev512[:, tb * 128:(tb + 1) * 128], idf),
                        inc=(mm if tb == 3 else None))
                p.wait("vector", mm)
                p.wait("vector", ld, xa_ld)
                p.emit("vector", lambda e, tq=tq, oc=oc: e.tensor_tensor(
                    out=x1.rearrange("p (t j) -> p t j", j=D)[:, tq * 4:(tq + 1) * 4, oc * 128:(oc + 1) * 128],
                    in0=x1.rearrange("p (t j) -> p t j", j=D)[:, tq * 4:(tq + 1) * 4, oc * 128:(oc + 1) * 128],
                    in1=ps_tf.rearrange("p (t j) -> p t j", j=128), op=ALU.add), inc=ev)
                p.wait("tensor", ev)
            p.wait("scalar", ev)

        if debug & 1:
            p.wait("sync", ev)
            for t in range(NT):
                p.dma("sync", dbg_x1a[t * 128:(t + 1) * 128, :], x1[:, t * D:(t + 1) * D], st)
            p.wait("vector", st)
        barrier()
        sa = big3[:, 0:1024]
        sb = big3[:, 1024:2048]
        xb = big3[:, 2048:2560].bitcast(BF16)[:, 0:1024]
        stats = {"m": stat[:, 1:2], "var": stat[:, 2:3], "sd": stat[:, 3:4],
                 "rstd": stat[:, 4:5], "eps_col": stat[:, 5:6]}
        p.emit("vector", lambda e: e.memset(stat[:, 5:6], EPS))
        p.wait("vector", ev)
        for t in range(NT):
            xt = x1[:, t * D:(t + 1) * D]
            _ln_tile(p, xt, sb, sa, stats)
            p.emit("vector", lambda e: e.tensor_copy(out=xb, in_=sb), inc=p.sems["lncp"])
            cpv = p.sems["lncp"].n
            p.wait("tensor", p.sems["lncp"], cpv)
            p.wait("tensor", ev)
            for c in range(KC):
                p.emit("tensor", lambda e, c=c: e.transpose(
                    ps_t[:, c * 128:(c + 1) * 128], xb[:, c * 128:(c + 1) * 128], idb),
                    inc=(mm if c == KC - 1 else None))
            p.wait("scalar", mm)
            p.emit("scalar", lambda e, t=t: e.activation(
                out=xn1_fm.rearrange("p (c j) -> p c j", j=R)[:, :, t * 128:(t + 1) * 128],
                in_=ps_t.rearrange("p (c j) -> p c j", j=128), func=AF.Copy), inc=ev)
            p.wait("vector", ev)

        if debug & 2:
            p.wait("gpsimd", ev)
            p.dma("gpsimd", dbg_xn1, xn1_fm, st)
            p.wait("vector", st)
        barrier()
        qkv1 = big1.bitcast(BF16)
        wblks = [big3[:, 0:2048].bitcast(BF16), wpool[:, 0:4096]]
        bias5s = [q512b, q512c]
        prodv = big3[:, 0:2048]
        prodg = big3[:, 2048:3072]
        ctxsf = big3[:, 3072:4096]
        xbC = q512b.bitcast(BF16)[:, 0:1024]
        scoresb = q512c[:, 0:128]
        smx = q512c[:, 128:144]
        ssm = q512c[:, 144:160]
        src_ = q512c[:, 160:176]
        ctx_fm = big2.bitcast(BF16)[:, 0:16384]
        wb_used = [0, 0]; wb_ld = [0, 0]; b5_used = [0, 0]
        pp_free = [0, 0]
        p.wait("gpsimd", p.sems["lncp"])
        p.wait("gpsimd", ev)
        for pb in range(2):
            if pb == 1:
                p.wait("gpsimd", cp)
            for oc in range(6):
                sl = oc % 2
                p.wait("gpsimd", mm, at=wb_used[sl])
                p.dma("gpsimd", wblks[sl].rearrange("p (k j) -> p k j", j=512),
                      ipw_t.rearrange("(k p) f -> p k f", p=128)[:, :, oc * 512:(oc + 1) * 512], sg[sl])
                p.wait("gpsimd", ev, at=b5_used[sl])
                p.dma("gpsimd", bias5s[sl].rearrange("p (o j) -> p o j", o=1),
                      ipb_row[0:1, oc * 512:(oc + 1) * 512].rearrange(
                          "a (o j) -> a o j", o=1).broadcast_to([128, 1, 512]), sw[sl])
                p.wait("tensor", sg[sl])
                for si in range(8):
                    t_idx = si * 2 + pb
                    psl = si % 2
                    pp = ps_a if psl == 0 else ps_b
                    p.wait("tensor", ev, at=pp_free[psl])
                    for kc in range(KC):
                        p.emit("tensor", lambda e, pp=pp, kc=kc, t_idx=t_idx, sl=sl: e.matmul(
                            pp, xn1_fm.rearrange("p (c j) -> p c j", j=R)[:, kc, t_idx * 128:(t_idx + 1) * 128],
                            wblks[sl][:, kc * 512:(kc + 1) * 512],
                            start=(kc == 0), stop=(kc == KC - 1)), inc=(mm if kc == KC - 1 else None))
                    if si == 7:
                        wb_used[sl] = mm.n
                    p.wait("vector", mm, at=mm.n)
                    p.wait("vector", sw[sl])
                    p.emit("vector", lambda e, pp=pp, si=si, oc=oc, sl=sl: e.tensor_tensor(
                        out=qkv1.rearrange("p (s f) -> p s f", f=3072)[:, si, oc * 512:(oc + 1) * 512],
                        in0=pp, in1=bias5s[sl], op=ALU.add), inc=ev)
                    pp_free[psl] = ev.n
                    if si == 7:
                        b5_used[sl] = ev.n
            if debug & 4:
                p.wait("gpsimd", ev)
                if pb == 1:
                    p.dma("gpsimd", dbg_qkv1, qkv1, st)
                    p.wait("vector", st)
            qv = qkv1.rearrange("p (s th d) -> p s th d", th=3 * NH, d=64)
            p.wait("vector", ev)
            p.wait("gpsimd", ev)
            sc_rdy = 0
            for si in range(8):
                for tp in range(4):
                    p.emit("vector", lambda e, si=si, tp=tp, qv=qv: e.tensor_tensor(
                        out=prodv.rearrange("p (t2 h d) -> p t2 h d", t2=2, d=64),
                        in0=qv[:, si:si + 1, 0:NH, :].broadcast_to([128, 2, NH, 64]),
                        in1=qv[:, 2 * tp:2 * tp + 2, NH:2 * NH, :], op=ALU.mult))
                    p.emit("vector", lambda e, tp=tp: e.tensor_reduce(
                        out=scoresb.rearrange("p (h t) -> p h t", t=8)[:, :, 2 * tp:2 * tp + 2],
                        in_=prodv.rearrange("p (t2 h d) -> p h t2 d", t2=2, d=64), axis=AX.X, op=ALU.add))
                p.drain("vector")
                p.emit("vector", lambda e: e.tensor_reduce(
                    out=smx, in_=scoresb.rearrange("p (h t) -> p h t", t=8), axis=AX.X, op=ALU.max))
                p.drain("vector")
                p.emit("vector", lambda e: e.tensor_tensor(
                    out=scoresb.rearrange("p (h t) -> p h t", t=8), in0=scoresb.rearrange("p (h t) -> p h t", t=8),
                    in1=smx.rearrange("p (h o) -> p h o", o=1).broadcast_to([128, 16, 8]), op=ALU.subtract), inc=cp)
                p.wait("scalar", cp, at=cp.n)
                p.emit("scalar", lambda e: e.activation(out=scoresb, in_=scoresb, func=AF.Exp, scale=0.125), inc=cp)
                p.wait("vector", cp, at=cp.n)
                p.emit("vector", lambda e: e.tensor_reduce(
                    out=ssm, in_=scoresb.rearrange("p (h t) -> p h t", t=8), axis=AX.X, op=ALU.add))
                p.drain("vector")
                p.emit("vector", lambda e: e.reciprocal(out=src_, in_=ssm))
                p.drain("vector")
                p.emit("vector", lambda e: e.tensor_tensor(
                    out=scoresb.rearrange("p (h t) -> p h t", t=8), in0=scoresb.rearrange("p (h t) -> p h t", t=8),
                    in1=src_.rearrange("p (h o) -> p h o", o=1).broadcast_to([128, 16, 8]), op=ALU.mult), inc=cp)
                sc_rdy = cp.n
                p.wait("gpsimd", cp, at=sc_rdy)
                for t in range(8):
                    p.emit("gpsimd", lambda e, t=t, qv=qv: e.tensor_tensor(
                        out=prodg[:, 0:1024].rearrange("p (h d) -> p h d", d=64),
                        in0=qv[:, t, 2 * NH:3 * NH, :],
                        in1=scoresb.rearrange("p (h t) -> p h t", t=8)[:, :, t:t + 1].broadcast_to([128, 16, 64]),
                        op=ALU.mult))
                    if t == 0:
                        p.emit("gpsimd", lambda e: e.tensor_copy(out=ctxsf, in_=prodg[:, 0:1024]))
                    else:
                        p.emit("gpsimd", lambda e: e.tensor_tensor(out=ctxsf, in0=ctxsf, in1=prodg[:, 0:1024], op=ALU.add))
                p.emit("gpsimd", lambda e: e.tensor_copy(out=xbC, in_=ctxsf), inc=cp)
                p.wait("tensor", cp, at=cp.n)
                p.wait("tensor", ev)
                t_idx = si * 2 + pb
                for c in range(KC):
                    p.emit("tensor", lambda e, c=c: e.transpose(
                        ps_t[:, c * 128:(c + 1) * 128], xbC[:, c * 128:(c + 1) * 128], idb),
                        inc=(mm if c == KC - 1 else None))
                p.wait("scalar", mm, at=mm.n)
                p.emit("scalar", lambda e, t_idx=t_idx: e.activation(
                    out=ctx_fm.rearrange("p (c j) -> p c j", j=R)[:, :, t_idx * 128:(t_idx + 1) * 128],
                    in_=ps_t.rearrange("p (c j) -> p c j", j=128), func=AF.Copy), inc=ev)
                p.wait("gpsimd", ev)
                p.wait("vector", cp)

        if debug & 8:
            p.wait("gpsimd", ev)
            p.dma("gpsimd", dbg_ctx, ctx_fm, st)
            p.wait("vector", st)
        barrier()
        for oc in range(8):
            p.wait("gpsimd", mm)
            p.dma("gpsimd", oplhs.rearrange("p (k j) -> p k j", j=128),
                  opw_t.rearrange("(k p) f -> p k f", p=128)[:, :, oc * 128:(oc + 1) * 128], ld)
            p.wait("tensor", ld)
            for tq in range(4):
                pp = ps_a if tq % 2 == 0 else ps_b
                p.wait("tensor", ev)
                for kc in range(KC):
                    p.emit("tensor", lambda e, pp=pp, kc=kc, oc=oc, tq=tq: e.matmul(
                        pp, oplhs[:, kc * 128:(kc + 1) * 128],
                        ctx_fm[:, kc * R + tq * 512: kc * R + (tq + 1) * 512],
                        start=(kc == 0), stop=(kc == KC - 1)), inc=(mm if kc == KC - 1 else None))
                p.wait("scalar", mm)
                p.emit("scalar", lambda e, pp=pp, oc=oc: e.activation(
                    out=ev512, in_=pp, func=AF.Identity, bias=opb[:, oc:oc + 1], scale=1.0), inc=ev)
                p.wait("tensor", ev)
                for tb in range(4):
                    p.emit("tensor", lambda e, tb=tb: e.transpose(
                        ps_tf[:, tb * 128:(tb + 1) * 128], ev512[:, tb * 128:(tb + 1) * 128], idf),
                        inc=(mm if tb == 3 else None))
                p.wait("vector", mm)
                p.emit("vector", lambda e, tq=tq, oc=oc: e.tensor_tensor(
                    out=x1.rearrange("p (t j) -> p t j", j=D)[:, tq * 4:(tq + 1) * 4, oc * 128:(oc + 1) * 128],
                    in0=x1.rearrange("p (t j) -> p t j", j=D)[:, tq * 4:(tq + 1) * 4, oc * 128:(oc + 1) * 128],
                    in1=ps_tf.rearrange("p (t j) -> p t j", j=128), op=ALU.add), inc=ev)
                p.wait("tensor", ev)

        if debug & 16:
            p.wait("sync", ev)
            for t in range(NT):
                p.dma("sync", dbg_x1d1[t * 128:(t + 1) * 128, :], x1[:, t * D:(t + 1) * D], st)
            p.wait("vector", st)
        barrier()
        prod = prodv
        p.wait("gpsimd", mm)
        p.dma("gpsimd", rtb.rearrange("p (k j) -> p k j", j=32),
              rt_t.rearrange("(k p) f -> p k f", p=128), ld)
        p.wait("tensor", ld)
        for t in range(NT):
            pp = ps_a if t % 2 == 0 else ps_b
            p.wait("tensor", ev)
            for kc in range(KC):
                p.emit("tensor", lambda e, pp=pp, kc=kc, t=t: e.matmul(
                    pp[:, 0:32],
                    xn1_fm.rearrange("p (c j) -> p c j", j=R)[:, kc, t * 128:(t + 1) * 128],
                    rtb[:, kc * 32:(kc + 1) * 32],
                    start=(kc == 0), stop=(kc == KC - 1)), inc=(mm if kc == KC - 1 else None))
            p.wait("vector", mm)
            wsl = wall[:, 0:32]
            p.emit("vector", lambda e, pp=pp, wsl=wsl: e.tensor_tensor(out=prod[:, 0:32], in0=pp[:, 0:32], in1=rbb, op=ALU.add), inc=ev)
            p.drain("vector")
            p.emit("vector", lambda e: e.reduce_max(out=stat[:, 6:7], in_=prod[:, 0:32], axis=AX.X))
            p.drain("vector")
            p.emit("vector", lambda e: e.tensor_scalar(out=prod[:, 0:32], in0=prod[:, 0:32],
                                                       scalar1=stat[:, 6:7], scalar2=None, op0=ALU.subtract), inc=cp)
            p.wait("scalar", cp)
            p.emit("scalar", lambda e: e.activation(out=prod[:, 0:32], in_=prod[:, 0:32], func=AF.Exp), inc=cp)
            p.wait("vector", cp)
            p.emit("vector", lambda e: e.reduce_sum(out=stat[:, 6:7], in_=prod[:, 0:32], axis=AX.X))
            p.drain("vector")
            p.emit("vector", lambda e: e.reciprocal(out=stat[:, 7:8], in_=stat[:, 6:7]))
            p.drain("vector")
            p.emit("vector", lambda e: e.tensor_scalar(out=prod[:, 0:32], in0=prod[:, 0:32],
                                                       scalar1=stat[:, 7:8], scalar2=None, op0=ALU.mult))
            p.drain("vector")
            p.emit("vector", lambda e: e.tensor_copy(out=prod[:, 32:64], in_=prod[:, 0:32]))
            p.emit("vector", lambda e, wsl=wsl: e.memset(wsl, 0.0))
            p.drain("vector")
            for it in range(4):
                p.emit("vector", lambda e: e.reduce_max(out=stat[:, 6:7], in_=prod[:, 32:64], axis=AX.X))
                p.drain("vector")
                p.emit("vector", lambda e: e.tensor_scalar(out=prod[:, 64:96], in0=prod[:, 32:64],
                                                           scalar1=stat[:, 6:7], scalar2=None, op0=ALU.is_ge))
                p.drain("vector")
                p.emit("vector", lambda e: e.tensor_tensor(out=prod[:, 96:128], in0=prod[:, 64:96], in1=prod[:, 0:32], op=ALU.mult))
                p.drain("vector")
                p.emit("vector", lambda e, wsl=wsl: e.tensor_tensor(out=wsl, in0=wsl, in1=prod[:, 96:128], op=ALU.add))
                p.emit("vector", lambda e: e.tensor_scalar(out=prod[:, 96:128], in0=prod[:, 64:96],
                                                           scalar1=-1.0, scalar2=1.0, op0=ALU.mult, op1=ALU.add))
                p.drain("vector")
                p.emit("vector", lambda e: e.tensor_tensor(out=prod[:, 32:64], in0=prod[:, 32:64], in1=prod[:, 96:128], op=ALU.mult))
                p.drain("vector")
                p.emit("vector", lambda e: e.tensor_tensor(out=prod[:, 32:64], in0=prod[:, 32:64], in1=prod[:, 64:96], op=ALU.subtract))
                p.drain("vector")
            p.emit("vector", lambda e, wsl=wsl: e.reduce_sum(out=stat[:, 6:7], in_=wsl, axis=AX.X))
            p.drain("vector")
            p.emit("vector", lambda e: e.tensor_scalar(out=stat[:, 6:7], in0=stat[:, 6:7],
                                                       scalar1=1e-12, scalar2=None, op0=ALU.max))
            p.drain("vector")
            p.emit("vector", lambda e: e.reciprocal(out=stat[:, 7:8], in_=stat[:, 6:7]))
            p.drain("vector")
            p.emit("vector", lambda e, wsl=wsl: e.tensor_scalar(out=wsl, in0=wsl,
                                                                scalar1=stat[:, 7:8], scalar2=None, op0=ALU.mult), inc=cp)
            p.wait("tensor", cp)
            p.wait("tensor", aux)
            p.emit("tensor", lambda e, wsl=wsl: e.transpose(ps_tf[0:32, 0:128], wsl, idf), inc=mm)
            p.wait("vector", mm)
            p.wait("vector", st)
            p.emit("vector", lambda e: e.tensor_copy(out=wtstg, in_=ps_tf[0:32, 0:128]), inc=aux)
            p.wait("sync", aux)
            p.dma("sync", wt_dram.ap()[0:32, t * 128:(t + 1) * 128], wtstg, st)
        p.wait("vector", st)

        barrier()
        h_fm = big2.bitcast(BF16)[:, 0:32 * 512]
        sils = [big3[:, 0:512], big3[:, 512:1024]]
        h1s = [big3[:, 1024:1536], big3[:, 1536:2048]]
        gus = [wpool[:, 0:2048], wpool[:, 2048:4096]]
        wbcs = [big1[:, 0:512], big1[:, 512:1024]]
        dls = [big1[:, 1024:3072].bitcast(BF16), big1[:, 3072:5120].bitcast(BF16)]
        gu_used = [0, 0]; wbc_used = [0, 0]; gu_ld = [0, 0]; wbc_ld = [0, 0]
        pe_free = [0, 0]
        dl_used = [0, 0]; dl_ld = [0, 0]; dn_free = [0, 0]
        if debug & 32:
            p.wait("sync", st)
            p.dma("sync", dbg_wt, wt_dram.ap(), st)
        p.wait("gpsimd", st)
        for tq in range(4):
            p.wait("vector", mm)
            p.wait("gpsimd", mm)
            p.wait("tensor", ev)
            for ei in range(32):
                sl = ei % 2
                pg = ps_a if sl == 0 else ps_c
                pu = ps_b if sl == 0 else ps_d
                p.wait("gpsimd", mm, at=gu_used[sl])
                p.dma("gpsimd", gus[sl].rearrange("p (w k j) -> p w k j", w=2, j=128)[:, 0, :, :],
                      gat.rearrange("(k p) f -> p k f", p=128)[:, :, ei * 128:(ei + 1) * 128], sg[sl])
                p.dma("gpsimd", gus[sl].rearrange("p (w k j) -> p w k j", w=2, j=128)[:, 1, :, :],
                      upt.rearrange("(k p) f -> p k f", p=128)[:, :, ei * 128:(ei + 1) * 128], sg[sl])
                p.wait("gpsimd", cp, at=wbc_used[sl])
                p.dma("gpsimd", wbcs[sl].rearrange("p (o j) -> p o j", o=1),
                      wt_dram.ap()[ei:ei + 1, tq * 512:(tq + 1) * 512].rearrange(
                          "a (o j) -> a o j", o=1).broadcast_to([128, 1, 512]), sw[sl])
                p.wait("tensor", sg[sl])
                p.wait("tensor", cp, at=pe_free[sl])
                for kc in range(KC):
                    p.emit("tensor", lambda e, pg=pg, kc=kc, tq=tq, sl=sl: e.matmul(
                        pg, gus[sl][:, kc * 128:(kc + 1) * 128],
                        xn1_fm[:, kc * R + tq * 512: kc * R + (tq + 1) * 512],
                        start=(kc == 0), stop=(kc == KC - 1)))
                    p.emit("tensor", lambda e, pu=pu, kc=kc, tq=tq, sl=sl: e.matmul(
                        pu, gus[sl][:, 1024 + kc * 128: 1024 + (kc + 1) * 128],
                        xn1_fm[:, kc * R + tq * 512: kc * R + (tq + 1) * 512],
                        start=(kc == 0), stop=(kc == KC - 1)), inc=(mm if kc == KC - 1 else None))
                gu_used[sl] = mm.n
                p.wait("scalar", mm, at=mm.n)
                p.emit("scalar", lambda e, pg=pg, sl=sl: e.activation(out=sils[sl], in_=pg, func=AF.Silu), inc=cp)
                p.wait("vector", cp, at=cp.n)
                p.wait("vector", sw[sl])
                p.emit("vector", lambda e, pu=pu, sl=sl: e.tensor_tensor(out=h1s[sl], in0=pu, in1=sils[sl], op=ALU.mult))
                p.emit("vector", lambda e, ei=ei, sl=sl: e.tensor_tensor(
                    out=h_fm[:, ei * 512:(ei + 1) * 512], in0=h1s[sl], in1=wbcs[sl], op=ALU.mult), inc=cp)
                wbc_used[sl] = cp.n
                pe_free[sl] = cp.n
            p.wait("tensor", cp)
            if (debug & 64) and tq == 0:
                p.wait("gpsimd", cp)
                p.dma("gpsimd", dbg_hfm, h_fm, st)
                p.wait("vector", st)
            for oc in range(8):
                sl = oc % 2
                pp = ps_a if sl == 0 else ps_b
                p.wait("gpsimd", mm, at=dl_used[sl])
                p.dma("gpsimd", dls[sl].rearrange("p (k j) -> p k j", j=128),
                      dat.rearrange("(k p) f -> p k f", p=128)[:, :, oc * 128:(oc + 1) * 128], sd[sl])
                p.wait("tensor", sd[sl])
                p.wait("tensor", ev, at=dn_free[sl])
                for kc in range(32):
                    p.emit("tensor", lambda e, pp=pp, kc=kc, sl=sl: e.matmul(
                        pp, dls[sl][:, kc * 128:(kc + 1) * 128], h_fm[:, kc * 512:(kc + 1) * 512],
                        start=(kc == 0), stop=(kc == 31)), inc=(mm if kc == 31 else None))
                dl_used[sl] = mm.n
                p.wait("scalar", mm, at=mm.n)
                p.wait("scalar", ev)
                p.emit("scalar", lambda e, pp=pp: e.activation(out=ev512, in_=pp, func=AF.Copy), inc=ev)
                p.wait("tensor", ev, at=ev.n)
                for tb in range(4):
                    p.emit("tensor", lambda e, tb=tb: e.transpose(
                        ps_tf[:, tb * 128:(tb + 1) * 128], ev512[:, tb * 128:(tb + 1) * 128], idf),
                        inc=(mm if tb == 3 else None))
                p.wait("vector", mm, at=mm.n)
                p.emit("vector", lambda e, tq=tq, oc=oc: e.tensor_tensor(
                    out=x1.rearrange("p (t j) -> p t j", j=D)[:, tq * 4:(tq + 1) * 4, oc * 128:(oc + 1) * 128],
                    in0=x1.rearrange("p (t j) -> p t j", j=D)[:, tq * 4:(tq + 1) * 4, oc * 128:(oc + 1) * 128],
                    in1=ps_tf.rearrange("p (t j) -> p t j", j=128), op=ALU.add), inc=ev)
                dn_free[sl] = ev.n
                p.wait("tensor", ev)

        if debug & 128:
            p.wait("sync", ev)
            for t in range(NT):
                p.dma("sync", dbg_x1d2[t * 128:(t + 1) * 128, :], x1[:, t * D:(t + 1) * D], st)
            p.wait("vector", st)
        barrier()
        s_h = big3.bitcast(BF16)[:, 0:4 * R]
        for oc in range(4):
            p.wait("gpsimd", mm)
            p.dma("gpsimd", sglhs.rearrange("p (k j) -> p k j", j=128),
                  sgt.rearrange("(k p) f -> p k f", p=128)[:, :, oc * 128:(oc + 1) * 128], ld)
            p.dma("gpsimd", sulhs.rearrange("p (k j) -> p k j", j=128),
                  sut.rearrange("(k p) f -> p k f", p=128)[:, :, oc * 128:(oc + 1) * 128], ld)
            p.wait("tensor", ld)
            for tq in range(4):
                pg = ps_a if tq % 2 == 0 else ps_c
                pu = ps_b if tq % 2 == 0 else ps_d
                p.wait("tensor", cp)
                for kc in range(KC):
                    p.emit("tensor", lambda e, pg=pg, kc=kc, tq=tq: e.matmul(
                        pg, sglhs[:, kc * 128:(kc + 1) * 128],
                        xn1_fm[:, kc * R + tq * 512: kc * R + (tq + 1) * 512],
                        start=(kc == 0), stop=(kc == KC - 1)))
                    p.emit("tensor", lambda e, pu=pu, kc=kc, tq=tq: e.matmul(
                        pu, sulhs[:, kc * 128:(kc + 1) * 128],
                        xn1_fm[:, kc * R + tq * 512: kc * R + (tq + 1) * 512],
                        start=(kc == 0), stop=(kc == KC - 1)), inc=(mm if kc == KC - 1 else None))
                p.wait("scalar", mm)
                p.emit("scalar", lambda e, pg=pg: e.activation(out=q512b, in_=pg, func=AF.Silu), inc=cp)
                p.wait("vector", cp)
                p.emit("vector", lambda e, pu=pu, oc=oc, tq=tq: e.tensor_tensor(
                    out=s_h[:, oc * R + tq * 512: oc * R + (tq + 1) * 512],
                    in0=pu, in1=q512b, op=ALU.mult), inc=cp)
        p.wait("tensor", cp)
        for oc in range(8):
            p.wait("gpsimd", mm)
            p.dma("gpsimd", sdlhs.rearrange("p (k j) -> p k j", j=128),
                  sdt.rearrange("(k p) f -> p k f", p=128)[:, :, oc * 128:(oc + 1) * 128], ld)
            p.wait("tensor", ld)
            for tq in range(4):
                pp = ps_a if tq % 2 == 0 else ps_b
                p.wait("tensor", ev)
                for kc in range(4):
                    p.emit("tensor", lambda e, pp=pp, kc=kc, tq=tq: e.matmul(
                        pp, sdlhs[:, kc * 128:(kc + 1) * 128],
                        s_h[:, kc * R + tq * 512: kc * R + (tq + 1) * 512],
                        start=(kc == 0), stop=(kc == 3)), inc=(mm if kc == 3 else None))
                p.wait("scalar", mm)
                p.emit("scalar", lambda e, pp=pp: e.activation(out=ev512, in_=pp, func=AF.Copy), inc=ev)
                p.wait("tensor", ev)
                for tb in range(4):
                    p.emit("tensor", lambda e, tb=tb: e.transpose(
                        ps_tf[:, tb * 128:(tb + 1) * 128], ev512[:, tb * 128:(tb + 1) * 128], idf),
                        inc=(mm if tb == 3 else None))
                p.wait("vector", mm)
                p.emit("vector", lambda e, tq=tq, oc=oc: e.tensor_tensor(
                    out=x1.rearrange("p (t j) -> p t j", j=D)[:, tq * 4:(tq + 1) * 4, oc * 128:(oc + 1) * 128],
                    in0=x1.rearrange("p (t j) -> p t j", j=D)[:, tq * 4:(tq + 1) * 4, oc * 128:(oc + 1) * 128],
                    in1=ps_tf.rearrange("p (t j) -> p t j", j=128), op=ALU.add), inc=ev)
                p.wait("tensor", ev)

        barrier()
        p.wait("sync", ev)
        for t in range(NT):
            p.dma("sync", out_o[t * 128:(t + 1) * 128, :], x1[:, t * D:(t + 1) * D], st)
        p.wait("gpsimd", st)
        p.build()
    return nc


# version 03
# speedup vs baseline: 1.2012x; 1.0775x over previous
from contextlib import ExitStack
import numpy as np
import ml_dtypes
import concourse.bass as bass
import concourse.mybir as mybir
from concourse.bass_utils import run_bass_kernel_spmd

F32 = mybir.dt.float32
BF16 = mybir.dt.bfloat16
I32 = mybir.dt.int32
ALU = mybir.AluOpType
AF = mybir.ActivationFunctionType
AX = mybir.AxisListType

NCORES = 8
R = 2048
NT = 16
D = 1024
KC = 8
NH = 16
HD = 64
EPS = 1e-5


class Sem:
    __slots__ = ("h", "n")

    def __init__(self, h):
        self.h = h
        self.n = 0


class Prog:
    ENGINES = ("sync", "scalar", "vector", "tensor", "gpsimd")

    def __init__(self, nc):
        self.nc = nc
        self.ops = {e: [] for e in self.ENGINES}

    def emit(self, ek, make, inc=None, inck=1):
        if inc is not None:
            def fn(eng, make=make, s=inc, k=inck):
                make(eng).then_inc(s.h, k)
            inc.n += inck
        else:
            fn = make
        self.ops[ek].append(fn)

    def dma(self, ek, out, in_, inc, **kw):
        self.emit(ek, lambda eng, o=out, i=in_, kw=kw: eng.dma_start(out=o, in_=i, **kw),
                  inc=inc, inck=16)

    def wait(self, ek, sem, at=None):
        v = sem.n if at is None else at
        self.ops[ek].append(lambda eng, s=sem, vv=v: eng.wait_ge(s.h, vv))

    def drain(self, ek):
        self.ops[ek].append(lambda eng: eng.drain())

    def build(self):
        with self.nc.Block() as block:
            for ek in self.ENGINES:
                def runner(eng, ek=ek):
                    for fn in self.ops[ek]:
                        fn(eng)
                getattr(block, ek)(runner)


def _ln_tile(p, xs, xn_out, sa, stats, extra_absmax=None):
    cp = p.sems["lncp"]
    v, s = "vector", "scalar"
    m, var, sd, rstd, eps_col = stats["m"], stats["var"], stats["sd"], stats["rstd"], stats["eps_col"]
    p.emit(v, lambda e, a=xs, o=m: e.reduce_sum(out=o, in_=a, axis=AX.X))
    p.drain(v)
    p.emit(v, lambda e, o=m, a=m: e.tensor_scalar(out=o, in0=a, scalar1=1.0 / D, scalar2=None, op0=ALU.mult))
    p.drain(v)
    p.emit(v, lambda e, o=sa, a=xs, mm=m: e.tensor_scalar(out=o, in0=a, scalar1=mm, scalar2=None, op0=ALU.subtract), inc=cp)
    p.wait(s, cp)
    p.emit(s, lambda e, o=xn_out, a=sa: e.activation(out=o, in_=a, func=AF.Square), inc=cp)
    p.wait(v, cp)
    p.emit(v, lambda e, o=var, a=xn_out: e.reduce_sum(out=o, in_=a, axis=AX.X))
    p.drain(v)
    p.emit(v, lambda e, o=var, a=var: e.tensor_scalar(out=o, in0=a, scalar1=1.0 / D, scalar2=None, op0=ALU.mult), inc=cp)
    p.wait(s, cp)
    p.emit(s, lambda e, o=sd, a=var, b=eps_col: e.activation(out=o, in_=a, func=AF.Sqrt, bias=b), inc=cp)
    p.wait(v, cp)
    p.emit(v, lambda e, o=rstd, a=sd: e.reciprocal(out=o, in_=a))
    p.drain(v)
    p.emit(v, lambda e, o=xn_out, a=sa, rr=rstd: e.tensor_scalar(out=o, in0=a, scalar1=rr, scalar2=None, op0=ALU.mult), inc=cp)
    if extra_absmax is not None:
        p.drain(v)
        p.emit(v, lambda e, o=extra_absmax, a=xn_out: e.tensor_reduce(
            out=o, in_=a, axis=AX.X, op=ALU.max, apply_absolute_value=True))
        p.drain(v)


def build_launch1():
    nc = bass.Bass("TRN2", target_bir_lowering=False)
    xin = nc.dram_tensor("xin", [R, D], F32, kind="ExternalInput").ap()
    w1t = nc.dram_tensor("w1t", [D, 3 * D], F32, kind="ExternalInput").ap()
    identb = nc.dram_tensor("identb", [128, 128], BF16, kind="ExternalInput").ap()
    identf = nc.dram_tensor("identf", [128, 128], F32, kind="ExternalInput").ap()

    xn_o = nc.dram_tensor("xn_o", [R, D], F32, kind="ExternalOutput").ap()
    phiq_fm_o = nc.dram_tensor("phiq_fm_o", [128, KC * R], BF16, kind="ExternalOutput").ap()
    phik_fm_o = nc.dram_tensor("phik_fm_o", [128, KC * R], BF16, kind="ExternalOutput").ap()
    phik_tm_o = nc.dram_tensor("phik_tm_o", [R, D], BF16, kind="ExternalOutput").ap()
    v_tm_o = nc.dram_tensor("v_tm_o", [R, NH * (HD + 1)], BF16, kind="ExternalOutput").ap()
    ssum_o = nc.dram_tensor("ssum_o", [64, 128 * 65], F32, kind="ExternalOutput").ap()
    sc0_o = nc.dram_tensor("sc0_o", [64, 128 * 65], F32, kind="ExternalOutput").ap()
    maxp1_o = nc.dram_tensor("maxp1_o", [1, 1], F32, kind="ExternalOutput").ap()

    with ExitStack() as ctx:
        sbuf = lambda n, sh, dt: ctx.enter_context(nc.sbuf_tensor(n, sh, dt))[:, :]
        psum = lambda n, sh, dt: ctx.enter_context(nc.psum_tensor(n, sh, dt))[:, :]
        sem = lambda n: ctx.enter_context(nc.semaphore(n))
        xw = sbuf("xw", [128, NT * D], F32)
        sa = sbuf("sa", [128, D], F32)
        sb = sbuf("sb", [128, D], F32)
        stat = sbuf("stat", [128, 8], F32)
        pmx = sbuf("pmx", [128, NT], F32)
        mxc = sbuf("mxc", [128, 1], F32)
        mx1 = sbuf("mx1", [1, 128], F32)
        xnn_bf = sbuf("xnn_bf", [128, D], BF16)
        xnn_fm = sbuf("xnn_fm", [128, KC * R], BF16)
        w1b = sbuf("w1b", [128, KC * 3 * D], BF16)
        idb = sbuf("idb", [128, 128], BF16)
        idf = sbuf("idf", [128, 128], F32)
        phr = sbuf("phr", [128, D], F32)
        phe = sbuf("phe", [128, D], F32)
        phq_bf = sbuf("phq_bf", [128, D], BF16)
        phk_bf = sbuf("phk_bf", [128, D], BF16)
        v_bf = sbuf("v_bf", [128, NH * (HD + 1)], BF16)
        tstage = sbuf("tstage", [128, D], BF16)
        scst = sbuf("scst", [64, 2 * NH * 65], F32)
        ps_qkv = psum("ps_qkv", [128, D], F32)
        ps_t = psum("ps_t", [128, D], BF16)
        ps_scd0 = psum("ps_scd0", [64, 512], F32)
        ps_scd1 = psum("ps_scd1", [64, 512], F32)
        ps_scz = psum("ps_scz", [64, 16], F32)
        ps_mx = psum("ps_mx", [1, 128], F32)
        s_ld = sem("s_ld"); s_lncp = sem("s_lncp"); s_mm = sem("s_mm")
        s_ev = sem("s_ev"); s_st = sem("s_st"); s_aux = sem("s_aux")

        p = Prog(nc)
        p.sems = {"lncp": Sem(s_lncp)}
        ld = Sem(s_ld); mm = Sem(s_mm); ev = Sem(s_ev); st = Sem(s_st); aux = Sem(s_aux)
        stats = {"m": stat[:, 0:1], "var": stat[:, 1:2], "sd": stat[:, 2:3],
                 "rstd": stat[:, 3:4], "eps_col": stat[:, 4:5]}

        p.emit("vector", lambda e: e.memset(stat[:, 4:5], EPS), inc=p.sems["lncp"])
        for t in range(NT):
            p.dma("sync", xw[:, t * D:(t + 1) * D], xin[t * 128:(t + 1) * 128, :], ld)
        p.dma("sync", idb[:, :], identb, ld)
        p.dma("sync", idf[:, :], identf, ld)
        for c in range(KC):
            p.dma("gpsimd", w1b[:, c * 3 * D:(c + 1) * 3 * D], w1t[c * 128:(c + 1) * 128, :], ld)
        ld_all = ld.n

        p.wait("vector", ld, ld_all)
        p.wait("scalar", ld, ld_all)
        for t in range(NT):
            xt = xw[:, t * D:(t + 1) * D]
            _ln_tile(p, xt, sb, sa, stats, extra_absmax=pmx[:, t:t + 1])
            p.emit("vector", lambda e, o=xt, a=sb: e.tensor_copy(out=o, in_=a), inc=p.sems["lncp"])
            _ln_tile(p, xt, sb, sa, stats)
            p.emit("vector", lambda e, o=xnn_bf, a=sb: e.tensor_copy(out=o, in_=a), inc=p.sems["lncp"])
            cpv = p.sems["lncp"].n
            p.wait("sync", p.sems["lncp"], cpv)
            p.dma("sync", xn_o[t * 128:(t + 1) * 128, :], xt, st)
            p.wait("tensor", p.sems["lncp"], cpv)
            if t > 0:
                p.wait("tensor", ev)
            for c in range(KC):
                p.emit("tensor", lambda e, c=c: e.transpose(
                    ps_t[:, c * 128:(c + 1) * 128], xnn_bf[:, c * 128:(c + 1) * 128], idb[:, :]),
                    inc=(mm if c == KC - 1 else None))
            p.wait("scalar", mm)
            p.emit("scalar", lambda e, t=t: e.activation(
                out=xnn_fm.rearrange("p (c j) -> p c j", j=R)[:, :, t * 128:(t + 1) * 128],
                in_=ps_t.rearrange("p (c j) -> p c j", j=128), func=AF.Copy), inc=ev)

        for t in range(NT):
            b, ch = t // 2, t % 2
            for part in range(3):
                p.wait("tensor", ev)
                for c in range(KC):
                    lhs = xnn_fm.rearrange("p (c j) -> p c j", j=R)[:, c, t * 128:(t + 1) * 128]
                    for jn in range(2):
                        rhs = w1b[:, c * 3 * D + part * D + jn * 512: c * 3 * D + part * D + (jn + 1) * 512]
                        p.emit("tensor", lambda e, l=lhs, r=rhs, jn=jn, c=c: e.matmul(
                            ps_qkv[:, jn * 512:(jn + 1) * 512], l, r,
                            start=(c == 0), stop=(c == KC - 1)),
                            inc=(mm if c == KC - 1 and jn == 1 else None))
                p.wait("scalar", mm)
                p.wait("vector", mm)
                if part < 2:
                    p.emit("scalar", lambda e: e.activation(out=phr[:, :], in_=ps_qkv[:, :], func=AF.Relu), inc=ev)
                    p.wait("vector", ev)
                    p.emit("vector", lambda e: e.tensor_tensor(out=phe[:, :], in0=ps_qkv[:, :], in1=phr[:, :], op=ALU.subtract), inc=ev)
                    p.wait("scalar", ev)
                    p.emit("scalar", lambda e: e.activation(out=phe[:, :], in_=phe[:, :], func=AF.Exp), inc=ev)
                    p.wait("vector", ev)
                    dst = phq_bf if part == 0 else phk_bf
                    p.emit("vector", lambda e, d=dst: e.tensor_tensor(out=d[:, :], in0=phe[:, :], in1=phr[:, :], op=ALU.add), inc=ev)
                else:
                    p.emit("vector", lambda e: e.memset(
                        v_bf.rearrange("p (h e) -> p h e", e=HD + 1)[:, :, HD:HD + 1], 1.0))
                    p.emit("vector", lambda e: e.tensor_copy(
                        out=v_bf.rearrange("p (h e) -> p h e", e=HD + 1)[:, :, 0:HD],
                        in_=ps_qkv.rearrange("p (h d) -> p h d", d=HD)), inc=ev)
                p.wait("tensor", ev)

            for src, dst_fm in ((phq_bf, phiq_fm_o), (phk_bf, phik_fm_o)):
                p.wait("tensor", ev)
                for c in range(KC):
                    p.emit("tensor", lambda e, s=src, c=c: e.transpose(
                        ps_t[:, c * 128:(c + 1) * 128], s[:, c * 128:(c + 1) * 128], idb[:, :]),
                        inc=(mm if c == KC - 1 else None))
                p.wait("scalar", mm)
                p.wait("scalar", st)
                p.emit("scalar", lambda e: e.activation(out=tstage[:, :], in_=ps_t[:, :], func=AF.Copy), inc=ev)
                p.wait("sync", ev)
                p.dma("sync", dst_fm.rearrange("p (c j) -> p c j", j=R)[:, :, t * 128:(t + 1) * 128],
                      tstage.rearrange("p (c j) -> p c j", j=128), st)

            p.wait("tensor", ev)
            for hh in range(NH):
                bank = ps_scd0 if hh < 8 else ps_scd1
                col = (hh % 8) * 64
                lhs = phk_bf[:, hh * 64:(hh + 1) * 64]
                rhs = v_bf[:, hh * 65:hh * 65 + 64]
                p.emit("tensor", lambda e, bk=bank, co=col, l=lhs, r=rhs: e.matmul(
                    bk[:, co:co + 64], l, r, start=True, stop=True))
                rhs1 = v_bf[:, hh * 65 + 64:hh * 65 + 65]
                p.emit("tensor", lambda e, hh=hh, l=lhs, r=rhs1: e.matmul(
                    ps_scz[:, hh:hh + 1], l, r, start=True, stop=True),
                    inc=(mm if hh == NH - 1 else None))
            p.wait("vector", mm)
            scv = scst.rearrange("p (c x e) -> p c x e", c=2, e=65)
            p.emit("vector", lambda e, ch=ch, scv=scv: e.tensor_copy(
                out=scv[0:64, ch, 0:8, 0:64],
                in_=ps_scd0.rearrange("p (x d) -> p x d", d=64)))
            p.emit("vector", lambda e, ch=ch, scv=scv: e.tensor_copy(
                out=scv[0:64, ch, 8:16, 0:64],
                in_=ps_scd1.rearrange("p (x d) -> p x d", d=64)))
            p.emit("vector", lambda e, ch=ch, scv=scv: e.tensor_copy(
                out=scv[0:64, ch, 0:16, 64:65],
                in_=ps_scz.rearrange("p (x o) -> p x o", o=1)), inc=ev)
            p.wait("tensor", ev)
            if ch == 1:
                blk = NH * 65
                p.wait("sync", ev)
                p.dma("sync", sc0_o[0:64, b * blk:(b + 1) * blk], scst[0:64, 0:blk], st)
                p.wait("vector", st)
                p.emit("vector", lambda e, blk=blk: e.tensor_tensor(
                    out=scst[0:64, 0:blk], in0=scst[0:64, 0:blk],
                    in1=scst[0:64, blk:2 * blk], op=ALU.add), inc=ev)
                p.wait("sync", ev)
                p.dma("sync", ssum_o[0:64, b * blk:(b + 1) * blk], scst[0:64, 0:blk], st)

            p.wait("sync", ev)
            p.dma("sync", phik_tm_o[t * 128:(t + 1) * 128, :], phk_bf[:, :], st)
            p.dma("sync", v_tm_o[t * 128:(t + 1) * 128, :], v_bf[:, :], st)
            p.wait("scalar", st)
            p.wait("vector", st)

        p.emit("vector", lambda e: e.reduce_max(out=mxc[:, :], in_=pmx[:, :], axis=AX.X), inc=ev)
        p.wait("tensor", ev)
        p.emit("tensor", lambda e: e.transpose(ps_mx[0:1, 0:128], mxc[:, 0:1], idf[:, :]), inc=mm)
        p.wait("vector", mm)
        p.emit("vector", lambda e: e.tensor_copy(out=mx1[:, :], in_=ps_mx[0:1, :]))
        p.drain("vector")
        p.emit("vector", lambda e: e.reduce_max(out=stat[0:1, 5:6], in_=mx1[0:1, :], axis=AX.X), inc=ev)
        p.wait("sync", ev)
        p.dma("sync", maxp1_o, stat[0:1, 5:6], st)

        p.wait("gpsimd", st)
        p.build()
    return nc


def _prep_host(inputs):
    h = {}
    x = np.ascontiguousarray(inputs["L0_attn_x"], dtype=np.float32)
    h["x_slices"] = [np.ascontiguousarray(x[:, c * 256:(c + 1) * 256, :].reshape(R, D))
                     for c in range(NCORES)]
    h["w1t"] = np.ascontiguousarray(inputs["L0_attn_W1"].T)
    h["identb"] = np.eye(128).astype(ml_dtypes.bfloat16)
    h["identf"] = np.eye(128, dtype=np.float32)
    h["triu"] = np.triu(np.ones((128, 128), np.float32))
    h["w2t"] = np.ascontiguousarray(inputs["L0_attn_W2"].T)
    h["wup"] = np.ascontiguousarray(inputs["L0_ffn_w_up"])
    h["wgate"] = np.ascontiguousarray(inputs["L0_ffn_w_gate"])
    h["wupt"] = np.ascontiguousarray(inputs["L0_ffn_w_up"].T)
    h["wgatet"] = np.ascontiguousarray(inputs["L0_ffn_w_gate"].T)
    h["wdown"] = np.ascontiguousarray(inputs["L0_ffn_w_down"])
    perm = np.zeros(3072, np.int64)
    for j in range(3):
        for hh in range(16):
            for d in range(64):
                perm[j * 1024 + hh * 64 + d] = hh * 192 + j * 64 + d
    h["ipw_t"] = np.ascontiguousarray(inputs["L1_attn_in_proj_weight"][perm, :].T)
    h["ipb_row"] = np.ascontiguousarray(inputs["L1_attn_in_proj_bias"][perm][None, :])
    h["opw_t"] = np.ascontiguousarray(inputs["L1_attn_out_proj_weight"].T)
    h["opb_row"] = np.ascontiguousarray(inputs["L1_attn_out_proj_bias"][None, :])
    h["rt_t"] = np.ascontiguousarray(inputs["L1_ffn_router_w"].T)
    h["rb_row"] = np.ascontiguousarray(inputs["L1_ffn_router_bias"][None, :])
    h["gat"] = np.ascontiguousarray(inputs["L1_ffn_gate_w"].reshape(4096, 1024).T)
    h["upt"] = np.ascontiguousarray(inputs["L1_ffn_up_w"].reshape(4096, 1024).T)
    h["dat"] = np.ascontiguousarray(inputs["L1_ffn_down_w"].transpose(0, 2, 1).reshape(4096, 1024))
    h["sgt"] = np.ascontiguousarray(inputs["L1_ffn_sgate_w"].T)
    h["sut"] = np.ascontiguousarray(inputs["L1_ffn_sup_w"].T)
    h["sdt"] = np.ascontiguousarray(inputs["L1_ffn_sdown_w"].T)
    return h


def kernel(**inputs):
    h = _prep_host(inputs)
    cores = list(range(NCORES))

    nc1 = build_launch1()
    im1 = [{"xin": h["x_slices"][c], "w1t": h["w1t"], "identb": h["identb"],
            "identf": h["identf"]} for c in cores]
    r1 = run_bass_kernel_spmd(nc1, im1, core_ids=cores).results

    maxp1_all = np.stack([r1[c]["maxp1_o"][0] for c in cores])
    nc2 = build_launch2()
    im2 = []
    for c in cores:
        prev = np.zeros((7, 64, 128 * 65), np.float32)
        for j in range(c):
            prev[j] = r1[j]["ssum_o"]
        im2.append({
            "xin": h["x_slices"][c], "xn_i": r1[c]["xn_o"],
            "phiq_fm": r1[c]["phiq_fm_o"], "phik_fm": r1[c]["phik_fm_o"],
            "phik_tm": r1[c]["phik_tm_o"], "v_tm": r1[c]["v_tm_o"],
            "sc0_i": r1[c]["sc0_o"], "ssum_prev": prev, "maxp_all": maxp1_all,
            "w2t": h["w2t"], "wupt": h["wupt"], "wgatet": h["wgatet"],
            "wup": h["wup"], "wgate": h["wgate"],
            "identb": h["identb"], "identf": h["identf"], "triu": h["triu"],
        })
    r2 = run_bass_kernel_spmd(nc2, im2, core_ids=cores).results

    maxp2_all = np.stack([r2[c]["maxp2_o"][0] for c in cores])
    maxp3_all = np.stack([r2[c]["maxp3_o"][0] for c in cores])
    nc3 = build_launch3()
    im3 = [{"up_raw_i": r2[c]["up_raw_o"], "gate_raw_i": r2[c]["gate_raw_o"],
            "maxp2_all": maxp2_all, "maxp3_all": maxp3_all,
            "identf": h["identf"]} for c in cores]
    r3 = run_bass_kernel_spmd(nc3, im3, core_ids=cores).results

    maxp4_all = np.stack([r3[c]["maxp4_o"][0] for c in cores])
    nc4 = build_launch4()
    im4 = [{"t_i": r3[c]["t_o"], "upq_i": r3[c]["upq_o"],
            "maxp4_all": maxp4_all, "identf": h["identf"]} for c in cores]
    r4 = run_bass_kernel_spmd(nc4, im4, core_ids=cores).results

    maxp5_all = np.stack([r4[c]["maxp5_o"][0] for c in cores])
    nc5 = build_launch5()
    im5 = [{"oraw_i": r4[c]["oraw_o"], "maxp5_all": maxp5_all, "xa_i": r2[c]["xa_o"],
            "wdown": h["wdown"], "ipw_t": h["ipw_t"], "ipb_row": h["ipb_row"],
            "opw_t": h["opw_t"], "opb_row": h["opb_row"], "rt_t": h["rt_t"],
            "rb_row": h["rb_row"], "gat": h["gat"], "upt": h["upt"], "dat": h["dat"],
            "sgt": h["sgt"], "sut": h["sut"], "sdt": h["sdt"],
            "identb": h["identb"], "identf": h["identf"]} for c in cores]
    r5 = run_bass_kernel_spmd(nc5, im5, core_ids=cores).results

    out = np.zeros((8, 2048, 1024), np.float32)
    for c in cores:
        out[:, c * 256:(c + 1) * 256, :] = r5[c]["out_o"].reshape(8, 256, 1024)
    return out


def build_launch2(stop=99):
    nc = bass.Bass("TRN2", target_bir_lowering=False)
    ap = lambda n, sh, dt, kind: nc.dram_tensor(n, sh, dt, kind=kind).ap()
    xin = ap("xin", [R, D], F32, "ExternalInput")
    xn_i = ap("xn_i", [R, D], F32, "ExternalInput")
    phiq_fm = ap("phiq_fm", [128, KC * R], BF16, "ExternalInput")
    phik_fm = ap("phik_fm", [128, KC * R], BF16, "ExternalInput")
    phik_tm = ap("phik_tm", [R, D], BF16, "ExternalInput")
    v_tm = ap("v_tm", [R, NH * 65], BF16, "ExternalInput")
    sc0_i = ap("sc0_i", [64, 128 * 65], F32, "ExternalInput")
    ssum_prev = ap("ssum_prev", [7, 64, 128 * 65], F32, "ExternalInput")
    maxp_all = ap("maxp_all", [8, 1], F32, "ExternalInput")
    w2t = ap("w2t", [D, D], F32, "ExternalInput")
    wupt = ap("wupt", [D, 3 * D], F32, "ExternalInput")
    wgatet = ap("wgatet", [D, 3 * D], F32, "ExternalInput")
    wup = ap("wup", [3 * D, D], F32, "ExternalInput")
    wgate = ap("wgate", [3 * D, D], F32, "ExternalInput")
    identb = ap("identb", [128, 128], BF16, "ExternalInput")
    identf = ap("identf", [128, 128], F32, "ExternalInput")
    triu = ap("triu", [128, 128], F32, "ExternalInput")

    xa_o = ap("xa_o", [R, D], F32, "ExternalOutput")
    up_raw_o = ap("up_raw_o", [128, 24 * R], F32, "ExternalOutput")
    gate_raw_o = ap("gate_raw_o", [128, 24 * R], F32, "ExternalOutput")
    maxp2_o = ap("maxp2_o", [1, 1], F32, "ExternalOutput")
    maxp3_o = ap("maxp3_o", [1, 1], F32, "ExternalOutput")
    scr = nc.dram_tensor("scr", [1, 4], F32)

    BLK = NH * 65

    with ExitStack() as ctx:
        sbuf = lambda n, sh, dt: ctx.enter_context(nc.sbuf_tensor(n, sh, dt))[:, :]
        psum = lambda n, sh, dt: ctx.enter_context(nc.psum_tensor(n, sh, dt))[:, :]
        sem = lambda n: ctx.enter_context(nc.semaphore(n))
        sinf = sbuf("sinf", [64, BLK], F32)
        smidf = sbuf("smidf", [64, BLK], F32)
        stg = sbuf("stg", [64, BLK], F32)
        sin_all = sbuf("sin_all", [128, 8 * 8 * 65], BF16)
        smid_all = sbuf("smid_all", [128, 8 * 8 * 65], BF16)
        pq_b = sbuf("pq_b", [128, 2 * KC * 256], BF16)
        pk_b = sbuf("pk_b", [128, 2 * KC * 256], BF16)
        v_b = sbuf("v_b", [128, 2 * 2 * BLK], BF16)
        att_bf = sbuf("att_bf", [128, 2 * 512], BF16)
        rec4 = sbuf("rec4", [128, 8], F32)
        ctxt = sbuf("ctxt", [128, D], F32)
        ctx_bf = sbuf("ctx_bf", [128, D], BF16)
        ctx_fm = sbuf("ctx_fm", [128, NT * D], BF16)
        w2b = sbuf("w2b", [128, KC * D], BF16)
        w2o = sbuf("w2o", [128, 512], F32)
        xa_st = sbuf("xa_st", [128, 4 * D], F32)
        xnt = sbuf("xnt", [128, 2 * D], F32)
        xq_i32 = sbuf("xq_i32", [128, D], I32)
        dm_s = sbuf("dm_s", [128, D], F32)
        rr_s = sbuf("rr_s", [128, D], F32)
        xq_bf = sbuf("xq_bf", [128, D], BF16)
        xq_fm = sbuf("xq_fm", [128, KC * R], BF16)
        wtile = sbuf("wtile", [128, 2 * D], F32)
        bmx = sbuf("bmx", [128, 2 * 32], F32)
        sblk = sbuf("sblk", [128, 2 * 32], F32)
        rsblk = sbuf("rsblk", [128, 2 * 32], F32)
        yw = sbuf("yw", [128, 2 * D], F32)
        runm = sbuf("runm", [128, 2], F32)
        cols = sbuf("cols", [128, 4], F32)
        m1row = sbuf("m1row", [1, 8], F32)
        m1sc = sbuf("m1sc", [1, 8], F32)
        mx1 = sbuf("mx1", [1, 128], F32)
        idb = sbuf("idb", [128, 128], BF16)
        idf = sbuf("idf", [128, 128], F32)
        trm = sbuf("trm", [128, 128], F32)
        pb0 = psum("pb0", [128, 512], F32)
        pb1 = psum("pb1", [128, 512], F32)
        pb2 = psum("pb2", [128, 512], F32)
        pb3 = psum("pb3", [128, 512], F32)
        ps_t = psum("ps_t", [128, D], BF16)
        ps_tf = psum("ps_tf", [128, 512], F32)
        ps_mx = psum("ps_mx", [1, 128], F32)
        s_ld = sem("s_ld"); s_mm = sem("s_mm"); s_ev = sem("s_ev")
        s_st = sem("s_st"); s_aux = sem("s_aux"); s_cp = sem("s_cp")
        slot_sems = [sem(f"s2_sl{i}") for i in range(6)]

        p = Prog(nc)
        ld = Sem(s_ld); mm = Sem(s_mm); ev = Sem(s_ev); st = Sem(s_st)
        aux = Sem(s_aux); cp = Sem(s_cp)
        sA = [Sem(slot_sems[0]), Sem(slot_sems[1])]
        sW = [Sem(slot_sems[2]), Sem(slot_sems[3])]
        sX = [Sem(slot_sems[4]), Sem(slot_sems[5])]

        p.dma("sync", idb, identb, ld)
        p.dma("sync", idf, identf, ld)
        p.dma("sync", trm, triu, ld)
        p.dma("sync", m1row, maxp_all.rearrange("a b -> b a"), ld)
        ld0 = ld.n

        p.wait("vector", ld, ld0)
        p.emit("vector", lambda e: e.reduce_max(out=m1sc[0:1, 0:1], in_=m1row[0:1, :], axis=AX.X))
        p.drain("vector")
        p.emit("vector", lambda e: e.tensor_scalar(out=m1sc[0:1, 1:2], in0=m1sc[0:1, 0:1],
                                                   scalar1=1.0 / 32767.0, scalar2=1e-12, op0=ALU.mult, op1=ALU.max))
        p.drain("vector")
        p.emit("vector", lambda e: e.reciprocal(out=m1sc[0:1, 2:3], in_=m1sc[0:1, 1:2]), inc=cp)
        p.wait("sync", cp)
        p.dma("sync", scr.ap()[0:1, 0:2], m1sc[0:1, 1:3], aux)
        p.wait("gpsimd", aux)
        p.dma("gpsimd", cols[:, 0:1], scr.ap()[0:1, 0:1].broadcast_to([128, 1]), ld)
        p.dma("gpsimd", cols[:, 1:2], scr.ap()[0:1, 1:2].broadcast_to([128, 1]), ld)
        p.emit("vector", lambda e: e.memset(cols[:, 2:3], 32768.0))
        p.emit("vector", lambda e: e.memset(runm, 0.0), inc=cp)

        if stop <= 0:
            p.wait("gpsimd", st); p.wait("gpsimd", ld); p.wait("gpsimd", cp)
            p.build()
            return nc
        for b in range(8):
            p.wait("vector", ld)
            p.emit("vector", lambda e: e.memset(sinf, 0.0), inc=cp)
            for j in range(7):
                p.wait("sync", cp)
                p.dma("sync", stg, ssum_prev[j, 0:64, b * BLK:(b + 1) * BLK], ld)
                p.wait("vector", ld)
                p.emit("vector", lambda e: e.tensor_tensor(out=sinf, in0=sinf, in1=stg, op=ALU.add), inc=cp)
            p.wait("sync", cp)
            p.dma("sync", stg, sc0_i[0:64, b * BLK:(b + 1) * BLK], ld)
            p.wait("vector", ld)
            p.emit("vector", lambda e: e.tensor_tensor(out=smidf, in0=sinf, in1=stg, op=ALU.add), inc=cp)
            p.wait("gpsimd", cp)
            for src, dst in ((sinf, sin_all), (smidf, smid_all)):
                sv = src.rearrange("p (h2 two e) -> p h2 two e", two=2, e=65)
                dv = dst.rearrange("p (b h2 e) -> p b h2 e", b=8, e=65)
                p.dma("gpsimd", dv[0:64, b, :, :], sv[0:64, :, 0, :], ld)
                p.dma("gpsimd", dv[64:128, b, :, :], sv[0:64, :, 1, :], ld)
        states_ld = ld.n
        states_cp = cp.n

        if stop <= 1:
            p.wait("gpsimd", st); p.wait("gpsimd", ld); p.wait("gpsimd", cp)
            p.build()
            return nc
        p.wait("scalar", ld, ld.n)
        p.wait("scalar", cp, states_cp)
        xn_free = [0, 0]
        for t in range(NT):
            sl = t % 2
            p.wait("sync", cp, at=xn_free[sl])
            p.dma("sync", xnt[:, sl * D:(sl + 1) * D], xn_i[t * 128:(t + 1) * 128, :], sX[sl])
            p.wait("scalar", sX[sl])
            xsl = xnt[:, sl * D:(sl + 1) * D]
            p.emit("scalar", lambda e, xsl=xsl: e.activation(out=xq_i32, in_=xsl, func=AF.Identity,
                                                             bias=cols[:, 2:3], scale=cols[:, 1:2]), inc=cp)
            p.wait("vector", cp, at=cp.n)
            p.wait("vector", sX[sl])
            p.emit("vector", lambda e, xsl=xsl: e.scalar_tensor_tensor(out=dm_s, in0=xsl, scalar=cols[:, 1:2],
                                                                       in1=xq_i32, op0=ALU.mult, op1=ALU.subtract))
            p.drain("vector")
            p.emit("vector", lambda e: e.scalar_tensor_tensor(out=rr_s, in0=dm_s, scalar=-32767.5,
                                                              in1=xq_i32, op0=ALU.is_ge, op1=ALU.add))
            p.drain("vector")
            p.emit("vector", lambda e: e.tensor_scalar(out=xq_bf, in0=rr_s, scalar1=32768.0,
                                                       scalar2=cols[:, 0:1], op0=ALU.subtract, op1=ALU.mult), inc=cp)
            xn_free[sl] = cp.n
            p.wait("tensor", cp, at=cp.n)
            p.wait("tensor", ev)
            for c in range(KC):
                p.emit("tensor", lambda e, c=c: e.transpose(
                    ps_t[:, c * 128:(c + 1) * 128], xq_bf[:, c * 128:(c + 1) * 128], idb),
                    inc=(mm if c == KC - 1 else None))
            p.wait("scalar", mm, at=mm.n)
            p.emit("scalar", lambda e, t=t: e.activation(
                out=xq_fm.rearrange("p (c j) -> p c j", j=R)[:, :, t * 128:(t + 1) * 128],
                in_=ps_t.rearrange("p (c j) -> p c j", j=128), func=AF.Copy), inc=ev)
            p.wait("vector", ev)

        if stop <= 2:
            p.wait("gpsimd", st); p.wait("gpsimd", ld); p.wait("gpsimd", cp); p.wait("gpsimd", ev)
            p.build()
            return nc
        p.wait("vector", ld, states_ld)
        batch_free = [0, 0]
        att_free = [0, 0]
        y_free = [0, 0]
        for b in range(8):
            sl = b % 2
            p.wait("gpsimd", mm, at=batch_free[sl])
            p.dma("gpsimd", pq_b.rearrange("p (s c j) -> p s c j", s=2, j=256)[:, sl, :, :],
                  phiq_fm.rearrange("p (c j) -> p c j", j=R)[:, :, b * 256:(b + 1) * 256], sA[sl])
            p.dma("gpsimd", pk_b.rearrange("p (s c j) -> p s c j", s=2, j=256)[:, sl, :, :],
                  phik_fm.rearrange("p (c j) -> p c j", j=R)[:, :, b * 256:(b + 1) * 256], sA[sl])
            p.dma("gpsimd", v_b.rearrange("p (s c j) -> p s c j", s=2, j=BLK)[:, sl, :, :],
                  v_tm[b * 256:(b + 1) * 256, :].rearrange("(c p) j -> p c j", p=128), sA[sl])
            p.wait("tensor", sA[sl])
            if stop == 28:
                continue
            pqs = pq_b[:, sl * KC * 256:(sl + 1) * KC * 256]
            pks = pk_b[:, sl * KC * 256:(sl + 1) * KC * 256]
            vbs = v_b[:, sl * 2 * BLK:(sl + 1) * 2 * BLK]
            for chn in range(2):
                for g in range(4):
                    gp = g % 2
                    par = g // 2
                    base4 = (g % 2) * 4
                    heads = [2 * (base4 + i) + par for i in range(4)]
                    patt = pb0 if gp == 0 else pb1
                    py = pb2 if gp == 0 else pb3
                    absl = att_bf[:, gp * 512:(gp + 1) * 512]
                    p.wait("tensor", ev, at=att_free[gp])
                    for hi in range(4):
                        hh = heads[hi]
                        cc, po = hh // 2, (hh % 2) * 64
                        lk = pks[po:po + 64, cc * 256 + chn * 128: cc * 256 + (chn + 1) * 128]
                        lq = pqs[po:po + 64, cc * 256 + chn * 128: cc * 256 + (chn + 1) * 128]
                        p.emit("tensor", lambda e, patt=patt, hi=hi, lk=lk, lq=lq: e.matmul(
                            patt[:, hi * 128:(hi + 1) * 128], lk, lq, start=True, stop=True),
                            inc=(mm if hi == 3 else None))
                    p.wait("vector", mm, at=mm.n)
                    if stop == 29:
                        continue
                    p.emit("vector", lambda e, patt=patt, absl=absl: e.tensor_tensor(
                        out=absl.rearrange("p (x j) -> p x j", j=128),
                        in0=patt.rearrange("p (x j) -> p x j", j=128),
                        in1=trm.rearrange("p (o j) -> p o j", o=1).broadcast_to([128, 4, 128]),
                        op=ALU.mult), inc=ev)
                    att_free[gp] = ev.n
                    if stop == 30:
                        continue
                    p.wait("tensor", ev, at=ev.n)
                    p.wait("tensor", ev, at=y_free[gp])
                    sall = sin_all if chn == 0 else smid_all
                    for hi in range(4):
                        hh = heads[hi]
                        cc, po = hh // 2, (hh % 2) * 64
                        lq = pqs[po:po + 64, cc * 256 + chn * 128: cc * 256 + (chn + 1) * 128]
                        p.emit("tensor", lambda e, py=py, hi=hi, absl=absl, vbs=vbs, chn=chn, hh=hh: e.matmul(
                            py[:, hi * 65:hi * 65 + 65], absl[:, hi * 128:(hi + 1) * 128],
                            vbs[:, chn * BLK + hh * 65: chn * BLK + (hh + 1) * 65],
                            start=True, stop=False))
                        p.emit("tensor", lambda e, py=py, hi=hi, lq=lq, sall=sall, b=b, hh=hh, po=po: e.matmul(
                            py[:, hi * 65:hi * 65 + 65], lq,
                            sall[po:po + 64, (b * 8 + hh // 2) * 65:(b * 8 + hh // 2 + 1) * 65],
                            start=False, stop=True), inc=(mm if hi == 3 else None))
                    p.wait("vector", mm, at=mm.n)
                    if stop == 31:
                        continue
                    rsl = rec4[:, gp * 4:(gp + 1) * 4]
                    p.emit("vector", lambda e, py=py, rsl=rsl: e.reciprocal(
                        out=rsl, in_=py[:, 0:260].rearrange("p (x e) -> p x e", e=65)[:, :, 64:65]))
                    p.drain("vector")
                    p.emit("vector", lambda e, py=py, par=par, base4=base4, rsl=rsl: e.tensor_tensor(
                        out=ctxt.rearrange("p (h2 par d) -> p h2 par d", par=2, d=64)[:, base4:base4 + 4, par, :],
                        in0=py[:, 0:260].rearrange("p (x e) -> p x e", e=65)[:, :, 0:64],
                        in1=rsl.rearrange("p (x o) -> p x o", o=1).broadcast_to([128, 4, 64]),
                        op=ALU.mult), inc=ev)
                    y_free[gp] = ev.n
                if stop in (30, 31, 32):
                    continue
                p.wait("scalar", ev)
                p.emit("scalar", lambda e: e.activation(out=ctx_bf, in_=ctxt, func=AF.Copy), inc=cp)
                p.wait("tensor", cp, at=cp.n)
                p.wait("tensor", ev)
                for c in range(KC):
                    p.emit("tensor", lambda e, c=c: e.transpose(
                        ps_t[:, c * 128:(c + 1) * 128], ctx_bf[:, c * 128:(c + 1) * 128], idb),
                        inc=(mm if c == KC - 1 else None))
                p.wait("scalar", mm, at=mm.n)
                t_idx = b * 2 + chn
                p.emit("scalar", lambda e, t_idx=t_idx: e.activation(
                    out=ctx_fm.rearrange("p (c j) -> p c j", j=R)[:, :, t_idx * 128:(t_idx + 1) * 128],
                    in_=ps_t.rearrange("p (c j) -> p c j", j=128), func=AF.Copy), inc=ev)
                p.wait("vector", ev)
            batch_free[sl] = mm.n

        if stop <= 3:
            p.wait("gpsimd", st); p.wait("gpsimd", ld); p.wait("gpsimd", cp); p.wait("gpsimd", ev); p.wait("gpsimd", mm)
            p.build()
            return nc
        if stop in (28, 29, 30, 31, 32, 33):
            p.wait("gpsimd", st); p.wait("gpsimd", ld); p.wait("gpsimd", cp); p.wait("gpsimd", ev); p.wait("gpsimd", mm)
            p.build()
            return nc
        for c in range(KC):
            p.dma("gpsimd", w2b[:, c * D:(c + 1) * D], w2t[c * 128:(c + 1) * 128, :], ld)
        w2_ld = ld.n
        p.wait("tensor", ld, w2_ld)
        p.wait("tensor", ev)
        for tokc in range(4):
            p.wait("sync", ev if tokc == 0 else st)
            for tb in range(4):
                p.dma("sync", xa_st[:, tb * D:(tb + 1) * D],
                      xin[(tokc * 4 + tb) * 128:(tokc * 4 + tb + 1) * 128, :], ld)
            xa_ld = ld.n
            p.wait("vector", ld, xa_ld)
            for oc in range(KC):
                pp = pb0 if oc % 2 == 0 else pb1
                for c in range(KC):
                    p.emit("tensor", lambda e, pp=pp, oc=oc, c=c, tokc=tokc: e.matmul(
                        pp, w2b[:, c * D + oc * 128: c * D + (oc + 1) * 128],
                        ctx_fm[:, c * R + tokc * 512: c * R + (tokc + 1) * 512],
                        start=(c == 0), stop=(c == KC - 1)), inc=(mm if c == KC - 1 else None))
                p.wait("scalar", mm, at=mm.n)
                p.wait("scalar", ev)
                p.emit("scalar", lambda e, pp=pp: e.activation(out=w2o, in_=pp, func=AF.Copy), inc=ev)
                p.wait("tensor", ev, at=ev.n)
                for tb in range(4):
                    p.emit("tensor", lambda e, tb=tb: e.transpose(
                        ps_tf[:, tb * 128:(tb + 1) * 128], w2o[:, tb * 128:(tb + 1) * 128], idf),
                        inc=(mm if tb == 3 else None))
                p.wait("vector", mm, at=mm.n)
                p.emit("vector", lambda e, oc=oc: e.tensor_tensor(
                    out=xa_st.rearrange("p (t j) -> p t j", j=D)[:, :, oc * 128:(oc + 1) * 128],
                    in0=xa_st.rearrange("p (t j) -> p t j", j=D)[:, :, oc * 128:(oc + 1) * 128],
                    in1=ps_tf.rearrange("p (t j) -> p t j", j=128), op=ALU.add), inc=ev)
                p.wait("tensor", ev)
            p.wait("sync", ev)
            for tb in range(4):
                p.dma("sync", xa_o[(tokc * 4 + tb) * 128:(tokc * 4 + tb + 1) * 128, :],
                      xa_st[:, tb * D:(tb + 1) * D], st)

        if stop <= 4:
            p.wait("gpsimd", st); p.wait("gpsimd", ld); p.wait("gpsimd", cp); p.wait("gpsimd", ev); p.wait("gpsimd", mm)
            p.build()
            return nc
        iw0 = dm_s.bitcast(I32); iw1 = rr_s.bitcast(I32)
        dw = xnt
        wq_bf = ctxt.bitcast(BF16)
        wlhs = xa_st[:, 0:1024].bitcast(BF16)
        upst = xa_st[:, 2048:3072]
        p.emit("vector", lambda e: e.memset(m1sc[0:1, 4:5], 16.0))
        p.emit("vector", lambda e: e.memset(cols[:, 3:4], 16.0), inc=cp)
        p.wait("scalar", cp)
        w_free = [0, 0]
        lhs_free = [0, 0]
        pp_ev = [0, 0]
        for wi, (wsrc, dst) in enumerate(((wup, up_raw_o), (wgate, gate_raw_o))):
            for oci in range(24):
                oc = oci
                sl = (wi * 24 + oci) % 2
                wts = wtile[:, sl * D:(sl + 1) * D]
                yws = yw[:, sl * D:(sl + 1) * D]
                iws = iw0 if sl == 0 else iw1
                dws = dw[:, sl * D:(sl + 1) * D]
                wqs = wq_bf[:, sl * D:(sl + 1) * D]
                wls = wlhs[:, sl * D:(sl + 1) * D]
                bms = bmx[:, sl * 32:(sl + 1) * 32]
                sbs = sblk[:, sl * 32:(sl + 1) * 32]
                rss = rsblk[:, sl * 32:(sl + 1) * 32]
                p.wait("sync", cp, at=w_free[sl])
                p.dma("sync", wts, wsrc[oc * 128:(oc + 1) * 128, :], sW[sl])
                p.wait("vector", sW[sl])
                p.emit("vector", lambda e, wts=wts, bms=bms: e.tensor_reduce(
                    out=bms, in_=wts.rearrange("p (g x) -> p g x", x=32), axis=AX.X,
                    op=ALU.max, apply_absolute_value=True))
                p.drain("vector")
                p.emit("vector", lambda e, bms=bms, sbs=sbs: e.tensor_scalar(
                    out=sbs, in0=bms, scalar1=1.0 / 7.0, scalar2=1e-12, op0=ALU.mult, op1=ALU.max))
                p.drain("vector")
                p.emit("vector", lambda e, sbs=sbs, rss=rss: e.reciprocal(out=rss, in_=sbs))
                p.drain("vector")
                p.emit("vector", lambda e, wts=wts, yws=yws, rss=rss: e.tensor_tensor(
                    out=yws.rearrange("p (g x) -> p g x", x=32), in0=wts.rearrange("p (g x) -> p g x", x=32),
                    in1=rss.rearrange("p (g o) -> p g o", o=1).broadcast_to([128, 32, 32]), op=ALU.mult), inc=cp)
                w_free[sl] = cp.n
                p.wait("scalar", cp, at=cp.n)
                p.emit("scalar", lambda e, yws=yws, iws=iws: e.activation(
                    out=iws, in_=yws, func=AF.Identity, bias=cols[:, 3:4], scale=1.0), inc=cp)
                p.wait("vector", cp, at=cp.n)
                p.emit("vector", lambda e, yws=yws, iws=iws, dws=dws: e.scalar_tensor_tensor(
                    out=dws, in0=yws, scalar=16.0, in1=iws, op0=ALU.add, op1=ALU.subtract))
                p.drain("vector")
                p.emit("vector", lambda e, dws=dws, iws=iws: e.scalar_tensor_tensor(
                    out=dws, in0=dws, scalar=0.5, in1=iws, op0=ALU.is_ge, op1=ALU.add))
                p.drain("vector")
                p.emit("vector", lambda e, dws=dws: e.tensor_scalar(
                    out=dws, in0=dws, scalar1=16.0, scalar2=None, op0=ALU.subtract))
                p.drain("vector")
                p.emit("vector", lambda e, dws=dws, wqs=wqs, sbs=sbs: e.tensor_tensor(
                    out=wqs.rearrange("p (g x) -> p g x", x=32), in0=dws.rearrange("p (g x) -> p g x", x=32),
                    in1=sbs.rearrange("p (g o) -> p g o", o=1).broadcast_to([128, 32, 32]), op=ALU.mult), inc=cp)
                p.wait("tensor", cp, at=cp.n)
                p.wait("tensor", mm, at=lhs_free[sl])
                p.wait("tensor", ev)
                for c in range(KC):
                    p.emit("tensor", lambda e, c=c, wqs=wqs: e.transpose(
                        ps_t[:, c * 128:(c + 1) * 128], wqs[:, c * 128:(c + 1) * 128], idb),
                        inc=(mm if c == KC - 1 else None))
                p.wait("scalar", mm, at=mm.n)
                p.emit("scalar", lambda e, wls=wls: e.activation(out=wls, in_=ps_t, func=AF.Copy), inc=ev)
                p.wait("tensor", ev, at=ev.n)
                for tokc in range(4):
                    pp = pb0 if tokc % 2 == 0 else pb1
                    usl = upst[:, (tokc % 2) * 512:(tokc % 2 + 1) * 512]
                    p.wait("tensor", ev, at=pp_ev[tokc % 2])
                    for c in range(KC):
                        p.emit("tensor", lambda e, pp=pp, c=c, tokc=tokc, wls=wls: e.matmul(
                            pp, wls[:, c * 128:(c + 1) * 128],
                            xq_fm[:, c * R + tokc * 512: c * R + (tokc + 1) * 512],
                            start=(c == 0), stop=(c == KC - 1)), inc=(mm if c == KC - 1 else None))
                    if tokc == 3:
                        lhs_free[sl] = mm.n
                    p.wait("scalar", mm, at=mm.n)
                    p.wait("scalar", st)
                    p.wait("scalar", cp)
                    p.emit("scalar", lambda e, usl=usl, pp=pp: e.activation(out=usl, in_=pp, func=AF.Copy), inc=ev)
                    pp_ev[tokc % 2] = ev.n
                    p.wait("sync", ev, at=ev.n)
                    p.dma("sync", dst.rearrange("p (o j) -> p o j", j=R)[:, oc, tokc * 512:(tokc + 1) * 512], usl, st)
                    p.wait("vector", ev, at=ev.n)
                    p.emit("vector", lambda e, usl=usl: e.tensor_reduce(
                        out=rec4[:, 0:1], in_=usl, axis=AX.X, op=ALU.max, apply_absolute_value=True))
                    p.drain("vector")
                    p.emit("vector", lambda e, wi=wi: e.tensor_tensor(
                        out=runm[:, wi:wi + 1], in0=runm[:, wi:wi + 1], in1=rec4[:, 0:1], op=ALU.max), inc=cp)
        for wi, dst in ((0, maxp2_o), (1, maxp3_o)):
            p.wait("tensor", cp)
            if wi == 1:
                p.wait("tensor", ev)
            p.emit("tensor", lambda e, wi=wi: e.transpose(ps_mx[0:1, 0:128], runm[:, wi:wi + 1], idf), inc=mm)
            p.wait("vector", mm, at=mm.n)
            p.emit("vector", lambda e: e.tensor_copy(out=mx1, in_=ps_mx[0:1, :]), inc=ev)
            p.drain("vector")
            p.emit("vector", lambda e: e.reduce_max(out=m1sc[0:1, 3:4], in_=mx1[0:1, :], axis=AX.X), inc=ev)
            p.wait("sync", ev)
            p.dma("sync", dst, m1sc[0:1, 3:4], st)
            p.wait("vector", st)

        p.wait("gpsimd", st)
        p.build()
    return nc


def _phase_barrier(p, sems, dum_v, dum_s, dum_g, bar):
    p.emit("vector", lambda e, d=dum_v: e.memset(d, 0.0), inc=bar)
    p.emit("scalar", lambda e, d=dum_s: e.activation(out=d, in_=d, func=AF.Copy), inc=bar)
    p.emit("gpsimd", lambda e, d=dum_g: e.memset(d, 0.0), inc=bar)
    tgt = bar.n
    for ek in ("sync", "scalar", "vector", "tensor", "gpsimd"):
        p.wait(ek, bar, at=tgt)
        for sm in sems:
            p.wait(ek, sm)


def _emit_scales(p, nc_scr, mrow, msc, cols_s, cols_rs, ld, cp, aux, idx):
    p.emit("vector", lambda e: e.reduce_max(out=msc[0:1, 0:1], in_=mrow[0:1, :], axis=AX.X))
    p.drain("vector")
    p.emit("vector", lambda e: e.tensor_scalar(out=msc[0:1, 1:2], in0=msc[0:1, 0:1],
                                               scalar1=1.0 / 32767.0, scalar2=1e-12, op0=ALU.mult, op1=ALU.max))
    p.drain("vector")
    p.emit("vector", lambda e: e.reciprocal(out=msc[0:1, 2:3], in_=msc[0:1, 1:2]), inc=cp)
    p.wait("sync", cp)
    p.dma("sync", nc_scr.ap()[0:1, idx:idx + 2], msc[0:1, 1:3], aux)
    p.wait("gpsimd", aux)
    p.dma("gpsimd", cols_s, nc_scr.ap()[0:1, idx:idx + 1].broadcast_to([128, 1]), ld)
    p.dma("gpsimd", cols_rs, nc_scr.ap()[0:1, idx + 1:idx + 2].broadcast_to([128, 1]), ld)


def _emit_qdq(p, src, dst, i32b, dmb, s_col, rs_col, b_col, cp):
    p.wait("scalar", cp)
    p.emit("scalar", lambda e: e.activation(out=i32b, in_=src, func=AF.Identity,
                                            bias=b_col, scale=rs_col), inc=cp)
    p.wait("vector", cp)
    p.emit("vector", lambda e: e.scalar_tensor_tensor(out=dmb, in0=src, scalar=rs_col,
                                                      in1=i32b, op0=ALU.mult, op1=ALU.subtract))
    p.drain("vector")
    p.emit("vector", lambda e: e.scalar_tensor_tensor(out=dmb, in0=dmb, scalar=-32767.5,
                                                      in1=i32b, op0=ALU.is_ge, op1=ALU.add))
    p.drain("vector")
    p.emit("vector", lambda e: e.tensor_scalar(out=dst, in0=dmb, scalar1=32768.0,
                                               scalar2=s_col, op0=ALU.subtract, op1=ALU.mult), inc=cp)


def _emit_colmax_out(p, runcol, mx1, msc_sl, ps_mx, idf, dst, mm, ev, st):
    p.emit("tensor", lambda e: e.transpose(ps_mx[0:1, 0:128], runcol, idf), inc=mm)
    p.wait("vector", mm)
    p.emit("vector", lambda e: e.tensor_copy(out=mx1, in_=ps_mx[0:1, :]), inc=ev)
    p.drain("vector")
    p.emit("vector", lambda e: e.reduce_max(out=msc_sl, in_=mx1[0:1, :], axis=AX.X), inc=ev)
    p.wait("sync", ev)
    p.dma("sync", dst, msc_sl, st)
    p.wait("vector", st)


def build_launch3():
    nc = bass.Bass("TRN2", target_bir_lowering=False)
    ap = lambda n, sh, dt, kind: nc.dram_tensor(n, sh, dt, kind=kind).ap()
    up_raw_i = ap("up_raw_i", [128, 24 * R], F32, "ExternalInput")
    gate_raw_i = ap("gate_raw_i", [128, 24 * R], F32, "ExternalInput")
    maxp2_all = ap("maxp2_all", [8, 1], F32, "ExternalInput")
    maxp3_all = ap("maxp3_all", [8, 1], F32, "ExternalInput")
    identf = ap("identf", [128, 128], F32, "ExternalInput")
    upq_o = ap("upq_o", [128, 24 * R], F32, "ExternalOutput")
    t_o = ap("t_o", [128, 24 * R], F32, "ExternalOutput")
    maxp4_o = ap("maxp4_o", [1, 1], F32, "ExternalOutput")
    scr = nc.dram_tensor("scr", [1, 8], F32)

    with ExitStack() as ctx:
        sbuf = lambda n, sh, dt: ctx.enter_context(nc.sbuf_tensor(n, sh, dt))[:, :]
        psum = lambda n, sh, dt: ctx.enter_context(nc.psum_tensor(n, sh, dt))[:, :]
        sem = lambda n: ctx.enter_context(nc.semaphore(n))
        upc = sbuf("upc", [128, R], F32)
        gac = sbuf("gac", [128, R], F32)
        i32b = sbuf("i32b", [128, R], I32)
        dmb = sbuf("dmb", [128, R], F32)
        gq = sbuf("gq", [128, R], F32)
        sg = sbuf("sg", [128, R], F32)
        outb = sbuf("outb", [128, R], F32)
        m2row = sbuf("m2row", [1, 8], F32)
        m3row = sbuf("m3row", [1, 8], F32)
        msc = sbuf("msc", [1, 8], F32)
        cols = sbuf("cols", [128, 8], F32)
        runm = sbuf("runm", [128, 1], F32)
        mx1 = sbuf("mx1", [1, 128], F32)
        idf = sbuf("idf", [128, 128], F32)
        ps_mx = psum("ps_mx", [1, 128], F32)
        s_ld = sem("s_ld"); s_cp = sem("s_cp"); s_ev = sem("s_ev")
        s_st = sem("s_st"); s_aux = sem("s_aux"); s_mm = sem("s_mm")
        p = Prog(nc)
        ld = Sem(s_ld); cp = Sem(s_cp); ev = Sem(s_ev); st = Sem(s_st)
        aux = Sem(s_aux); mm = Sem(s_mm)

        p.dma("sync", m2row, maxp2_all.rearrange("a b -> b a"), ld)
        p.dma("sync", m3row, maxp3_all.rearrange("a b -> b a"), ld)
        p.dma("sync", idf, identf, ld)
        p.emit("vector", lambda e: e.memset(cols[:, 4:5], 32768.0))
        p.emit("vector", lambda e: e.memset(runm, 0.0), inc=cp)
        p.wait("vector", ld)
        _emit_scales(p, scr, m2row, msc, cols[:, 0:1], cols[:, 1:2], ld, cp, aux, 0)
        _emit_scales(p, scr, m3row, msc, cols[:, 2:3], cols[:, 3:4], ld, cp, aux, 2)
        p.wait("scalar", ld)
        p.wait("scalar", cp)
        p.wait("vector", ld)
        for oc in range(24):
            sl = lambda tsr: tsr
            p.wait("sync", cp)
            p.wait("sync", st)
            p.dma("sync", upc, up_raw_i[:, oc * R:(oc + 1) * R], ld)
            p.dma("sync", gac, gate_raw_i[:, oc * R:(oc + 1) * R], ld)
            p.wait("scalar", ld)
            p.wait("vector", ld)
            _emit_qdq(p, upc, outb, i32b, dmb, cols[:, 0:1], cols[:, 1:2], cols[:, 4:5], cp)
            p.wait("sync", cp)
            p.dma("sync", upq_o[:, oc * R:(oc + 1) * R], outb, st)
            _emit_qdq(p, gac, gq, i32b, dmb, cols[:, 2:3], cols[:, 3:4], cols[:, 4:5], cp)
            p.wait("scalar", cp)
            p.emit("scalar", lambda e: e.activation(out=sg, in_=gq, func=AF.Sigmoid), inc=cp)
            p.emit("scalar", lambda e: e.activation(out=i32b, in_=sg, func=AF.Identity, scale=65536.0), inc=cp)
            p.wait("vector", cp)
            p.emit("vector", lambda e: e.scalar_tensor_tensor(out=dmb, in0=sg, scalar=65536.0,
                                                              in1=i32b, op0=ALU.mult, op1=ALU.subtract))
            p.drain("vector")
            p.emit("vector", lambda e: e.scalar_tensor_tensor(out=dmb, in0=dmb, scalar=0.5,
                                                              in1=i32b, op0=ALU.is_ge, op1=ALU.add))
            p.drain("vector")
            p.emit("vector", lambda e: e.tensor_scalar(out=sg, in0=dmb, scalar1=65535.0,
                                                       scalar2=1.0 / 65536.0, op0=ALU.min, op1=ALU.mult))
            p.drain("vector")
            p.wait("vector", st)
            p.emit("vector", lambda e: e.tensor_tensor(out=outb, in0=gq, in1=sg, op=ALU.mult), inc=cp)
            p.wait("sync", cp)
            p.dma("sync", t_o[:, oc * R:(oc + 1) * R], outb, st)
            p.emit("vector", lambda e: e.tensor_reduce(out=dmb[:, 0:1], in_=outb, axis=AX.X,
                                                       op=ALU.max, apply_absolute_value=True))
            p.drain("vector")
            p.emit("vector", lambda e: e.tensor_tensor(out=runm, in0=runm, in1=dmb[:, 0:1], op=ALU.max), inc=cp)
        p.wait("tensor", cp)
        _emit_colmax_out(p, runm[:, 0:1], mx1, msc[0:1, 7:8], ps_mx, idf, maxp4_o, mm, ev, st)
        p.wait("gpsimd", st)
        p.build()
    return nc


def build_launch4():
    nc = bass.Bass("TRN2", target_bir_lowering=False)
    ap = lambda n, sh, dt, kind: nc.dram_tensor(n, sh, dt, kind=kind).ap()
    t_i = ap("t_i", [128, 24 * R], F32, "ExternalInput")
    upq_i = ap("upq_i", [128, 24 * R], F32, "ExternalInput")
    maxp4_all = ap("maxp4_all", [8, 1], F32, "ExternalInput")
    identf = ap("identf", [128, 128], F32, "ExternalInput")
    oraw_o = ap("oraw_o", [128, 24 * R], F32, "ExternalOutput")
    maxp5_o = ap("maxp5_o", [1, 1], F32, "ExternalOutput")
    scr = nc.dram_tensor("scr", [1, 8], F32)

    with ExitStack() as ctx:
        sbuf = lambda n, sh, dt: ctx.enter_context(nc.sbuf_tensor(n, sh, dt))[:, :]
        psum = lambda n, sh, dt: ctx.enter_context(nc.psum_tensor(n, sh, dt))[:, :]
        sem = lambda n: ctx.enter_context(nc.semaphore(n))
        tc = sbuf("tc", [128, R], F32)
        uc = sbuf("uc", [128, R], F32)
        i32b = sbuf("i32b", [128, R], I32)
        dmb = sbuf("dmb", [128, R], F32)
        g2 = sbuf("g2", [128, R], F32)
        outb = sbuf("outb", [128, R], F32)
        m4row = sbuf("m4row", [1, 8], F32)
        msc = sbuf("msc", [1, 8], F32)
        cols = sbuf("cols", [128, 4], F32)
        runm = sbuf("runm", [128, 1], F32)
        mx1 = sbuf("mx1", [1, 128], F32)
        idf = sbuf("idf", [128, 128], F32)
        ps_mx = psum("ps_mx", [1, 128], F32)
        s_ld = sem("s_ld"); s_cp = sem("s_cp"); s_ev = sem("s_ev")
        s_st = sem("s_st"); s_aux = sem("s_aux"); s_mm = sem("s_mm")
        p = Prog(nc)
        ld = Sem(s_ld); cp = Sem(s_cp); ev = Sem(s_ev); st = Sem(s_st)
        aux = Sem(s_aux); mm = Sem(s_mm)

        p.dma("sync", m4row, maxp4_all.rearrange("a b -> b a"), ld)
        p.dma("sync", idf, identf, ld)
        p.emit("vector", lambda e: e.memset(cols[:, 2:3], 32768.0))
        p.emit("vector", lambda e: e.memset(runm, 0.0), inc=cp)
        p.wait("vector", ld)
        _emit_scales(p, scr, m4row, msc, cols[:, 0:1], cols[:, 1:2], ld, cp, aux, 0)
        p.wait("scalar", ld)
        p.wait("scalar", cp)
        for oc in range(24):
            p.wait("sync", cp)
            p.wait("sync", st)
            p.dma("sync", tc, t_i[:, oc * R:(oc + 1) * R], ld)
            p.dma("sync", uc, upq_i[:, oc * R:(oc + 1) * R], ld)
            p.wait("scalar", ld)
            p.wait("vector", ld)
            _emit_qdq(p, tc, g2, i32b, dmb, cols[:, 0:1], cols[:, 1:2], cols[:, 2:3], cp)
            p.emit("vector", lambda e: e.tensor_tensor(out=outb, in0=g2, in1=uc, op=ALU.mult), inc=cp)
            p.wait("sync", cp)
            p.dma("sync", oraw_o[:, oc * R:(oc + 1) * R], outb, st)
            p.emit("vector", lambda e: e.tensor_reduce(out=dmb[:, 0:1], in_=outb, axis=AX.X,
                                                       op=ALU.max, apply_absolute_value=True))
            p.drain("vector")
            p.emit("vector", lambda e: e.tensor_tensor(out=runm, in0=runm, in1=dmb[:, 0:1], op=ALU.max), inc=cp)
        p.wait("tensor", cp)
        _emit_colmax_out(p, runm[:, 0:1], mx1, msc[0:1, 7:8], ps_mx, idf, maxp5_o, mm, ev, st)
        p.wait("gpsimd", st)
        p.build()
    return nc


def build_launch5(debug=0):
    nc = bass.Bass("TRN2", target_bir_lowering=False)
    ap = lambda n, sh, dt, kind: nc.dram_tensor(n, sh, dt, kind=kind).ap()
    oraw_i = ap("oraw_i", [128, 24 * R], F32, "ExternalInput")
    maxp5_all = ap("maxp5_all", [8, 1], F32, "ExternalInput")
    xa_i = ap("xa_i", [R, D], F32, "ExternalInput")
    wdown = ap("wdown", [D, 3 * D], F32, "ExternalInput")
    ipw_t = ap("ipw_t", [D, 3 * D], F32, "ExternalInput")
    ipb_row = ap("ipb_row", [1, 3 * D], F32, "ExternalInput")
    opw_t = ap("opw_t", [D, D], F32, "ExternalInput")
    opb_row = ap("opb_row", [1, D], F32, "ExternalInput")
    rt_t = ap("rt_t", [D, 32], F32, "ExternalInput")
    rb_row = ap("rb_row", [1, 32], F32, "ExternalInput")
    gat = ap("gat", [D, 4096], F32, "ExternalInput")
    upt = ap("upt", [D, 4096], F32, "ExternalInput")
    dat = ap("dat", [4096, D], F32, "ExternalInput")
    sgt = ap("sgt", [D, 512], F32, "ExternalInput")
    sut = ap("sut", [D, 512], F32, "ExternalInput")
    sdt = ap("sdt", [512, D], F32, "ExternalInput")
    identb = ap("identb", [128, 128], BF16, "ExternalInput")
    identf = ap("identf", [128, 128], F32, "ExternalInput")
    out_o = ap("out_o", [R, D], F32, "ExternalOutput")
    scr = nc.dram_tensor("scr", [1, 8], F32)
    wt_dram = nc.dram_tensor("wt_dram", [32, R], F32)
    if debug:
        dbg_x1a = ap("dbg_x1a", [R, D], F32, "ExternalOutput")
        dbg_xn1 = ap("dbg_xn1", [128, KC * R], F32, "ExternalOutput")
        dbg_qkv1 = ap("dbg_qkv1", [128, 8 * 3072], F32, "ExternalOutput")
        dbg_ctx = ap("dbg_ctx", [128, NT * D], F32, "ExternalOutput")
        dbg_wt = ap("dbg_wt", [32, R], F32, "ExternalOutput")
        dbg_x1d1 = ap("dbg_x1d1", [R, D], F32, "ExternalOutput")
        dbg_hfm = ap("dbg_hfm", [128, 32 * 512], F32, "ExternalOutput")
        dbg_x1d2 = ap("dbg_x1d2", [R, D], F32, "ExternalOutput")

    with ExitStack() as ctx:
        sbuf = lambda n, sh, dt: ctx.enter_context(nc.sbuf_tensor(n, sh, dt))[:, :]
        psum = lambda n, sh, dt: ctx.enter_context(nc.psum_tensor(n, sh, dt))[:, :]
        sem = lambda n: ctx.enter_context(nc.semaphore(n))
        x1 = sbuf("x1", [128, NT * D], F32)
        xn1_fm = sbuf("xn1_fm", [128, KC * R], BF16)
        big1 = sbuf("big1", [128, 12288], F32)
        big2 = sbuf("big2", [128, 8192], F32)
        big3 = sbuf("big3", [128, 4096], F32)
        q512b = sbuf("q512b", [128, 512], F32)
        q512c = sbuf("q512c", [128, 512], F32)
        ev512 = sbuf("ev512", [128, 512], F32)
        bias5 = q512b
        wpool = sbuf("wpool", [128, 4096], BF16)
        dlhs = wpool
        gulhs = wpool[:, 0:2048]
        oplhs = wpool[:, 0:1024]
        rtb = wpool[:, 0:256]
        sglhs = wpool[:, 0:1024]
        sulhs = wpool[:, 1024:2048]
        sdlhs = wpool[:, 0:512]
        wbc = big1[:, 0:512]
        wtstg = big3[0:32, 1024:1152]
        wall = sbuf("wall", [128, 32], F32)
        opb = sbuf("opb", [128, 8], F32)
        rbb = sbuf("rbb", [128, 32], F32)
        bmx3 = sbuf("bmx3", [128, 96], F32)
        stat = sbuf("stat", [128, 8], F32)
        cols = sbuf("cols", [128, 4], F32)
        m5row = sbuf("m5row", [1, 8], F32)
        msc = sbuf("msc", [1, 8], F32)
        idb = sbuf("idb", [128, 128], BF16)
        idf = sbuf("idf", [128, 128], F32)
        ps_a = psum("ps_a", [128, 512], F32)
        ps_b = psum("ps_b", [128, 512], F32)
        ps_c = psum("ps_c", [128, 512], F32)
        ps_d = psum("ps_d", [128, 512], F32)
        ps_t = psum("ps_t", [128, D], BF16)
        ps_tf = psum("ps_tf", [128, 512], F32)
        s_ld = sem("s_ld"); s_cp = sem("s_cp"); s_ev = sem("s_ev")
        s_st = sem("s_st"); s_aux = sem("s_aux"); s_mm = sem("s_mm")
        s_lncp = sem("s_lncp")
        s_bar = sem("s_bar")
        slot_sems = [sem(f"s_sl{i}") for i in range(6)]
        p = Prog(nc)
        p.sems = {"lncp": Sem(s_lncp)}
        ld = Sem(s_ld); cp = Sem(s_cp); ev = Sem(s_ev); st = Sem(s_st)
        aux = Sem(s_aux); mm = Sem(s_mm)
        sg = [Sem(slot_sems[0]), Sem(slot_sems[1])]
        sw = [Sem(slot_sems[2]), Sem(slot_sems[3])]
        sd = [Sem(slot_sems[4]), Sem(slot_sems[5])]
        bar = Sem(s_bar)
        allsems = [ld, cp, ev, st, aux, mm, p.sems["lncp"]] + sg + sw + sd
        barrier = lambda: _phase_barrier(p, allsems, stat[0:1, 6:7], stat[0:1, 7:8], msc[0:1, 6:7], bar)

        p.dma("sync", m5row, maxp5_all.rearrange("a b -> b a"), ld)
        p.dma("sync", idb, identb, ld)
        p.dma("sync", idf, identf, ld)
        p.emit("vector", lambda e: e.memset(cols[:, 2:3], 32768.0), inc=cp)
        p.wait("vector", ld)
        _emit_scales(p, scr, m5row, msc, cols[:, 0:1], cols[:, 1:2], ld, cp, aux, 0)
        p.dma("gpsimd", opb, opb_row.rearrange("a (c p) -> p a c", p=128)[:, 0, :], ld,
              allow_slow_non_contiguous=True)
        p.dma("gpsimd", rbb, rb_row.broadcast_to([128, 32]), ld)
        for t in range(NT):
            p.dma("sync", x1[:, t * D:(t + 1) * D], xa_i[t * 128:(t + 1) * 128, :], ld)
        xa_ld = ld.n

        wdlhs = big1.bitcast(BF16)
        wtile = big3[:, 0:1024]
        yw = big3[:, 1024:2048]
        iw = big3[:, 2048:3072].bitcast(I32)[:, 0:1024]
        dw = big3[:, 3072:4096]
        wq_bf = ev512.bitcast(BF16)
        p.emit("vector", lambda e: e.memset(stat[:, 0:1], 16.0), inc=cp)
        p.wait("scalar", ld, xa_ld)
        for oc in range(8):
            for pc in range(3):
                p.wait("sync", cp)
                p.dma("sync", wtile, wdown[oc * 128:(oc + 1) * 128, pc * 1024:(pc + 1) * 1024], ld)
                p.wait("vector", ld)
                p.emit("vector", lambda e: e.tensor_reduce(
                    out=bmx3[:, 0:32], in_=wtile.rearrange("p (g x) -> p g x", x=32), axis=AX.X,
                    op=ALU.max, apply_absolute_value=True))
                p.drain("vector")
                p.emit("vector", lambda e: e.tensor_scalar(out=bmx3[:, 32:64], in0=bmx3[:, 0:32],
                                                           scalar1=1.0 / 7.0, scalar2=1e-12, op0=ALU.mult, op1=ALU.max))
                p.drain("vector")
                p.emit("vector", lambda e: e.reciprocal(out=bmx3[:, 64:96], in_=bmx3[:, 32:64]))
                p.drain("vector")
                p.emit("vector", lambda e: e.tensor_tensor(
                    out=yw.rearrange("p (g x) -> p g x", x=32), in0=wtile.rearrange("p (g x) -> p g x", x=32),
                    in1=bmx3[:, 64:96].rearrange("p (g o) -> p g o", o=1).broadcast_to([128, 32, 32]),
                    op=ALU.mult), inc=cp)
                p.wait("scalar", cp)
                p.emit("scalar", lambda e: e.activation(out=iw, in_=yw, func=AF.Identity,
                                                        bias=stat[:, 0:1], scale=1.0), inc=cp)
                p.wait("vector", cp)
                p.emit("vector", lambda e: e.scalar_tensor_tensor(out=dw, in0=yw, scalar=16.0,
                                                                  in1=iw, op0=ALU.add, op1=ALU.subtract))
                p.drain("vector")
                p.emit("vector", lambda e: e.scalar_tensor_tensor(out=dw, in0=dw, scalar=0.5,
                                                                  in1=iw, op0=ALU.is_ge, op1=ALU.add))
                p.drain("vector")
                p.emit("vector", lambda e: e.tensor_scalar(out=dw, in0=dw, scalar1=16.0, scalar2=None,
                                                           op0=ALU.subtract))
                p.drain("vector")
                p.emit("vector", lambda e: e.tensor_tensor(
                    out=wq_bf.rearrange("p (g x) -> p g x", x=32), in0=dw.rearrange("p (g x) -> p g x", x=32),
                    in1=bmx3[:, 32:64].rearrange("p (g o) -> p g o", o=1).broadcast_to([128, 32, 32]),
                    op=ALU.mult), inc=cp)
                p.wait("tensor", cp)
                p.wait("tensor", ev)
                for c in range(8):
                    p.emit("tensor", lambda e, c=c: e.transpose(
                        ps_t[:, c * 128:(c + 1) * 128], wq_bf[:, c * 128:(c + 1) * 128], idb),
                        inc=(mm if c == 7 else None))
                p.wait("scalar", mm)
                p.emit("scalar", lambda e, oc=oc, pc=pc: e.activation(
                    out=wdlhs.rearrange("p (o k j) -> p o k j", o=8, j=128)[:, oc, pc * 8:(pc + 1) * 8, :],
                    in_=ps_t.rearrange("p (k j) -> p k j", j=128), func=AF.Copy), inc=ev)
        o_slab = big2.bitcast(BF16)[:, 0:24 * 512]
        p.wait("scalar", cp)
        p.wait("scalar", ev)
        for tq in range(4):
            for kc in range(24):
                p.wait("sync", cp)
                p.dma("sync", q512b, oraw_i[:, kc * R + tq * 512: kc * R + (tq + 1) * 512], ld)
                p.wait("scalar", ld)
                _emit_qdq(p, q512b, o_slab[:, kc * 512:(kc + 1) * 512],
                          q512c.bitcast(I32)[:, 0:512], ev512, cols[:, 0:1], cols[:, 1:2], cols[:, 2:3], cp)
            p.wait("tensor", cp)
            for oc in range(8):
                pp = ps_a if oc % 2 == 0 else ps_b
                for kc in range(24):
                    p.emit("tensor", lambda e, pp=pp, oc=oc, kc=kc: e.matmul(
                        pp, wdlhs.rearrange("p (o k j) -> p o k j", o=8, j=128)[:, oc, kc, :],
                        o_slab[:, kc * 512:(kc + 1) * 512],
                        start=(kc == 0), stop=(kc == 23)), inc=(mm if kc == 23 else None))
                p.wait("scalar", mm)
                p.wait("scalar", st)
                p.emit("scalar", lambda e, pp=pp: e.activation(out=ev512, in_=pp, func=AF.Copy), inc=ev)
                p.wait("tensor", ev)
                for tb in range(4):
                    p.emit("tensor", lambda e, tb=tb: e.transpose(
                        ps_tf[:, tb * 128:(tb + 1) * 128], ev512[:, tb * 128:(tb + 1) * 128], idf),
                        inc=(mm if tb == 3 else None))
                p.wait("vector", mm)
                p.wait("vector", ld, xa_ld)
                p.emit("vector", lambda e, tq=tq, oc=oc: e.tensor_tensor(
                    out=x1.rearrange("p (t j) -> p t j", j=D)[:, tq * 4:(tq + 1) * 4, oc * 128:(oc + 1) * 128],
                    in0=x1.rearrange("p (t j) -> p t j", j=D)[:, tq * 4:(tq + 1) * 4, oc * 128:(oc + 1) * 128],
                    in1=ps_tf.rearrange("p (t j) -> p t j", j=128), op=ALU.add), inc=ev)
                p.wait("tensor", ev)
            p.wait("scalar", ev)

        if debug & 1:
            p.wait("sync", ev)
            for t in range(NT):
                p.dma("sync", dbg_x1a[t * 128:(t + 1) * 128, :], x1[:, t * D:(t + 1) * D], st)
            p.wait("vector", st)
        barrier()
        sa = big3[:, 0:1024]
        sb = big3[:, 1024:2048]
        xb = big3[:, 2048:2560].bitcast(BF16)[:, 0:1024]
        stats = {"m": stat[:, 1:2], "var": stat[:, 2:3], "sd": stat[:, 3:4],
                 "rstd": stat[:, 4:5], "eps_col": stat[:, 5:6]}
        p.emit("vector", lambda e: e.memset(stat[:, 5:6], EPS))
        p.wait("vector", ev)
        for t in range(NT):
            xt = x1[:, t * D:(t + 1) * D]
            _ln_tile(p, xt, sb, sa, stats)
            p.emit("vector", lambda e: e.tensor_copy(out=xb, in_=sb), inc=p.sems["lncp"])
            cpv = p.sems["lncp"].n
            p.wait("tensor", p.sems["lncp"], cpv)
            p.wait("tensor", ev)
            for c in range(KC):
                p.emit("tensor", lambda e, c=c: e.transpose(
                    ps_t[:, c * 128:(c + 1) * 128], xb[:, c * 128:(c + 1) * 128], idb),
                    inc=(mm if c == KC - 1 else None))
            p.wait("scalar", mm)
            p.emit("scalar", lambda e, t=t: e.activation(
                out=xn1_fm.rearrange("p (c j) -> p c j", j=R)[:, :, t * 128:(t + 1) * 128],
                in_=ps_t.rearrange("p (c j) -> p c j", j=128), func=AF.Copy), inc=ev)
            p.wait("vector", ev)

        if debug & 2:
            p.wait("gpsimd", ev)
            p.dma("gpsimd", dbg_xn1, xn1_fm, st)
            p.wait("vector", st)
        barrier()
        qkv1 = big1.bitcast(BF16)
        wblks = [big3[:, 0:2048].bitcast(BF16), wpool[:, 0:4096]]
        bias5s = [q512b, q512c]
        prodv = big3[:, 0:2048]
        prodg = big3[:, 2048:3072]
        ctxsf = big3[:, 3072:4096]
        xbC = q512b.bitcast(BF16)[:, 0:1024]
        scoresb = q512c[:, 0:128]
        smx = q512c[:, 128:144]
        ssm = q512c[:, 144:160]
        src_ = q512c[:, 160:176]
        ctx_fm = big2.bitcast(BF16)[:, 0:16384]
        wb_used = [0, 0]; wb_ld = [0, 0]; b5_used = [0, 0]
        pp_free = [0, 0]
        p.wait("gpsimd", p.sems["lncp"])
        p.wait("gpsimd", ev)
        for pb in range(2):
            if pb == 1:
                p.wait("gpsimd", cp)
            for oc in range(6):
                sl = oc % 2
                p.wait("gpsimd", mm, at=wb_used[sl])
                p.dma("gpsimd", wblks[sl].rearrange("p (k j) -> p k j", j=512),
                      ipw_t.rearrange("(k p) f -> p k f", p=128)[:, :, oc * 512:(oc + 1) * 512], sg[sl])
                p.wait("gpsimd", ev, at=b5_used[sl])
                p.dma("gpsimd", bias5s[sl].rearrange("p (o j) -> p o j", o=1),
                      ipb_row[0:1, oc * 512:(oc + 1) * 512].rearrange(
                          "a (o j) -> a o j", o=1).broadcast_to([128, 1, 512]), sw[sl])
                p.wait("tensor", sg[sl])
                for si in range(8):
                    t_idx = si * 2 + pb
                    psl = si % 2
                    pp = ps_a if psl == 0 else ps_b
                    p.wait("tensor", ev, at=pp_free[psl])
                    for kc in range(KC):
                        p.emit("tensor", lambda e, pp=pp, kc=kc, t_idx=t_idx, sl=sl: e.matmul(
                            pp, xn1_fm.rearrange("p (c j) -> p c j", j=R)[:, kc, t_idx * 128:(t_idx + 1) * 128],
                            wblks[sl][:, kc * 512:(kc + 1) * 512],
                            start=(kc == 0), stop=(kc == KC - 1)), inc=(mm if kc == KC - 1 else None))
                    if si == 7:
                        wb_used[sl] = mm.n
                    p.wait("vector", mm, at=mm.n)
                    p.wait("vector", sw[sl])
                    p.emit("vector", lambda e, pp=pp, si=si, oc=oc, sl=sl: e.tensor_tensor(
                        out=qkv1.rearrange("p (s f) -> p s f", f=3072)[:, si, oc * 512:(oc + 1) * 512],
                        in0=pp, in1=bias5s[sl], op=ALU.add), inc=ev)
                    pp_free[psl] = ev.n
                    if si == 7:
                        b5_used[sl] = ev.n
            if debug & 4:
                p.wait("gpsimd", ev)
                if pb == 1:
                    p.dma("gpsimd", dbg_qkv1, qkv1, st)
                    p.wait("vector", st)
            qv = qkv1.rearrange("p (s th d) -> p s th d", th=3 * NH, d=64)
            p.wait("vector", ev)
            p.wait("gpsimd", ev)
            sc_rdy = 0
            for si in range(8):
                for tp in range(4):
                    p.emit("vector", lambda e, si=si, tp=tp, qv=qv: e.tensor_tensor(
                        out=prodv.rearrange("p (t2 h d) -> p t2 h d", t2=2, d=64),
                        in0=qv[:, si:si + 1, 0:NH, :].broadcast_to([128, 2, NH, 64]),
                        in1=qv[:, 2 * tp:2 * tp + 2, NH:2 * NH, :], op=ALU.mult))
                    p.emit("vector", lambda e, tp=tp: e.tensor_reduce(
                        out=scoresb.rearrange("p (h t) -> p h t", t=8)[:, :, 2 * tp:2 * tp + 2],
                        in_=prodv.rearrange("p (t2 h d) -> p h t2 d", t2=2, d=64), axis=AX.X, op=ALU.add))
                p.drain("vector")
                p.emit("vector", lambda e: e.tensor_reduce(
                    out=smx, in_=scoresb.rearrange("p (h t) -> p h t", t=8), axis=AX.X, op=ALU.max))
                p.drain("vector")
                p.emit("vector", lambda e: e.tensor_tensor(
                    out=scoresb.rearrange("p (h t) -> p h t", t=8), in0=scoresb.rearrange("p (h t) -> p h t", t=8),
                    in1=smx.rearrange("p (h o) -> p h o", o=1).broadcast_to([128, 16, 8]), op=ALU.subtract), inc=cp)
                p.wait("scalar", cp, at=cp.n)
                p.emit("scalar", lambda e: e.activation(out=scoresb, in_=scoresb, func=AF.Exp, scale=0.125), inc=cp)
                p.wait("vector", cp, at=cp.n)
                p.emit("vector", lambda e: e.tensor_reduce(
                    out=ssm, in_=scoresb.rearrange("p (h t) -> p h t", t=8), axis=AX.X, op=ALU.add))
                p.drain("vector")
                p.emit("vector", lambda e: e.reciprocal(out=src_, in_=ssm))
                p.drain("vector")
                p.emit("vector", lambda e: e.tensor_tensor(
                    out=scoresb.rearrange("p (h t) -> p h t", t=8), in0=scoresb.rearrange("p (h t) -> p h t", t=8),
                    in1=src_.rearrange("p (h o) -> p h o", o=1).broadcast_to([128, 16, 8]), op=ALU.mult), inc=cp)
                sc_rdy = cp.n
                p.wait("gpsimd", cp, at=sc_rdy)
                for t in range(8):
                    p.emit("gpsimd", lambda e, t=t, qv=qv: e.tensor_tensor(
                        out=prodg[:, 0:1024].rearrange("p (h d) -> p h d", d=64),
                        in0=qv[:, t, 2 * NH:3 * NH, :],
                        in1=scoresb.rearrange("p (h t) -> p h t", t=8)[:, :, t:t + 1].broadcast_to([128, 16, 64]),
                        op=ALU.mult))
                    if t == 0:
                        p.emit("gpsimd", lambda e: e.tensor_copy(out=ctxsf, in_=prodg[:, 0:1024]))
                    else:
                        p.emit("gpsimd", lambda e: e.tensor_tensor(out=ctxsf, in0=ctxsf, in1=prodg[:, 0:1024], op=ALU.add))
                p.emit("gpsimd", lambda e: e.tensor_copy(out=xbC, in_=ctxsf), inc=cp)
                p.wait("tensor", cp, at=cp.n)
                p.wait("tensor", ev)
                t_idx = si * 2 + pb
                for c in range(KC):
                    p.emit("tensor", lambda e, c=c: e.transpose(
                        ps_t[:, c * 128:(c + 1) * 128], xbC[:, c * 128:(c + 1) * 128], idb),
                        inc=(mm if c == KC - 1 else None))
                p.wait("scalar", mm, at=mm.n)
                p.emit("scalar", lambda e, t_idx=t_idx: e.activation(
                    out=ctx_fm.rearrange("p (c j) -> p c j", j=R)[:, :, t_idx * 128:(t_idx + 1) * 128],
                    in_=ps_t.rearrange("p (c j) -> p c j", j=128), func=AF.Copy), inc=ev)
                p.wait("gpsimd", ev)
                p.wait("vector", cp)

        if debug & 8:
            p.wait("gpsimd", ev)
            p.dma("gpsimd", dbg_ctx, ctx_fm, st)
            p.wait("vector", st)
        barrier()
        for oc in range(8):
            p.wait("gpsimd", mm)
            p.dma("gpsimd", oplhs.rearrange("p (k j) -> p k j", j=128),
                  opw_t.rearrange("(k p) f -> p k f", p=128)[:, :, oc * 128:(oc + 1) * 128], ld)
            p.wait("tensor", ld)
            for tq in range(4):
                pp = ps_a if tq % 2 == 0 else ps_b
                p.wait("tensor", ev)
                for kc in range(KC):
                    p.emit("tensor", lambda e, pp=pp, kc=kc, oc=oc, tq=tq: e.matmul(
                        pp, oplhs[:, kc * 128:(kc + 1) * 128],
                        ctx_fm[:, kc * R + tq * 512: kc * R + (tq + 1) * 512],
                        start=(kc == 0), stop=(kc == KC - 1)), inc=(mm if kc == KC - 1 else None))
                p.wait("scalar", mm)
                p.emit("scalar", lambda e, pp=pp, oc=oc: e.activation(
                    out=ev512, in_=pp, func=AF.Identity, bias=opb[:, oc:oc + 1], scale=1.0), inc=ev)
                p.wait("tensor", ev)
                for tb in range(4):
                    p.emit("tensor", lambda e, tb=tb: e.transpose(
                        ps_tf[:, tb * 128:(tb + 1) * 128], ev512[:, tb * 128:(tb + 1) * 128], idf),
                        inc=(mm if tb == 3 else None))
                p.wait("vector", mm)
                p.emit("vector", lambda e, tq=tq, oc=oc: e.tensor_tensor(
                    out=x1.rearrange("p (t j) -> p t j", j=D)[:, tq * 4:(tq + 1) * 4, oc * 128:(oc + 1) * 128],
                    in0=x1.rearrange("p (t j) -> p t j", j=D)[:, tq * 4:(tq + 1) * 4, oc * 128:(oc + 1) * 128],
                    in1=ps_tf.rearrange("p (t j) -> p t j", j=128), op=ALU.add), inc=ev)
                p.wait("tensor", ev)

        if debug & 16:
            p.wait("sync", ev)
            for t in range(NT):
                p.dma("sync", dbg_x1d1[t * 128:(t + 1) * 128, :], x1[:, t * D:(t + 1) * D], st)
            p.wait("vector", st)
        barrier()
        prod = prodv
        p.wait("gpsimd", mm)
        p.dma("gpsimd", rtb.rearrange("p (k j) -> p k j", j=32),
              rt_t.rearrange("(k p) f -> p k f", p=128), ld)
        p.wait("tensor", ld)
        for t in range(NT):
            pp = ps_a if t % 2 == 0 else ps_b
            p.wait("tensor", ev)
            for kc in range(KC):
                p.emit("tensor", lambda e, pp=pp, kc=kc, t=t: e.matmul(
                    pp[:, 0:32],
                    xn1_fm.rearrange("p (c j) -> p c j", j=R)[:, kc, t * 128:(t + 1) * 128],
                    rtb[:, kc * 32:(kc + 1) * 32],
                    start=(kc == 0), stop=(kc == KC - 1)), inc=(mm if kc == KC - 1 else None))
            p.wait("vector", mm)
            wsl = wall[:, 0:32]
            p.emit("vector", lambda e, pp=pp, wsl=wsl: e.tensor_tensor(out=prod[:, 0:32], in0=pp[:, 0:32], in1=rbb, op=ALU.add), inc=ev)
            p.drain("vector")
            p.emit("vector", lambda e: e.reduce_max(out=stat[:, 6:7], in_=prod[:, 0:32], axis=AX.X))
            p.drain("vector")
            p.emit("vector", lambda e: e.tensor_scalar(out=prod[:, 0:32], in0=prod[:, 0:32],
                                                       scalar1=stat[:, 6:7], scalar2=None, op0=ALU.subtract), inc=cp)
            p.wait("scalar", cp)
            p.emit("scalar", lambda e: e.activation(out=prod[:, 0:32], in_=prod[:, 0:32], func=AF.Exp), inc=cp)
            p.wait("vector", cp)
            p.emit("vector", lambda e: e.reduce_sum(out=stat[:, 6:7], in_=prod[:, 0:32], axis=AX.X))
            p.drain("vector")
            p.emit("vector", lambda e: e.reciprocal(out=stat[:, 7:8], in_=stat[:, 6:7]))
            p.drain("vector")
            p.emit("vector", lambda e: e.tensor_scalar(out=prod[:, 0:32], in0=prod[:, 0:32],
                                                       scalar1=stat[:, 7:8], scalar2=None, op0=ALU.mult))
            p.drain("vector")
            p.emit("vector", lambda e: e.tensor_copy(out=prod[:, 32:64], in_=prod[:, 0:32]))
            p.emit("vector", lambda e, wsl=wsl: e.memset(wsl, 0.0))
            p.drain("vector")
            for it in range(4):
                p.emit("vector", lambda e: e.reduce_max(out=stat[:, 6:7], in_=prod[:, 32:64], axis=AX.X))
                p.drain("vector")
                p.emit("vector", lambda e: e.tensor_scalar(out=prod[:, 64:96], in0=prod[:, 32:64],
                                                           scalar1=stat[:, 6:7], scalar2=None, op0=ALU.is_ge))
                p.drain("vector")
                p.emit("vector", lambda e: e.tensor_tensor(out=prod[:, 96:128], in0=prod[:, 64:96], in1=prod[:, 0:32], op=ALU.mult))
                p.drain("vector")
                p.emit("vector", lambda e, wsl=wsl: e.tensor_tensor(out=wsl, in0=wsl, in1=prod[:, 96:128], op=ALU.add))
                p.emit("vector", lambda e: e.tensor_scalar(out=prod[:, 96:128], in0=prod[:, 64:96],
                                                           scalar1=-1.0, scalar2=1.0, op0=ALU.mult, op1=ALU.add))
                p.drain("vector")
                p.emit("vector", lambda e: e.tensor_tensor(out=prod[:, 32:64], in0=prod[:, 32:64], in1=prod[:, 96:128], op=ALU.mult))
                p.drain("vector")
                p.emit("vector", lambda e: e.tensor_tensor(out=prod[:, 32:64], in0=prod[:, 32:64], in1=prod[:, 64:96], op=ALU.subtract))
                p.drain("vector")
            p.emit("vector", lambda e, wsl=wsl: e.reduce_sum(out=stat[:, 6:7], in_=wsl, axis=AX.X))
            p.drain("vector")
            p.emit("vector", lambda e: e.tensor_scalar(out=stat[:, 6:7], in0=stat[:, 6:7],
                                                       scalar1=1e-12, scalar2=None, op0=ALU.max))
            p.drain("vector")
            p.emit("vector", lambda e: e.reciprocal(out=stat[:, 7:8], in_=stat[:, 6:7]))
            p.drain("vector")
            p.emit("vector", lambda e, wsl=wsl: e.tensor_scalar(out=wsl, in0=wsl,
                                                                scalar1=stat[:, 7:8], scalar2=None, op0=ALU.mult), inc=cp)
            p.wait("tensor", cp)
            p.wait("tensor", aux)
            p.emit("tensor", lambda e, wsl=wsl: e.transpose(ps_tf[0:32, 0:128], wsl, idf), inc=mm)
            p.wait("vector", mm)
            p.wait("vector", st)
            p.emit("vector", lambda e: e.tensor_copy(out=wtstg, in_=ps_tf[0:32, 0:128]), inc=aux)
            p.wait("sync", aux)
            p.dma("sync", wt_dram.ap()[0:32, t * 128:(t + 1) * 128], wtstg, st)
        p.wait("vector", st)

        barrier()
        h_fm = big2.bitcast(BF16)[:, 0:32 * 512]
        sils = [big3[:, 0:512], big3[:, 512:1024]]
        h1s = [big3[:, 1024:1536], big3[:, 1536:2048]]
        gus = [wpool[:, 0:2048], wpool[:, 2048:4096]]
        wbcs = [big1[:, 0:512], big1[:, 512:1024]]
        dls = [big1[:, 1024:3072].bitcast(BF16), big1[:, 3072:5120].bitcast(BF16)]
        gu_used = [0, 0]; wbc_used = [0, 0]; gu_ld = [0, 0]; wbc_ld = [0, 0]
        pe_free = [0, 0]
        dl_used = [0, 0]; dl_ld = [0, 0]; dn_free = [0, 0]
        if debug & 32:
            p.wait("sync", st)
            p.dma("sync", dbg_wt, wt_dram.ap(), st)
        p.wait("gpsimd", st)
        for tq in range(4):
            p.wait("vector", mm)
            p.wait("gpsimd", mm)
            p.wait("tensor", ev)
            for ei in range(32):
                sl = ei % 2
                pg = ps_a if sl == 0 else ps_c
                pu = ps_b if sl == 0 else ps_d
                p.wait("gpsimd", mm, at=gu_used[sl])
                p.dma("gpsimd", gus[sl].rearrange("p (w k j) -> p w k j", w=2, j=128)[:, 0, :, :],
                      gat.rearrange("(k p) f -> p k f", p=128)[:, :, ei * 128:(ei + 1) * 128], sg[sl])
                p.dma("gpsimd", gus[sl].rearrange("p (w k j) -> p w k j", w=2, j=128)[:, 1, :, :],
                      upt.rearrange("(k p) f -> p k f", p=128)[:, :, ei * 128:(ei + 1) * 128], sg[sl])
                p.wait("gpsimd", cp, at=wbc_used[sl])
                p.dma("gpsimd", wbcs[sl].rearrange("p (o j) -> p o j", o=1),
                      wt_dram.ap()[ei:ei + 1, tq * 512:(tq + 1) * 512].rearrange(
                          "a (o j) -> a o j", o=1).broadcast_to([128, 1, 512]), sw[sl])
                p.wait("tensor", sg[sl])
                p.wait("tensor", cp, at=pe_free[sl])
                for kc in range(KC):
                    p.emit("tensor", lambda e, pg=pg, kc=kc, tq=tq, sl=sl: e.matmul(
                        pg, gus[sl][:, kc * 128:(kc + 1) * 128],
                        xn1_fm[:, kc * R + tq * 512: kc * R + (tq + 1) * 512],
                        start=(kc == 0), stop=(kc == KC - 1)))
                    p.emit("tensor", lambda e, pu=pu, kc=kc, tq=tq, sl=sl: e.matmul(
                        pu, gus[sl][:, 1024 + kc * 128: 1024 + (kc + 1) * 128],
                        xn1_fm[:, kc * R + tq * 512: kc * R + (tq + 1) * 512],
                        start=(kc == 0), stop=(kc == KC - 1)), inc=(mm if kc == KC - 1 else None))
                gu_used[sl] = mm.n
                p.wait("scalar", mm, at=mm.n)
                p.emit("scalar", lambda e, pg=pg, sl=sl: e.activation(out=sils[sl], in_=pg, func=AF.Silu), inc=cp)
                p.wait("vector", cp, at=cp.n)
                p.wait("vector", sw[sl])
                p.emit("vector", lambda e, pu=pu, sl=sl: e.tensor_tensor(out=h1s[sl], in0=pu, in1=sils[sl], op=ALU.mult))
                p.emit("vector", lambda e, ei=ei, sl=sl: e.tensor_tensor(
                    out=h_fm[:, ei * 512:(ei + 1) * 512], in0=h1s[sl], in1=wbcs[sl], op=ALU.mult), inc=cp)
                wbc_used[sl] = cp.n
                pe_free[sl] = cp.n
            p.wait("tensor", cp)
            if (debug & 64) and tq == 0:
                p.wait("gpsimd", cp)
                p.dma("gpsimd", dbg_hfm, h_fm, st)
                p.wait("vector", st)
            for oc in range(8):
                sl = oc % 2
                pp = ps_a if sl == 0 else ps_b
                p.wait("gpsimd", mm, at=dl_used[sl])
                p.dma("gpsimd", dls[sl].rearrange("p (k j) -> p k j", j=128),
                      dat.rearrange("(k p) f -> p k f", p=128)[:, :, oc * 128:(oc + 1) * 128], sd[sl])
                p.wait("tensor", sd[sl])
                p.wait("tensor", ev, at=dn_free[sl])
                for kc in range(32):
                    p.emit("tensor", lambda e, pp=pp, kc=kc, sl=sl: e.matmul(
                        pp, dls[sl][:, kc * 128:(kc + 1) * 128], h_fm[:, kc * 512:(kc + 1) * 512],
                        start=(kc == 0), stop=(kc == 31)), inc=(mm if kc == 31 else None))
                dl_used[sl] = mm.n
                p.wait("scalar", mm, at=mm.n)
                p.wait("scalar", ev)
                p.emit("scalar", lambda e, pp=pp: e.activation(out=ev512, in_=pp, func=AF.Copy), inc=ev)
                p.wait("tensor", ev, at=ev.n)
                for tb in range(4):
                    p.emit("tensor", lambda e, tb=tb: e.transpose(
                        ps_tf[:, tb * 128:(tb + 1) * 128], ev512[:, tb * 128:(tb + 1) * 128], idf),
                        inc=(mm if tb == 3 else None))
                p.wait("vector", mm, at=mm.n)
                p.emit("vector", lambda e, tq=tq, oc=oc: e.tensor_tensor(
                    out=x1.rearrange("p (t j) -> p t j", j=D)[:, tq * 4:(tq + 1) * 4, oc * 128:(oc + 1) * 128],
                    in0=x1.rearrange("p (t j) -> p t j", j=D)[:, tq * 4:(tq + 1) * 4, oc * 128:(oc + 1) * 128],
                    in1=ps_tf.rearrange("p (t j) -> p t j", j=128), op=ALU.add), inc=ev)
                dn_free[sl] = ev.n
                p.wait("tensor", ev)

        if debug & 128:
            p.wait("sync", ev)
            for t in range(NT):
                p.dma("sync", dbg_x1d2[t * 128:(t + 1) * 128, :], x1[:, t * D:(t + 1) * D], st)
            p.wait("vector", st)
        barrier()
        s_h = big3.bitcast(BF16)[:, 0:4 * R]
        for oc in range(4):
            p.wait("gpsimd", mm)
            p.dma("gpsimd", sglhs.rearrange("p (k j) -> p k j", j=128),
                  sgt.rearrange("(k p) f -> p k f", p=128)[:, :, oc * 128:(oc + 1) * 128], ld)
            p.dma("gpsimd", sulhs.rearrange("p (k j) -> p k j", j=128),
                  sut.rearrange("(k p) f -> p k f", p=128)[:, :, oc * 128:(oc + 1) * 128], ld)
            p.wait("tensor", ld)
            for tq in range(4):
                pg = ps_a if tq % 2 == 0 else ps_c
                pu = ps_b if tq % 2 == 0 else ps_d
                p.wait("tensor", cp)
                for kc in range(KC):
                    p.emit("tensor", lambda e, pg=pg, kc=kc, tq=tq: e.matmul(
                        pg, sglhs[:, kc * 128:(kc + 1) * 128],
                        xn1_fm[:, kc * R + tq * 512: kc * R + (tq + 1) * 512],
                        start=(kc == 0), stop=(kc == KC - 1)))
                    p.emit("tensor", lambda e, pu=pu, kc=kc, tq=tq: e.matmul(
                        pu, sulhs[:, kc * 128:(kc + 1) * 128],
                        xn1_fm[:, kc * R + tq * 512: kc * R + (tq + 1) * 512],
                        start=(kc == 0), stop=(kc == KC - 1)), inc=(mm if kc == KC - 1 else None))
                p.wait("scalar", mm)
                p.emit("scalar", lambda e, pg=pg: e.activation(out=q512b, in_=pg, func=AF.Silu), inc=cp)
                p.wait("vector", cp)
                p.emit("vector", lambda e, pu=pu, oc=oc, tq=tq: e.tensor_tensor(
                    out=s_h[:, oc * R + tq * 512: oc * R + (tq + 1) * 512],
                    in0=pu, in1=q512b, op=ALU.mult), inc=cp)
        p.wait("tensor", cp)
        for oc in range(8):
            p.wait("gpsimd", mm)
            p.dma("gpsimd", sdlhs.rearrange("p (k j) -> p k j", j=128),
                  sdt.rearrange("(k p) f -> p k f", p=128)[:, :, oc * 128:(oc + 1) * 128], ld)
            p.wait("tensor", ld)
            for tq in range(4):
                pp = ps_a if tq % 2 == 0 else ps_b
                p.wait("tensor", ev)
                for kc in range(4):
                    p.emit("tensor", lambda e, pp=pp, kc=kc, tq=tq: e.matmul(
                        pp, sdlhs[:, kc * 128:(kc + 1) * 128],
                        s_h[:, kc * R + tq * 512: kc * R + (tq + 1) * 512],
                        start=(kc == 0), stop=(kc == 3)), inc=(mm if kc == 3 else None))
                p.wait("scalar", mm)
                p.emit("scalar", lambda e, pp=pp: e.activation(out=ev512, in_=pp, func=AF.Copy), inc=ev)
                p.wait("tensor", ev)
                for tb in range(4):
                    p.emit("tensor", lambda e, tb=tb: e.transpose(
                        ps_tf[:, tb * 128:(tb + 1) * 128], ev512[:, tb * 128:(tb + 1) * 128], idf),
                        inc=(mm if tb == 3 else None))
                p.wait("vector", mm)
                p.emit("vector", lambda e, tq=tq, oc=oc: e.tensor_tensor(
                    out=x1.rearrange("p (t j) -> p t j", j=D)[:, tq * 4:(tq + 1) * 4, oc * 128:(oc + 1) * 128],
                    in0=x1.rearrange("p (t j) -> p t j", j=D)[:, tq * 4:(tq + 1) * 4, oc * 128:(oc + 1) * 128],
                    in1=ps_tf.rearrange("p (t j) -> p t j", j=128), op=ALU.add), inc=ev)
                p.wait("tensor", ev)

        barrier()
        p.wait("sync", ev)
        for t in range(NT):
            p.dma("sync", out_o[t * 128:(t + 1) * 128, :], x1[:, t * D:(t + 1) * D], st)
        p.wait("gpsimd", st)
        p.build()
    return nc
